# version 02
# baseline (speedup 1.0000x reference)
import numpy as np
import concourse.bass as bass
import concourse.mybir as mybir

F32 = mybir.dt.float32
BF16 = mybir.dt.bfloat16
AF = mybir.ActivationFunctionType
ALU = mybir.AluOpType
AX = mybir.AxisListType

ENGINES = ("tensor", "vector", "scalar", "gpsimd", "sync")


class Sem:
    __slots__ = ("h", "count")

    def __init__(self, h):
        self.h = h
        self.count = 0


class Op:
    __slots__ = ("engine", "method", "args", "kwargs", "waits", "incs", "isdma")

    def __init__(self, engine, method, args, kwargs, waits, incs, isdma):
        self.engine = engine
        self.method = method
        self.args = args
        self.kwargs = kwargs
        self.waits = waits
        self.incs = incs
        self.isdma = isdma


class Ker:
    def __init__(self, name="k", serial=True):
        self.nc = bass.Bass("TRN2", target_bir_lowering=False, name=name)
        self.seq = []
        self.sems = []
        self.serial = serial
        self._c = None
        self._built = False

    def sem(self, name=None):
        if name is None:
            name = f"sem{len(self.sems)}"
        s = Sem(self.nc.alloc_semaphore(name=name))
        self.sems.append(s)
        return s

    @property
    def c(self):
        if self._c is None:
            self._c = self.sem("chain")
        return self._c

    def sb(self, name, shape, dt=F32, **kw):
        return self.nc.alloc_sbuf_tensor(name, list(shape), dt, **kw)

    def ps(self, name, shape, dt=F32):
        return self.nc.alloc_psum_tensor(name, list(shape), dt)

    def dram(self, name, shape, dt=F32, kind="Internal", **kw):
        return self.nc.dram_tensor(name, list(shape), dt, kind=kind, **kw)

    def const(self, data, name=None):
        return self.nc.inline_tensor(np.ascontiguousarray(data), name=name)

    def fconst(self, val, dt=F32):
        if not hasattr(self, "_fconsts"):
            self._fconsts = {}
        key = (float(val), dt)
        if key not in self._fconsts:
            t = self.sb(f"fc_{len(self._fconsts)}", (128, 1), dt)
            self.g("memset", t[:], float(val))
            self._fconsts[key] = t
        return self._fconsts[key][:]

    def op(self, engine, method, *args, wait=None, inc=None, **kwargs):
        isdma = method.startswith("dma") or method == "collective_compute"
        for (s, dlt) in (inc or []):
            s.count += dlt
        self.seq.append(Op(engine, method, args, kwargs, list(wait or []), list(inc or []), isdma))

    def t(self, method, *a, **k): self.op("tensor", method, *a, **k)
    def v(self, method, *a, **k): self.op("vector", method, *a, **k)
    def s(self, method, *a, **k): self.op("scalar", method, *a, **k)
    def g(self, method, *a, **k): self.op("gpsimd", method, *a, **k)
    def y(self, method, *a, **k): self.op("sync", method, *a, **k)

    def wait(self, engine, sem, val=None):
        if val is None and sem is self._c:
            v = None
        elif val is None:
            v = sem.count
        else:
            v = val
        self.seq.append(Op(engine, "_wait", (), {}, [(sem, v)], [], False))

    def barrier(self):
        pass



    def _ap_ranges(self, a):
        import concourse.bass as bass
        if not hasattr(a, "tensor") or not hasattr(a, "ap"):
            return None
        try:
            t = a.tensor
            name = t.name
            esz = {"dt.float32": 4, "dt.bfloat16": 2, "dt.int32": 4, "dt.uint8": 1,
                   "dt.float16": 2}.get(str(a.dtype), 4)
            off = a.offset
            if not isinstance(off, int):
                return (name, 0, 1 << 40, False)
            ext = 1
            for (step, count) in list(a.ap):
                ext += abs(int(step)) * (int(count) - 1)
            start = int(off) * esz
            end = start + ext * esz
            spc = str(getattr(a, "space", ""))
            is_psum = "PSUM" in spc.upper() or name.startswith("PS")
            return (name, start, end, is_psum)
        except Exception:
            return None

    def _op_accesses(self, o):
        m = o.method
        args = list(o.args)
        kw = o.kwargs
        outs, ins = [], []
        if m == "dma_start":
            if "out" in kw:
                outs.append(kw["out"])
            if "in_" in kw:
                ins.append(kw["in_"])
            for a in args:
                ins.append(a)
        elif m == "collective_compute":
            return "BARRIER"
        elif m in ("memset", "memzero"):
            outs.append(args[0])
        elif m == "max_with_indices":
            outs += args[0:2]; ins += args[2:]
        else:
            if args:
                outs.append(args[0])
                ins += args[1:]
            if "accum_out" in kw and kw["accum_out"] is not None:
                outs.append(kw["accum_out"])
            for kname, v in kw.items():
                if kname in ("bias", "scale", "alpha", "scalar1", "scalar2", "initial"):
                    ins.append(v)
        if m in ("matmul", "transpose"):
            ins.append(args[0])
        rr, ww = [], []
        for a in ins:
            r = self._ap_ranges(a)
            if r is not None:
                rr.append(r)
        for a in outs:
            r = self._ap_ranges(a)
            if r is not None:
                ww.append(r)
        return (rr, ww)

    def _build_auto(self):
        nc = self.nc
        seq = self.seq
        n = len(seq)
        GR = 16384
        R = 4
        sC = {e: self.sem(f"ac_{e}") for e in ENGINES}
        for o in seq:
            o.waits = [(s2, v2) for (s2, v2) in o.waits if v2 is not None]
        openg = []
        opidx_on_engine = []
        nloc = {e: 0 for e in ENGINES}
        for o in seq:
            openg.append(o.engine)
            opidx_on_engine.append(nloc[o.engine])
            nloc[o.engine] += 1
        lasync = {e: [] for e in ENGINES}
        for o in seq:
            lasync[o.engine].append(o.isdma)
        isbar_g = [o.method == "collective_compute" for o in seq]
        bar_local = {(openg[i], opidx_on_engine[i]) for i in range(n) if isbar_g[i]}
        last_writer = {}
        last_readers = {}
        bar_after = None
        coll_cnt = {e: 0 for e in ENGINES}
        last_on = {e: -1 for e in ENGINES}
        last_dma_on = {e: -1 for e in ENGINES}
        deps_of = [None] * n
        bar_info = [None] * n
        TD = {e: set() for e in ENGINES}
        for i, o in enumerate(seq):
            e = o.engine
            li = opidx_on_engine[i]
            if isbar_g[i]:
                wc = {e2: last_on[e2] for e2 in ENGINES if last_on[e2] >= 0}
                wd = {e2: last_dma_on[e2] for e2 in ENGINES if last_dma_on[e2] >= 0}
                wb = {e2: c for e2, c in coll_cnt.items() if c > 0}
                for e2, dl in wd.items():
                    TD[e2].add(dl)
                bar_info[i] = (wc, wd, wb)
                coll_cnt[e] += 1
                bar_after = (e, coll_cnt[e])
                last_on[e] = li
                continue
            rr, ww = self._op_accesses(o)
            deps = []

            def keep(de, dl, e=e):
                if de != e:
                    return True
                return lasync[de][dl] or de == "vector"

            def slots(r):
                name, s0, s1, isps = r
                if isps:
                    yield (name, -1)
                    return
                for sl in range(s0 // GR, max(s0 // GR, (s1 - 1) // GR) + 1):
                    yield (name, sl)

            for r in rr:
                for key in slots(r):
                    lw = last_writer.get(key)
                    if lw is not None and keep(*lw):
                        deps.append(lw)
            for r in ww:
                for key in slots(r):
                    lw = last_writer.get(key)
                    if lw is not None and keep(*lw):
                        deps.append(lw)
                    lrs = last_readers.get(key)
                    if lrs:
                        for e2, l2 in lrs.items():
                            if keep(e2, l2):
                                deps.append((e2, l2))
            need = {}
            for (de, dl) in deps:
                kind = "D" if lasync[de][dl] else "C"
                kk = (de, kind)
                need[kk] = max(need.get(kk, -1), dl)
            for (de, kind), dl in need.items():
                if kind == "D":
                    TD[de].add(dl)
            deps_of[i] = (need, bar_after)
            for r in rr:
                for key in slots(r):
                    last_readers.setdefault(key, {})
                    prev2 = last_readers[key].get(e, -1)
                    last_readers[key][e] = max(prev2, li)
            for r in ww:
                for key in slots(r):
                    last_writer[key] = (e, li)
                    last_readers[key] = {}
            last_on[e] = li
            if o.isdma:
                last_dma_on[e] = li
        for e in ENGINES:
            if last_dma_on[e] >= 0:
                TD[e].add(last_dma_on[e])
        flagC = {e: set() for e in ENGINES}
        for item in deps_of:
            if item:
                need, _ = item
                for (de, kind), dl in need.items():
                    if kind == "C":
                        flagC[de].add(dl)
        meth_local0 = {e: [] for e in ENGINES}
        for o in seq:
            meth_local0[o.engine].append(o.method)
        for i in range(n):
            if bar_info[i]:
                wc = bar_info[i][0]
                for e2, dl in wc.items():
                    if (not lasync[e2][dl] and (e2, dl) not in bar_local
                            and meth_local0[e2][dl] != "_wait"):
                        flagC[e2].add(dl)
        meth_local = {e: [] for e in ENGINES}
        for o in seq:
            meth_local[o.engine].append(o.method)
        for e in ENGINES:
            for li in range(nloc[e] - 1, -1, -1):
                if (not lasync[e][li] and (e, li) not in bar_local
                        and meth_local[e][li] != "_wait"):
                    flagC[e].add(li)
                    break
        cumC = {e: [0] * nloc[e] for e in ENGINES}
        for e in ENGINES:
            rc = 0
            for li in range(nloc[e]):
                if li in flagC[e]:
                    rc += 1
                cumC[e][li] = rc
        dma_sem = {}
        drain = {}
        closeW = {}
        dsems = {}
        dfinal = {}
        sB = {}
        for e in ENGINES:
            if coll_cnt[e] > 0:
                sB[e] = self.sem(f"ab_{e}")
            dmas = [li for li in range(nloc[e])
                    if lasync[e][li] and (e, li) not in bar_local]
            if not dmas:
                continue
            sems_e = [self.sem(f"ad_{e}{r}") for r in range(R)]
            dsems[e] = sems_e
            semtot = [0] * R
            ep = 0
            first = True
            for li in dmas:
                r = ep % R
                if first and semtot[r] > 0:
                    drain[(e, li)] = (sems_e[r], semtot[r])
                first = False
                semtot[r] += 16
                dma_sem[(e, li)] = sems_e[r]
                if li in TD[e]:
                    closeW[(e, li)] = [(sems_e[r2], semtot[r2])
                                       for r2 in range(R) if semtot[r2] > 0]
                    ep += 1
                    first = True
            dfinal[e] = [(sems_e[r2], semtot[r2]) for r2 in range(R) if semtot[r2] > 0]
        waited = {e: {} for e in ENGINES}

        def wait_val(o, e, sem, v, key):
            if v > waited[e].get(key, 0):
                o.waits.append((sem, v))
                waited[e][key] = v

        def wait_dma_prefix(o, e, de, dl):
            for (sem, v) in closeW[(de, dl)]:
                wait_val(o, e, sem, v, ("D", de, id(sem)))

        for i, o in enumerate(seq):
            e = o.engine
            li = opidx_on_engine[i]
            if isbar_g[i]:
                wc, wd, wb = bar_info[i]
                for e2, c in wb.items():
                    wait_val(o, e, sB[e2], c, ("B", e2))
                for e2, dl in wc.items():
                    v = cumC[e2][dl]
                    if v > 0 and not (e2 == e):
                        wait_val(o, e, sC[e2], v, ("C", e2))
                    elif v > 0:
                        wait_val(o, e, sC[e2], v, ("C", e2))
                for e2, dl in wd.items():
                    wait_dma_prefix(o, e, e2, dl)
                o.incs.append((sB[e], 1))
                continue
            need, ba = deps_of[i]
            if ba is not None:
                be, bc = ba
                wait_val(o, e, sB[be], bc, ("B", be))
            for (de, kind), dl in sorted(need.items()):
                if kind == "D":
                    wait_dma_prefix(o, e, de, dl)
                else:
                    v = cumC[de][dl]
                    if v > 0:
                        wait_val(o, e, sC[de], v, ("C", de))
            dr = drain.get((e, li))
            if dr is not None:
                o.waits.append((dr[0], dr[1]))
            if (e, li) in dma_sem:
                o.incs.append((dma_sem[(e, li)], 16))
            elif li in flagC[e]:
                o.incs.append((sC[e], 1))
        tail = Op("sync", "_wait", (), {}, [], [], False)
        for e in ENGINES:
            if nloc[e] and cumC[e][nloc[e] - 1] > 0:
                tail.waits.append((sC[e], cumC[e][nloc[e] - 1]))
            for (sem, v) in dfinal.get(e, []):
                tail.waits.append((sem, v))
            if coll_cnt[e] > 0:
                tail.waits.append((sB[e], coll_cnt[e]))
        seq.append(tail)

    def build(self):
        assert not self._built
        self._built = True
        nc = self.nc
        seq = self.seq
        n = len(seq)
        import os
        full = os.environ.get("TRNKIT_FULLSERIAL") == "1"
        brk = set(filter(None, (os.environ.get("TRNKIT_BREAK") or "vector").split(",")))
        if os.environ.get("TRNKIT_AUTO") == "1" or getattr(self, "auto_sched", False):
            self._build_auto()
            seq = self.seq
            n = len(seq)
        elif self.serial:
            c = self.c
            cum = 0
            prev = None
            for i, o in enumerate(seq):
                nxt = seq[i + 1] if i + 1 < n else None
                if full or o.engine in brk:
                    nxt = None
                o.waits = [(s, cum if v is None else v) for (s, v) in o.waits]
                need_wait = prev is not None and (full or o.engine in brk
                                                  or prev.engine != o.engine or prev.isdma)
                if need_wait:
                    o.waits.append((c, cum))
                if o.method != "_wait":
                    if o.isdma:
                        if o.method == "collective_compute":
                            o.incs.append((c, 1)); cum += 1
                        else:
                            o.incs.append((c, 16)); cum += 16
                    elif nxt is None or nxt.engine != o.engine:
                        o.incs.append((c, 1)); cum += 1
                prev = o
            c.count = cum
        per = {e: [] for e in ENGINES}
        for o in seq:
            per[o.engine].append(o)
        with nc.Block() as block:
            for ename in ENGINES:
                recs = per[ename]
                if not recs:
                    continue

                def body(eng, recs=recs):
                    for o in recs:
                        for (s, v) in o.waits:
                            eng.wait_ge(s.h, v)
                        if o.method == "_wait":
                            continue
                        inst = getattr(eng, o.method)(*o.args, **o.kwargs)
                        for (s, d) in o.incs:
                            if inst is not None:
                                inst.then_inc(s.h, d)
                            else:
                                eng.nop().then_inc(s.h, d)

                getattr(block, ename)(body)
        nc.all_engine_barrier()
        nums = sorted({s.h.num for s in self.sems})
        for nn in nums:
            nc.gpsimd.sem_clear(range(nn, nn + 1))
        nc.all_engine_barrier()
        return nc


"""Emitters for the mk_hybrid_d4 megakernel. Feature-major convention:
activations [d on partitions (tiles of 128), tokens on free]. Per-core:
T=2048 tokens = 4 batches x 512 seq (batch-major column groups).

All emitters record into a trnkit.Ker (serial mode for v0 correctness).
Host staging helpers produce in_map entries (layout-only numpy).

PSUM pool (8 banks): ctx.PS[0..5] six (128,512) banks; ctx.PSW one (128,1024) = 2 banks.
Convention (serial mode, no concurrency): norm uses PS[0],PS[1],PSW; proj uses PS[2],PS[3];
other emitters may use PS[4],PS[5].
"""
import numpy as np
import ml_dtypes

import concourse.mybir as mybir

U8 = mybir.dt.uint8
I32 = mybir.dt.int32

B = 4
SC = 512
T = B * SC
D = 1024
DP = D // 128
TT = 512
NTT = T // TT
NCORES = 8
LSEQ = 4096
TWO_PI = float(2 * np.pi)


def bf(a):
    return np.ascontiguousarray(a).astype(ml_dtypes.bfloat16)


def cols(vec):
    n = vec.shape[0]
    return np.ascontiguousarray(vec.reshape(n // 128, 128).T)


class Ctx:
    def __init__(self, k: Ker):
        self.k = k
        self.onesf = k.sb("ones_f", (128, 1), F32)
        self.onesf1 = k.sb("ones_f1", (1, 128), F32)
        self.onesb1 = k.sb("ones_b1", (1, 128), BF16)
        k.g("memset", self.onesf[:], 1.0)
        k.g("memset", self.onesf1[:], 1.0)
        k.g("memset", self.onesb1[:], 1.0)
        self.PS = [k.ps(f"PS{i}", (128, 512)) for i in range(6)]
        self.PSW = k.ps("PSW", (128, 1024))
        self.identb = None
        self._scr = {}
        self._wp = None
        self._hp = None
        self._pool8k = None

    @property
    def pool8k(self):
        if self._pool8k is None:
            self._pool8k = self.k.sb("pool8k", (128, 4096), BF16)
        return self._pool8k

    @property
    def wp(self):
        if self._wp is None:
            self._wp = WStripes(self, "wp", 8, 1024)
        return self._wp

    @property
    def hp(self):
        if self._hp is None:
            self._hp = [self.k.sb(f"hp{i}", (128, TT), BF16) for i in range(44)]
        return self._hp

    def scr(self, name, shape, dt=F32):
        key = name
        if key not in self._scr:
            self._scr[key] = self.k.sb("scr_" + name, shape, dt)
            self._scrmeta = getattr(self, "_scrmeta", {})
            self._scrmeta[key] = (tuple(shape), dt)
        else:
            m = self._scrmeta[key]
            assert m == (tuple(shape), dt), f"scr {name} shape mismatch {m} vs {shape},{dt}"
        return self._scr[key]

    def ident_bf(self):
        if self.identb is None:
            self.identb = self.k.sb("ident_bf", (128, 128), BF16)
            ic = self.k.const(np.eye(128, dtype=np.float32), "identc")
            self.k.g("dma_start", out=self.identb[:], in_=ic.ap())
        return self.identb

    def ident_f32(self):
        if not hasattr(self, "identf"):
            self.identf = self.k.sb("ident_f32", (128, 128))
            icf = self.k.const(np.eye(128, dtype=np.float32), "identcf")
            self.k.g("dma_start", out=self.identf[:], in_=icf.ap())
        return self.identf

    def bcast_row(self, row_ap, out_psum_ap, n):
        self.k.t("matmul", out_psum_ap, self.onesf1[:], row_ap, start=True, stop=True)


def emit_layernorm(ctx, x_tiles, xn_tiles, tt0, ttw=TT, xn2_tiles=None, xn_local=True):
    k = ctx.k
    psA, psB, psW = ctx.PS[0], ctx.PS[1], ctx.PSW
    sq = ctx.scr("ln_sq", (128, TT))
    rows = ctx.scr("ln_rows", (1, 2 * TT))
    sl = slice(tt0, tt0 + ttw)
    for dp in range(DP):
        k.s("activation", sq[:, 0:ttw], x_tiles[dp][:, sl], AF.Square)
        k.t("matmul", psA[0:1, 0:ttw], ctx.onesf[:], x_tiles[dp][:, sl],
            start=(dp == 0), stop=(dp == DP - 1))
        k.t("matmul", psB[0:1, 0:ttw], ctx.onesf[:], sq[:, 0:ttw],
            start=(dp == 0), stop=(dp == DP - 1))
    k.v("tensor_scalar_mul", rows[0:1, 0:ttw], psA[0:1, 0:ttw], 1.0 / D)
    k.v("tensor_scalar_mul", rows[0:1, TT:TT + ttw], psB[0:1, 0:ttw], 1.0 / D)
    musq = ctx.scr("ln_musq", (1, TT))
    k.v("tensor_mul", musq[0:1, 0:ttw], rows[0:1, 0:ttw], rows[0:1, 0:ttw])
    k.v("tensor_sub", rows[0:1, TT:TT + ttw], musq[0:1, 0:ttw], rows[0:1, TT:TT + ttw])
    k.s("activation", rows[0:1, TT:TT + ttw], rows[0:1, TT:TT + ttw], AF.Sqrt,
        bias=ctx.k.fconst(1e-5)[0:1, :], scale=-1.0)
    k.v("reciprocal", rows[0:1, TT:TT + ttw], rows[0:1, TT:TT + ttw])
    k.t("matmul", psW[:, 0:ttw], ctx.onesf1[:], rows[0:1, 0:ttw], start=True, stop=True)
    k.t("matmul", psW[:, TT:TT + ttw], ctx.onesf1[:], rows[0:1, TT:TT + ttw], start=True, stop=True)
    sub = ctx.scr("ln_sub", (128, TT))
    osl = slice(0, ttw) if xn_local else sl
    for dp in range(DP):
        k.v("tensor_sub", sub[:, 0:ttw], x_tiles[dp][:, sl], psW[:, 0:ttw])
        k.v("tensor_mul", xn_tiles[dp][:, osl], sub[:, 0:ttw], psW[:, TT:TT + ttw])
        if xn2_tiles is not None:
            k.v("tensor_mul", xn2_tiles[dp][:, osl], sub[:, 0:ttw], psW[:, TT:TT + ttw])


class WStripes:

    def __init__(self, ctx, tag, kp_n, ncols):
        self.KP = kp_n
        self.ncols = ncols
        self.sb = [ctx.k.sb(f"w_{tag}_{i}", (128, ncols), BF16) for i in range(kp_n)]

    def load(self, ctx, w_dram_ap, c0, cw, r0=0, dst0=0):
        for kp in range(self.KP):
            ctx.k.g("dma_start", out=self.sb[kp][:, dst0:dst0 + cw],
                    in_=w_dram_ap[r0 + kp * 128:r0 + (kp + 1) * 128, c0:c0 + cw])


def emit_mm_chain(ctx, ps_ap, stripes, m0, in_tiles, tsl, kps=None, start=True, stop=True):
    k = ctx.k
    kps = kps if kps is not None else range(stripes.KP)
    kps = list(kps)
    for i, kp in enumerate(kps):
        k.t("matmul", ps_ap, stripes.sb[kp][:, m0:m0 + 128], in_tiles[kp][:, tsl],
            start=(start and i == 0), stop=(stop and i == len(kps) - 1))


def emit_sin(ctx, out_ap, in_ap, shape):
    k = ctx.k
    p, n = shape
    mi = ctx.scr("l2f_h0", (128, TT))[:].bitcast(I32)
    mf = ctx.scr("moeha", (128, TT))
    k.v("tensor_scalar_mul", mi[0:p, 0:n], in_ap, 1.0 / TWO_PI)
    k.v("tensor_copy", mf[0:p, 0:n], mi[0:p, 0:n])
    k.v("scalar_tensor_tensor", mf[0:p, 0:n], mf[0:p, 0:n], -TWO_PI, in_ap,
        ALU.mult, ALU.add)
    msk = ctx.scr("l2f_h0", (128, TT))
    k.v("tensor_scalar", msk[0:p, 0:n], mf[0:p, 0:n], float(np.pi), -TWO_PI,
        ALU.is_gt, ALU.mult)
    k.v("tensor_tensor", mf[0:p, 0:n], mf[0:p, 0:n], msk[0:p, 0:n], ALU.add)
    k.v("tensor_scalar", msk[0:p, 0:n], mf[0:p, 0:n], float(-np.pi), TWO_PI,
        ALU.is_lt, ALU.mult)
    k.v("tensor_tensor", mf[0:p, 0:n], mf[0:p, 0:n], msk[0:p, 0:n], ALU.add)
    k.s("activation", out_ap, mf[0:p, 0:n], AF.Sin)


def emit_l0_ffn(ctx, x_tiles, d):
    k = ctx.k
    DFF = 4096
    MP1 = DFF // 128
    xn = [ctx.scr(f"xn{i}", (128, TT), BF16) for i in range(DP)]
    hp = ctx.hp
    w1s = ctx.wp
    w2s = ctx.wp
    W1CH = 4
    b1c = ctx.scr("l0_b1c", (128, MP1))
    mpc = ctx.scr("l0_mpc", (128, MP1))
    thc = ctx.scr("l0_thc", (128, MP1))
    b2c = ctx.scr("l0_b2c", (128, DP))
    tmc = ctx.scr("l0_tmc", (128, MP1))
    k.g("dma_start", out=b1c[:], in_=d["l0f_b1c"].ap())
    k.g("dma_start", out=thc[:], in_=d["l0f_thetac"].ap())
    k.g("dma_start", out=mpc[:], in_=d["l0f_wic"].ap())
    k.g("dma_start", out=tmc[:], in_=d["l0f_wrc"].ap())
    k.g("dma_start", out=b2c[:], in_=d["l0f_b2c"].ap())
    sn = ctx.scr("l0_sn", (128, MP1))
    cs = ctx.scr("l0_cs", (128, MP1))
    emit_sin(ctx, sn[:], thc[:], (128, MP1))
    k.v("tensor_scalar_add", cs[:], thc[:], float(np.pi / 2))
    emit_sin(ctx, cs[:], cs[:], (128, MP1))
    k.v("tensor_mul", cs[:], cs[:], tmc[:])
    k.v("tensor_mul", sn[:], sn[:], mpc[:])
    k.v("tensor_add", cs[:], cs[:], sn[:])
    k.v("tensor_mul", mpc[:], mpc[:], cs[:])
    k.v("tensor_scalar_mul", mpc[:], mpc[:], 0.1)

    hval = ctx.scr("l0_hval", (128, TT))
    arg = ctx.scr("l0_arg", (128, TT))
    ps8 = ctx.PS + [ctx.PSW[:, 0:512], ctx.PSW[:, 512:1024]]
    for tt in range(NTT):
        tsl = slice(tt * TT, (tt + 1) * TT)
        emit_layernorm(ctx, x_tiles, xn, tt * TT)
        xnsl = slice(tt * TT, (tt + 1) * TT) if False else slice(0, TT)
        for c in range(4):
            w1s.load(ctx, d["l0f_W1"].ap(), c * 1024, 1024)
            for ml in range(8):
                mp = c * 8 + ml
                ps = ctx.PS[2 + mp % 2]
                emit_mm_chain(ctx, ps[:, 0:TT], w1s, ml * 128, xn, xnsl)
                k.v("tensor_scalar_add", hval[:], ps[:, 0:TT], b1c[:, mp:mp + 1])
                k.v("tensor_scalar", arg[:], hval[:], 0.1, thc[:, mp:mp + 1], ALU.mult, ALU.add)
                emit_sin(ctx, arg[:], arg[:], (128, TT))
                k.v("scalar_tensor_tensor", hval[:], arg[:], mpc[:, mp:mp + 1], hval[:],
                    ALU.mult, ALU.add)
                k.v("tensor_relu", hp[mp][:, 0:TT], hval[:])
        for kc in range(4):
            w2s.load(ctx, d["l0f_W2"].ap(), 0, D, r0=kc * 1024)
            for mp in range(DP):
                emit_mm_chain(ctx, ps8[mp][0:128, 0:TT] if mp >= 6 else ps8[mp][:, 0:TT],
                              w2s, mp * 128, hp[kc * 8:(kc + 1) * 8], slice(0, TT),
                              start=(kc == 0), stop=(kc == 3))
        for mp in range(DP):
            psap = ps8[mp][0:128, 0:TT] if mp >= 6 else ps8[mp][:, 0:TT]
            k.v("tensor_scalar_add", hval[:], psap, b2c[:, mp:mp + 1])
            k.v("tensor_add", x_tiles[mp][:, tsl], x_tiles[mp][:, tsl], hval[:])


def stage_l0_ffn(inp):
    return {
        "l0f_W1": inp["L0_ffn_W1"],
        "l0f_b1c": cols(inp["L0_ffn_b1"]),
        "l0f_thetac": cols(inp["L0_ffn_quantum_phases"]),
        "l0f_wrc": cols(inp["L0_ffn_quantum_weights_real"]),
        "l0f_wic": cols(inp["L0_ffn_quantum_weights_imag"]),
        "l0f_W2": inp["L0_ffn_W2"],
        "l0f_b2c": cols(inp["L0_ffn_b2"]),
    }


def decl_l0_ffn(nc):
    d = {}
    d["l0f_W1"] = nc.dram_tensor("l0f_W1", [D, 4096], F32, kind="ExternalInput")
    d["l0f_b1c"] = nc.dram_tensor("l0f_b1c", [128, 32], F32, kind="ExternalInput")
    d["l0f_thetac"] = nc.dram_tensor("l0f_thetac", [128, 32], F32, kind="ExternalInput")
    d["l0f_wrc"] = nc.dram_tensor("l0f_wrc", [128, 32], F32, kind="ExternalInput")
    d["l0f_wic"] = nc.dram_tensor("l0f_wic", [128, 32], F32, kind="ExternalInput")
    d["l0f_W2"] = nc.dram_tensor("l0f_W2", [4096, D], F32, kind="ExternalInput")
    d["l0f_b2c"] = nc.dram_tensor("l0f_b2c", [128, 8], F32, kind="ExternalInput")
    return d


def host_fm(x_full, core):
    xs = x_full[:, core * SC:(core + 1) * SC, :]
    return np.ascontiguousarray(xs.transpose(2, 0, 1).reshape(D, T))


def host_unfm(x_fm_list):
    out = np.zeros((B, LSEQ, D), dtype=np.float32)
    for c, xf in enumerate(x_fm_list):
        out[:, c * SC:(c + 1) * SC, :] = np.asarray(xf, dtype=np.float32).reshape(D, B, SC).transpose(1, 2, 0)
    return out


def make_x_io(k):
    nc = k.nc
    x_in = nc.dram_tensor("x_in", [D, T], F32, kind="ExternalInput")
    x_out = nc.dram_tensor("x_out", [D, T], F32, kind="ExternalOutput")
    xt = [k.sb(f"x{i}", (128, T), F32) for i in range(DP)]
    for i in range(DP):
        k.y("dma_start", out=xt[i][:], in_=x_in.ap()[i * 128:(i + 1) * 128, :])
    return x_in, x_out, xt


def store_x(k, x_out, xt):
    for i in range(DP):
        k.y("dma_start", out=x_out.ap()[i * 128:(i + 1) * 128, :], in_=xt[i][:])
    k.wait("sync", k.c)


def emit_bitnet_quant(ctx, w_dram, wq_dram, n_in, n_out, tag, row0=0, row1=None):
    k = ctx.k
    row1 = row1 if row1 is not None else n_in
    wt = ctx.scr("ln_sq", (128, 512))
    ab = ctx.scr("ln_sub", (128, 512))
    acc = ctx.scr("bq_acc", (128, 1))
    col = ctx.scr("bq_col", (128, 1))
    thr = ctx.scr("bq_thr", (128, 2))
    row11 = ctx.scr("bq_row", (1, 2))
    qa = ctx.scr("l0_hval", (128, 512))
    qb = ctx.scr("at_pt", (128, 512), BF16)
    ps = ctx.PS[4]
    k.g("memset", acc[:], 0.0)
    for r0 in range(0, n_in, 128):
        for c0 in range(0, n_out, 512):
            k.y("dma_start", out=wt[:], in_=w_dram.ap()[r0:r0 + 128, c0:c0 + 512])
            k.s("activation", ab[:], wt[:], AF.Abs)
            k.v("tensor_reduce", col[:], ab[:], AX.X, ALU.add)
            k.v("tensor_add", acc[:], acc[:], col[:])
    k.t("matmul", ps[0:1, 0:1], ctx.onesf[:], acc[:], start=True, stop=True)
    k.v("tensor_copy", row11[0:1, 0:1], ps[0:1, 0:1])
    k.t("matmul", ps[:, 0:1], ctx.onesf1[:], row11[0:1, 0:1], start=True, stop=True)
    NW = float(n_in * n_out)
    k.v("tensor_scalar", thr[:, 0:1], ps[:, 0:1], 0.5 / NW, 0.5e-5, ALU.mult, ALU.add)
    k.v("tensor_scalar", thr[:, 1:2], ps[:, 0:1], -0.5 / NW, -0.5e-5, ALU.mult, ALU.add)
    for r0 in range(row0, row1, 128):
        for c0 in range(0, n_out, 512):
            k.y("dma_start", out=wt[:], in_=w_dram.ap()[r0:r0 + 128, c0:c0 + 512])
            k.v("tensor_single_scalar", qa[:], wt[:], thr[:, 0:1], ALU.is_gt)
            k.v("tensor_single_scalar", ab[:], wt[:], thr[:, 1:2], ALU.is_lt)
            k.v("tensor_sub", qb[:], qa[:], ab[:])
            k.y("dma_start", out=wq_dram.ap()[r0:r0 + 128, c0:c0 + 512], in_=qb[:])


def emit_l3_ffn(ctx, x_tiles, d):
    k = ctx.k
    DFF = 4096
    MP1 = DFF // 128
    xn = [ctx.scr(f"xn{i}", (128, TT), BF16) for i in range(DP)]
    hp = ctx.hp
    w1s = ctx.wp
    w2s = ctx.wp
    b1c = ctx.scr("l3_b1c", (128, MP1))
    b2c = ctx.scr("l3_b2c", (128, DP))
    k.g("dma_start", out=b1c[:], in_=d["l3f_b1c"].ap())
    k.g("dma_start", out=b2c[:], in_=d["l3f_b2c"].ap())
    hval = ctx.scr("l0_hval", (128, TT))
    ps8 = ctx.PS + [ctx.PSW[:, 0:512], ctx.PSW[:, 512:1024]]
    for tt in range(NTT):
        tsl = slice(tt * TT, (tt + 1) * TT)
        emit_layernorm(ctx, x_tiles, xn, tt * TT)
        for c in range(4):
            w1s.load(ctx, d["l3f_w1q"].ap(), c * 1024, 1024)
            for ml in range(8):
                mp = c * 8 + ml
                ps = ctx.PS[2 + mp % 2]
                emit_mm_chain(ctx, ps[:, 0:TT], w1s, ml * 128, xn, slice(0, TT))
                k.v("tensor_scalar_add", hval[:], ps[:, 0:TT], b1c[:, mp:mp + 1])
                k.v("tensor_relu", hp[mp][:, 0:TT], hval[:])
        for kc in range(4):
            w2s.load(ctx, d["l3f_w2q"].ap(), 0, D, r0=kc * 1024)
            for mp in range(DP):
                emit_mm_chain(ctx, ps8[mp][:, 0:TT], w2s, mp * 128, hp[kc * 8:(kc + 1) * 8],
                              slice(0, TT), start=(kc == 0), stop=(kc == 3))
        for mp in range(DP):
            k.v("tensor_scalar_add", hval[:], ps8[mp][:, 0:TT], b2c[:, mp:mp + 1])
            k.v("tensor_add", x_tiles[mp][:, tsl], x_tiles[mp][:, tsl], hval[:])


def decl_l3_ffn(nc, quant=True):
    d = {}
    d["l3f_W1T"] = nc.dram_tensor("l3f_W1T", [D, 4096], F32, kind="ExternalInput")
    d["l3f_W2T"] = nc.dram_tensor("l3f_W2T", [4096, D], F32, kind="ExternalInput")
    d["l3f_b1c"] = nc.dram_tensor("l3f_b1c", [128, 32], F32, kind="ExternalInput")
    d["l3f_b2c"] = nc.dram_tensor("l3f_b2c", [128, 8], F32, kind="ExternalInput")
    d["l3f_w1q"] = nc.dram_tensor("l3f_w1q", [D, 4096], BF16, kind="Internal")
    d["l3f_w2q"] = nc.dram_tensor("l3f_w2q", [4096, D], BF16, kind="Internal")
    return d


def stage_l3_ffn(inp):
    return {
        "l3f_W1T": np.ascontiguousarray(inp["L3_ffn_W1"].T),
        "l3f_W2T": np.ascontiguousarray(inp["L3_ffn_W2"].T),
        "l3f_b1c": cols(inp["L3_ffn_b1"]),
        "l3f_b2c": cols(inp["L3_ffn_b2"]),
    }


def emit_l1_ffn(ctx, x_tiles, d):
    k = ctx.k
    H = 5632
    HMP = H // 128
    HC = 512
    NHC = H // HC
    HCM = HC // 128
    xn = [ctx.scr(f"xn{i}", (128, TT), BF16) for i in range(DP)]
    hp = ctx.hp
    y_sb = [ctx.scr(f"moey{i}", (128, TT)) for i in range(DP)]
    w13 = ctx.wp
    w2s = ctx.wp
    gws = ctx.scr("moegw", (128, DP * 8), BF16)
    for kp in range(DP):
        k.g("dma_start", out=gws[:, kp * 8:(kp + 1) * 8], in_=d["l1f_gateT"].ap()[kp * 128:(kp + 1) * 128, :])
    ha = ctx.scr("moeha", (128, TT))
    hb = ctx.scr("moehb", (128, TT))
    lg = ctx.scr("moelg", (128, 8))
    m1 = ctx.scr("moem1", (128, 2))
    cwq = ctx.scr("moecw", (128, 4), BF16)
    cwpad = ctx.scr("c2r_pad", (128, 128), BF16)
    cwrow = ctx.pool8k[0:1, 0:2048]
    cwT = ctx.scr("c2r_T", (128, 128), BF16)
    cwbc = ctx.scr("at_pt", (128, TT), BF16)
    k.g("memset", cwpad[:], 0.0)
    psT = ctx.PS[5][:, 0:64].bitcast(BF16)
    for tt in range(NTT):
        tsl = slice(tt * TT, (tt + 1) * TT)
        emit_layernorm(ctx, x_tiles, xn, tt * TT)
        for tq in range(4):
            qsl = slice(tq * 128, (tq + 1) * 128)
            for kp in range(DP):
                k.t("matmul", ctx.PS[4][0:128, 0:8], xn[kp][:, qsl], gws[:, kp * 8:(kp + 1) * 8],
                    start=(kp == 0), stop=(kp == DP - 1))
            k.v("tensor_copy", lg[:], ctx.PS[4][0:128, 0:8])
            k.v("reduce_max", m1[:, 0:1], lg[:, 0:4], AX.X)
            k.v("tensor_single_scalar", ha[:, 0:4], lg[:, 0:4], m1[:, 0:1], ALU.is_ge)
            k.v("scalar_tensor_tensor", hb[:, 0:4], ha[:, 0:4], -1e30, lg[:, 0:4], ALU.mult, ALU.add)
            k.v("reduce_max", m1[:, 1:2], hb[:, 0:4], AX.X)
            k.v("tensor_single_scalar", ha[:, 0:4], lg[:, 0:4], m1[:, 1:2], ALU.is_ge)
            k.v("tensor_scalar_mul", m1[:, 0:1], m1[:, 0:1], -1.0)
            k.s("activation", hb[:, 0:4], lg[:, 0:4], AF.Exp, bias=m1[:, 0:1])
            k.v("tensor_mul", hb[:, 0:4], hb[:, 0:4], ha[:, 0:4])
            k.v("tensor_reduce", m1[:, 1:2], hb[:, 0:4], AX.X, ALU.add)
            k.v("reciprocal", m1[:, 1:2], m1[:, 1:2])
            k.v("tensor_single_scalar", cwq[:, 0:4], hb[:, 0:4], m1[:, 1:2], ALU.mult)
            k.v("tensor_copy", cwpad[:, 0:4], cwq[:, 0:4])
            k.t("transpose", psT, cwpad[:], ctx.ident_bf()[:])
            k.v("tensor_copy", cwT[:], psT[0:128, 0:128])
            for e in range(4):
                k.g("dma_start", out=cwrow[0:1, e * TT + tq * 128:e * TT + (tq + 1) * 128],
                    in_=cwT[e:e + 1, 0:128])
        for e in range(4):
            k.t("matmul", ctx.PS[4][0:128, 0:TT], ctx.onesb1[:], cwrow[0:1, e * TT:(e + 1) * TT],
                start=True, stop=True)
            k.v("tensor_copy", cwbc[:], ctx.PS[4][0:128, 0:TT])
            for hc in range(NHC):
                w13.load(ctx, d["l1f_w1"].ap()[e], hc * HC, HC, dst0=0)
                w13.load(ctx, d["l1f_w3"].ap()[e], hc * HC, HC, dst0=HC)
                for ml in range(HCM):
                    mp = hc * HCM + ml
                    emit_mm_chain(ctx, ctx.PS[2][:, 0:TT], w13, ml * 128, xn, slice(0, TT))
                    k.s("activation", ha[:], ctx.PS[2][:, 0:TT], AF.Silu)
                    emit_mm_chain(ctx, ctx.PS[3][:, 0:TT], w13, HC + ml * 128, xn, slice(0, TT))
                    k.v("tensor_mul", hb[:], ctx.PS[3][:, 0:TT], cwbc[:])
                    k.v("tensor_mul", hp[mp][:, 0:TT], ha[:], hb[:])
            for kc in range(6):
                kps = list(range(kc * 8, min(44, (kc + 1) * 8)))
                for i, kp in enumerate(kps):
                    k.g("dma_start", out=w2s.sb[i][:, 0:D],
                        in_=d["l1f_w2"].ap()[e, kp * 128:(kp + 1) * 128, :])
                for mp in range(DP):
                    ps = ctx.PS[2 + mp % 2]
                    for i, kp in enumerate(kps):
                        k.t("matmul", ps[:, 0:TT], w2s.sb[i][:, mp * 128:(mp + 1) * 128],
                            hp[kp][:, 0:TT], start=(i == 0), stop=(i == len(kps) - 1))
                    if e == 0 and kc == 0:
                        k.v("tensor_copy", y_sb[mp][:, 0:TT], ps[:, 0:TT])
                    else:
                        k.v("tensor_add", y_sb[mp][:, 0:TT], y_sb[mp][:, 0:TT], ps[:, 0:TT])
        for mp in range(DP):
            k.v("tensor_add", x_tiles[mp][:, tsl], x_tiles[mp][:, tsl], y_sb[mp][:, 0:TT])


def decl_l1_ffn(nc):
    d = {}
    d["l1f_gateT"] = nc.dram_tensor("l1f_gateT", [D, 8], F32, kind="ExternalInput")
    d["l1f_w1"] = nc.dram_tensor("l1f_w1", [4, D, 5632], F32, kind="ExternalInput")
    d["l1f_w2"] = nc.dram_tensor("l1f_w2", [4, 5632, D], F32, kind="ExternalInput")
    d["l1f_w3"] = nc.dram_tensor("l1f_w3", [4, D, 5632], F32, kind="ExternalInput")
    return d


def stage_l1_ffn(inp):
    return {
        "l1f_gateT": np.ascontiguousarray(inp["L1_ffn_gate_w"].T),
        "l1f_w1": np.ascontiguousarray(inp["L1_ffn_w1"][:4]),
        "l1f_w2": np.ascontiguousarray(inp["L1_ffn_w2"][:4]),
        "l1f_w3": np.ascontiguousarray(inp["L1_ffn_w3"][:4]),
    }



def attn_drams(nc, tag, kdim=1024, vdim=1024):
    d = {}
    d["k_in"] = nc.dram_tensor(f"{tag}_kin", [B * kdim, SC], BF16, kind="Internal")
    d["k_out"] = nc.dram_tensor(f"{tag}_kout", [NCORES * B * kdim, SC], BF16, kind="Internal")
    d["v_in"] = nc.dram_tensor(f"{tag}_vin", [T, vdim], BF16, kind="Internal")
    d["v_out"] = nc.dram_tensor(f"{tag}_vout", [NCORES * T, vdim], BF16, kind="Internal")
    return d


def emit_ag(ctx, ad):
    k = ctx.k
    k.g("collective_compute", "AllGather", ALU.bypass,
        replica_groups=[list(range(NCORES))],
        ins=[ad["k_in"].ap().opt()], outs=[ad["k_out"].ap().opt()])
    k.g("collective_compute", "AllGather", ALU.bypass,
        replica_groups=[list(range(NCORES))],
        ins=[ad["v_in"].ap().opt()], outs=[ad["v_out"].ap().opt()])


def emit_cols_to_row(ctx, src_ap, n, dst_row, dst_off, f32=False):
    k = ctx.k
    if f32:
        pad = ctx.scr("ln_sq", (128, TT))[:, 0:128]
        cT = ctx.scr("ln_sub", (128, TT))[:, 0:128]
        psT = ctx.PS[5][:, 0:128]
        ident = ctx.ident_f32()
    else:
        pad = ctx.scr("c2r_pad", (128, 128), BF16)
        cT = ctx.scr("c2r_T", (128, 128), BF16)
        psT = ctx.PS[5][:, 0:64].bitcast(BF16)
        ident = ctx.ident_bf()
    k.v("tensor_copy", pad[:, 0:n], src_ap)
    k.t("transpose", psT, pad[:, 0:128] if f32 else pad[:], ident[:])
    k.v("tensor_copy", cT[:, 0:128] if f32 else cT[:], psT[0:128, 0:128])
    for j in range(n):
        k.g("dma_start", out=dst_row[0:1, dst_off + j * 128:dst_off + (j + 1) * 128],
            in_=cT[j:j + 1, 0:128])


def emit_qkv_phase(ctx, x_tiles, d, ad, qscale, rope=None):
    k = ctx.k
    xn = [ctx.scr(f"xn{i}", (128, TT), BF16) for i in range(DP)]
    qsc2 = ctx.scr("at_qsc2", (128, TT), BF16)
    vsc = ctx.scr("at_mbcf", (128, 512))[:].bitcast(BF16)
    bvrow = ctx.scr("at_bvrow", (1, 1024), BF16)
    bvf = ctx.scr("ln_rows", (1, 2 * TT))
    hp = ctx.hp
    wq = ctx.wp
    k.g("dma_start", out=bvf[0:1, :], in_=d["bvrow"].ap())
    k.v("tensor_copy", bvrow[0:1, :], bvf[0:1, :])
    qbc = ctx.scr("at_qbc", (128, DP))
    kbc = ctx.scr("at_kbc", (128, DP))
    k.g("dma_start", out=qbc[:], in_=d["qbc"].ap())
    k.g("dma_start", out=kbc[:], in_=d["kbc"].ap())
    for b in range(B):
        if rope is not None:
            k.g("dma_start", out=rope["rc"][:, 0:TT], in_=rope["cosd"].ap()[:, b * TT:(b + 1) * TT])
            k.g("dma_start", out=rope["rc"][:, TT:2 * TT], in_=rope["sind"].ap()[:, b * TT:(b + 1) * TT])
        emit_layernorm(ctx, x_tiles, xn, b * TT)
        wq.load(ctx, d["qT"].ap(), 0, D)
        for mp in range(DP):
            ps = ctx.PS[2 + mp % 2]
            emit_mm_chain(ctx, ps[:, 0:TT], wq, mp * 128, xn, slice(0, TT))
            k.v("tensor_scalar", hp[mp * 4 + b][:, 0:TT], ps[:, 0:TT],
                qbc[:, mp:mp + 1], qscale, ALU.add, ALU.mult)
            if rope is not None:
                emit_rope(ctx, hp[mp * 4 + b], b, rope, qsc2)
        wq.load(ctx, d["kT"].ap(), 0, D)
        for mp in range(DP):
            ps = ctx.PS[2 + mp % 2]
            emit_mm_chain(ctx, ps[:, 0:TT], wq, mp * 128, xn, slice(0, TT))
            k.v("tensor_scalar", qsc2[:, 0:TT], ps[:, 0:TT],
                kbc[:, mp:mp + 1], 1.0, ALU.add, ALU.mult)
            if rope is not None:
                emit_rope(ctx, qsc2, b, rope, ctx.scr("at_qsc3", (128, TT), BF16))
            k.y("dma_start", out=ad["k_in"].ap()[b * D + mp * 128:b * D + (mp + 1) * 128, :],
                in_=qsc2[:, 0:TT])
        wq.load(ctx, d["vT"].ap(), 0, D)
        for ti in range(4):
            qsl = slice(ti * 128, (ti + 1) * 128)
            for half in range(2):
                psv = ctx.PS[2 + half]
                for kp in range(DP):
                    k.t("matmul", psv[0:128, 0:512], xn[kp][:, qsl],
                        wq.sb[kp][:, half * 512:(half + 1) * 512],
                        start=(kp == 0), stop=False)
                k.t("matmul", psv[0:128, 0:512], ctx.onesb1[:],
                    bvrow[0:1, half * 512:(half + 1) * 512], start=False, stop=True)
                k.v("tensor_copy", vsc[:, half * 512:(half + 1) * 512], psv[0:128, 0:512])
            k.y("dma_start", out=ad["v_in"].ap()[b * 512 + ti * 128:b * 512 + (ti + 1) * 128, :],
                in_=vsc)
    return xn


def emit_rope(ctx, tile, b, rope, scratch):
    k = ctx.k
    rot = ctx.PS[4]
    k.t("matmul", rot[:, 0:TT], rope["RT"][:], tile[:, 0:TT], start=True, stop=True)
    k.v("tensor_mul", scratch[:, 0:TT], tile[:, 0:TT], rope["cos"])
    k.v("tensor_mul", tile[:, 0:TT], rot[:, 0:TT], rope["sin"])
    k.v("tensor_add", tile[:, 0:TT], tile[:, 0:TT], scratch[:, 0:TT])


def emit_attn_softmax(ctx, x_tiles, d, ad, qscale, ntsets, rope=False, ghost=False, exactmax=False):
    k = ctx.k
    hp = ctx.hp
    hp0 = ctx.hp
    tsets = []
    for i in range(ntsets):
        tiles = [hp0[32 + i * 4 + sub] for sub in range(4)]
        sel = ctx.scr(f"at_sel{i}", (128, 8))
        for sub in range(4):
            k.g("dma_start", out=tiles[sub][:, 0:512],
                in_=d[f"tset{i}_tiles"].ap()[sub * 128:(sub + 1) * 128, :])
        k.g("dma_start", out=sel[:], in_=d[f"tset{i}_sel"].ap())
        tsets.append((tiles, sel))
    tsetsQ = []
    if exactmax:
        for i in range(ntsets):
            tilesq = [hp0[36 + i * 4 + qi] for qi in range(4)]
            for qi in range(4):
                k.g("dma_start", out=tilesq[qi][:, 0:512],
                    in_=d[f"tset{i}_tilesQ"].ap()[qi * 128:(qi + 1) * 128, :])
            tsetsQ.append(tilesq)
    offc = ctx.scr("at_offc", (128, 8))
    k.g("dma_start", out=offc[:], in_=d["offc"].ap())
    ghostc = None
    if ghost:
        ghostc = ctx.scr("at_ghostc", (1, 16))
        k.g("dma_start", out=ghostc[0:1, :], in_=d["ghostc"].ap())
    ropecfg = None
    if rope:
        rc = ctx.scr("at_ropec", (128, 2 * TT), BF16)
        rt = ctx.scr("at_ropeRT", (128, 128), BF16)
        k.g("dma_start", out=rt[:], in_=d["ropeRT"].ap())
        ropecfg = {"RT": rt, "cos": rc[:, 0:TT], "sin": rc[:, TT:2 * TT],
                   "cosd": d["ropecos"], "sind": d["ropesin"], "rc": rc}

    xn = emit_qkv_phase(ctx, x_tiles, d, ad, qscale, ropecfg)
    emit_ag(ctx, ad)

    kbh = ctx.pool8k[0:64, 0:4096]
    vbh01 = (ctx.wp.sb[0], ctx.wp.sb[1])
    mcols = ctx.scr("at_mcols", (128, 8))
    mall = ctx.scr("at_mall", (128, 4))
    mrow = ctx.scr("ln_rows", (1, 2 * TT))[0:1, 0:256].bitcast(BF16)
    mbc = ctx.scr("at_mbc", (128, 512), BF16)
    st = ctx.scr("at_st", (128, 512))
    pt = ctx.scr("at_pt", (128, 512), BF16)
    mrowf = ctx.scr("mb_brow", (1, TT)) if exactmax else None
    mbcf = ctx.scr("at_mbcf", (128, 512))
    lrow = ctx.scr("ln_musq", (1, TT))
    rlb = ctx.scr("at_mbcf", (128, 512))[0:64, :]
    onesbc = ctx.scr("at_onesbc", (128, 1), BF16)
    obc = ctx.scr("at_obc", (128, DP))
    k.g("memset", onesbc[:], 1.0)
    k.g("dma_start", out=obc[:], in_=d["obc"].ap())
    kout_r = ad["k_out"].ap().rearrange("(kc x) s -> x kc s", kc=NCORES)
    vout_r = ad["v_out"].ap().rearrange("(kc z hi p) o -> z p kc hi o", kc=NCORES, z=B, hi=4)
    oall = xn

    qwork = ctx.scr("at_qwork", (64, TT), BF16)
    for b in range(B):
        for h in range(16):
            qtile = hp[(h // 2) * 4 + b]
            r0 = (h % 2) * 64
            k.v("tensor_copy", qwork[:, 0:TT], qtile[r0:r0 + 64, 0:TT])
            qall = qwork[0:64, 0:TT]
            k.y("dma_start", out=kbh.rearrange("p (kc s) -> p kc s", kc=8),
                in_=kout_r[b * D + h * 64:b * D + h * 64 + 64])
            for kc in range(8):
                vb = vbh01[kc // 4]
                c0 = (kc % 4) * 256
                k.y("dma_start",
                    out=vb[:, c0:c0 + 256].rearrange("p (hi dd) -> p hi dd", hi=4),
                    in_=vout_r[b][:, kc, :, h * 64:(h + 1) * 64])
            for qi in range(4):
                qlhs = qwork[0:64, qi * 128:(qi + 1) * 128]
                for kc in range(8):
                    ps = ctx.PS[2 + kc % 2]
                    k.t("matmul", ps[0:128, 0:512], qlhs, kbh[0:64, kc * 512:(kc + 1) * 512],
                        start=True, stop=True)
                    if exactmax:
                        k.v("tensor_single_scalar", st[:], ps[0:128, 0:512],
                            offc[:, kc:kc + 1], ALU.add)
                        for i, (tiles_, sel_) in enumerate(tsets):
                            k.v("scalar_tensor_tensor", st[:], tsetsQ[i][qi][:, 0:512],
                                sel_[:, kc:kc + 1], st[:], ALU.mult, ALU.add)
                        k.v("reduce_max", mcols[:, kc:kc + 1], st[:], AX.X)
                    else:
                        k.v("reduce_max", mcols[:, kc:kc + 1], ps[0:128, 0:512], AX.X)
                k.v("reduce_max", mall[:, qi:qi + 1], mcols[:], AX.X)
            if exactmax:
                emit_cols_to_row(ctx, mall[:], 4, mrowf, 0, f32=True)
                k.t("matmul", ctx.PS[4][0:128, 0:512], ctx.onesf1[:], mrowf[0:1, :],
                    start=True, stop=True)
                k.v("tensor_copy", mbcf[:], ctx.PS[4][0:128, 0:512])
            else:
                emit_cols_to_row(ctx, mall[:], 4, mrow, 0)
                k.t("matmul", ctx.PS[4][0:128, 0:512], ctx.onesb1[:], mrow,
                    start=True, stop=True)
                k.v("tensor_copy", mbc[:], ctx.PS[4][0:128, 0:512])
            first = True
            for kc in range(8):
                for sub in range(4):
                    klhs = kbh[0:64, kc * 512 + sub * 128:kc * 512 + (sub + 1) * 128]
                    ps = ctx.PS[2 + sub % 2]
                    k.t("matmul", ps[0:128, 0:512], klhs, qall, start=True, stop=True)
                    k.v("tensor_sub", st[:], ps[0:128, 0:512], mbcf[:] if exactmax else mbc[:])
                    for (tiles, sel) in tsets:
                        k.v("scalar_tensor_tensor", st[:], tiles[sub][:, 0:512],
                            sel[:, kc:kc + 1], st[:], ALU.mult, ALU.add)
                    k.s("activation", pt[:], st[:], AF.Exp, bias=offc[:, kc:kc + 1])
                    k.t("matmul", ctx.PS[4][0:1, 0:512], onesbc[:], pt[:],
                        start=first, stop=(kc == 7 and sub == 3))
                    ti = kc * 4 + sub
                    vb = vbh01[ti // 16]
                    k.t("matmul", ctx.PS[5][0:64, 0:512], vb[:, (ti % 16) * 64:(ti % 16 + 1) * 64],
                        pt[:], start=first, stop=(kc == 7 and sub == 3))
                    first = False
            if ghost:
                k.v("tensor_single_scalar", lrow[0:1, :], ctx.PS[4][0:1, 0:512],
                    ghostc[0:1, h:h + 1], ALU.add)
                k.v("reciprocal", lrow[0:1, :], lrow[0:1, :])
            else:
                k.v("reciprocal", lrow[0:1, :], ctx.PS[4][0:1, 0:512])
            k.t("matmul", ctx.PS[2][0:64, 0:512], ctx.onesf1[0:1, 0:64], lrow[0:1, :],
                start=True, stop=True)
            k.v("tensor_copy", rlb[:], ctx.PS[2][0:64, 0:512])
            k.v("tensor_mul", oall[h // 2][r0:r0 + 64, 0:TT], ctx.PS[5][0:64, 0:512], rlb[:])
        ctx.wp.load(ctx, d["oT"].ap(), 0, D)
        hv = ctx.scr("at_st", (128, 512))
        for mp in range(DP):
            ps = ctx.PS[2 + mp % 2]
            emit_mm_chain(ctx, ps[:, 0:TT], ctx.wp, mp * 128, oall, slice(0, TT))
            k.v("scalar_tensor_tensor", hv[:], ps[:, 0:TT], 1.0,
                x_tiles[mp][:, b * TT:(b + 1) * TT], ALU.mult, ALU.add)
            k.v("tensor_scalar_add", x_tiles[mp][:, b * TT:(b + 1) * TT], hv[:],
                obc[:, mp:mp + 1])


def decl_attn_softmax(nc, tag, ntsets, rope=False, ghost=False):
    d = {}
    for nm in ("qT", "kT", "vT", "oT"):
        d[nm] = nc.dram_tensor(f"{tag}_{nm}", [D, D], F32, kind="ExternalInput")
    for nm in ("qbc", "kbc", "obc"):
        d[nm] = nc.dram_tensor(f"{tag}_{nm}", [128, 8], F32, kind="ExternalInput")
    d["bvrow"] = nc.dram_tensor(f"{tag}_bvrow", [1, 1024], F32, kind="ExternalInput")
    d["offc"] = nc.dram_tensor(f"{tag}_offc", [128, 8], F32, kind="ExternalInput")
    for i in range(ntsets):
        d[f"tset{i}_tiles"] = nc.dram_tensor(f"{tag}_ts{i}t", [512, 512], F32, kind="ExternalInput")
        d[f"tset{i}_sel"] = nc.dram_tensor(f"{tag}_ts{i}s", [128, 8], F32, kind="ExternalInput")
        d[f"tset{i}_tilesQ"] = nc.dram_tensor(f"{tag}_ts{i}tq", [512, 512], F32, kind="ExternalInput")
    if ghost:
        d["ghostc"] = nc.dram_tensor(f"{tag}_ghostc", [1, 16], F32, kind="ExternalInput")
    if rope:
        d["ropeRT"] = nc.dram_tensor(f"{tag}_ropeRT", [128, 128], F32, kind="ExternalInput")
        d["ropecos"] = nc.dram_tensor(f"{tag}_ropecos", [128, T], F32, kind="ExternalInput")
        d["ropesin"] = nc.dram_tensor(f"{tag}_ropesin", [128, T], F32, kind="ExternalInput")
    return d


def l0a_tq(tiles):
    full = tiles
    return np.ascontiguousarray(full.T)


def mask_causal(core):
    r = core
    offc = np.zeros((128, 8), np.float32)
    for kc in range(8):
        if kc > r:
            offc[:, kc] = -1e30
    tiles = np.zeros((512, 512), np.float32)
    for sub in range(4):
        for krow in range(128):
            kl = sub * 128 + krow
            tiles[sub * 128 + krow, :] = np.where(np.arange(512) >= kl, 0.0, -1e30)
    sel = np.zeros((128, 8), np.float32)
    sel[:, r] = 1.0
    return offc, tiles, sel


def mask_l1a(core):
    offc = np.zeros((128, 8), np.float32)
    tiles = np.zeros((512, 512), np.float32)
    for sub in range(4):
        for krow in range(128):
            kl = sub * 128 + krow
            tiles[sub * 128 + krow, :] = np.where(np.arange(512) >= kl, 0.0, -1e30)
    sel = np.ones((128, 8), np.float32)
    return offc, tiles, sel


def mask_l0a(core):
    r = core
    qg = 512 * r + np.arange(512)[None, :]
    offc = np.full((128, 8), -1e30, np.float32)
    tsets = []
    for which in range(3):
        kcs = {0: r, 1: r - 1, 2: 0}[which]
        sel = np.zeros((128, 8), np.float32)
        tiles = np.full((512, 512), 0.0, np.float32)
        if kcs >= 0 and not (which == 2 and (r == 0 or r == 1)):
            kg = 512 * kcs + np.arange(512)[:, None]
            diff = qg - kg
            win = (diff >= 0) & (diff < 256)
            col0 = (kg == 0)
            keep = win | col0
            tiles = np.where(keep, 0.0, -1e30).astype(np.float32)
            sel[:, kcs] = 1.0
            offc[:, kcs] = 0.0
        tsets.append((tiles, sel))
    return offc, tsets


def stage_l2_attn_b(inp, core):
    W = inp["L2_attn_b_Wqkv_w"]; bqkv = inp["L2_attn_b_Wqkv_b"]
    offc, tiles, sel = mask_causal(core)
    return {
        "l2b_qT": np.ascontiguousarray(W[0:1024].T), "l2b_kT": np.ascontiguousarray(W[1024:2048].T),
        "l2b_vT": np.ascontiguousarray(W[2048:3072].T),
        "l2b_oT": np.ascontiguousarray(inp["L2_attn_b_Wo_w"].T),
        "l2b_qbc": cols(bqkv[0:1024]), "l2b_kbc": cols(bqkv[1024:2048]),
        "l2b_obc": cols(inp["L2_attn_b_Wo_b"]),
        "l2b_bvrow": np.ascontiguousarray(bqkv[2048:3072].reshape(1, 1024)).astype(np.float32),
        "l2b_offc": offc, "l2b_ts0t": tiles, "l2b_ts0s": sel,
        "l2b_ts0tq": np.ascontiguousarray(tiles.T.reshape(4, 128, 512).reshape(512, 512)) if False else l0a_tq(tiles),
        "l2b_ghostc": np.ascontiguousarray(inp["L2_attn_b_ghost"].reshape(1, 16)),
    }


def stage_l3_attn_a(inp, core):
    qkv = inp["L3_attn_a_qkv_w"].reshape(1024, 16 * 192)
    qkvb = inp["L3_attn_a_qkv_b"].reshape(16 * 192)
    idx = np.arange(16 * 192).reshape(16, 192)
    qT = qkv[:, idx[:, 0:64].reshape(-1)]
    kT = qkv[:, idx[:, 64:128].reshape(-1)]
    vT = qkv[:, idx[:, 128:192].reshape(-1)]
    qb = qkvb[idx[:, 0:64].reshape(-1)]
    kb = qkvb[idx[:, 64:128].reshape(-1)]
    vb = qkvb[idx[:, 128:192].reshape(-1)]
    offc, tiles, sel = mask_causal(core)
    return {
        "l3a_qT": np.ascontiguousarray(qT), "l3a_kT": np.ascontiguousarray(kT),
        "l3a_vT": np.ascontiguousarray(vT),
        "l3a_oT": np.ascontiguousarray(inp["L3_attn_a_out_w"].reshape(1024, 1024)),
        "l3a_qbc": cols(qb), "l3a_kbc": cols(kb), "l3a_obc": cols(inp["L3_attn_a_out_b"]),
        "l3a_bvrow": np.ascontiguousarray(vb.reshape(1, 1024)),
        "l3a_offc": offc, "l3a_ts0t": tiles, "l3a_ts0s": sel,
        "l3a_ts0tq": l0a_tq(tiles),
    }


def stage_l1_attn_a(inp, core):
    offc, tiles, sel = mask_l1a(core)
    return {
        "l1a_qT": np.ascontiguousarray(inp["L1_attn_a_Wq"].T),
        "l1a_kT": np.ascontiguousarray(inp["L1_attn_a_Wk"].T),
        "l1a_vT": np.ascontiguousarray(inp["L1_attn_a_Wv"].T),
        "l1a_oT": np.ascontiguousarray(inp["L1_attn_a_Wo"].T),
        "l1a_qbc": cols(inp["L1_attn_a_bq"]), "l1a_kbc": cols(inp["L1_attn_a_bk"]),
        "l1a_obc": cols(inp["L1_attn_a_bo"]),
        "l1a_bvrow": np.ascontiguousarray(inp["L1_attn_a_bv"].reshape(1, 1024)),
        "l1a_offc": offc, "l1a_ts0t": tiles, "l1a_ts0s": sel,
        "l1a_ts0tq": l0a_tq(tiles),
    }


def host_rope_tables(core):
    DHEAD = 64
    inv_freq = 1.0 / 10000.0 ** (np.arange(0, DHEAD, 2, dtype=np.float64) / DHEAD)
    pos = np.arange(LSEQ, dtype=np.float64)
    freqs = pos[:, None] * inv_freq[None, :]
    emb = np.concatenate([freqs, freqs], axis=-1)
    cost, sint = np.cos(emb), np.sin(emb)
    sl = slice(core * SC, (core + 1) * SC)
    c = cost[sl].T
    sn = sint[sl].T
    def expand(t):
        t2 = np.concatenate([t, t], axis=0)
        return np.ascontiguousarray(np.tile(t2, (1, B))).astype(np.float32)
    R = np.zeros((64, 64), np.float32)
    R[np.arange(32), np.arange(32) + 32] = -1.0
    R[np.arange(32, 64), np.arange(32)] = 1.0
    RT = np.zeros((128, 128), np.float32)
    RT[0:64, 0:64] = R.T
    RT[64:128, 64:128] = R.T
    return expand(c), expand(sn), RT


def stage_l0_attn_a(inp, core):
    offc, tsets = mask_l0a(core)
    cosx, sinx, RT = host_rope_tables(core)
    m = {
        "l0a_qT": np.ascontiguousarray(inp["L0_attn_a_q_w"].T),
        "l0a_kT": np.ascontiguousarray(inp["L0_attn_a_k_w"].T),
        "l0a_vT": np.ascontiguousarray(inp["L0_attn_a_v_w"].T),
        "l0a_oT": np.ascontiguousarray(inp["L0_attn_a_o_w"].T),
        "l0a_qbc": cols(inp["L0_attn_a_q_b"]), "l0a_kbc": cols(inp["L0_attn_a_k_b"]),
        "l0a_obc": cols(inp["L0_attn_a_o_b"]),
        "l0a_bvrow": np.ascontiguousarray(inp["L0_attn_a_v_b"].reshape(1, 1024)),
        "l0a_offc": offc,
        "l0a_ropecos": cosx, "l0a_ropesin": sinx, "l0a_ropeRT": RT,
    }
    for i, (tiles, sel) in enumerate(tsets):
        m[f"l0a_ts{i}t"] = tiles
        m[f"l0a_ts{i}s"] = sel
        m[f"l0a_ts{i}tq"] = l0a_tq(tiles)
    return m


def emit_l1_attn_b(ctx, x_tiles, d, ad):
    k = ctx.k
    hp = ctx.hp
    xn = [ctx.scr(f"xn{i}", (128, TT), BF16) for i in range(DP)]
    y_sb = [ctx.scr(f"moey{i}", (128, TT)) for i in range(DP)]
    qbc = ctx.scr("at_qbc", (128, DP))
    kbc = ctx.scr("at_kbc", (128, DP))
    bvrow = ctx.scr("at_bvrow", (1, 1024), BF16)
    bvf = ctx.scr("ln_rows", (1, 2 * TT))
    qsc2 = ctx.scr("at_qsc2", (128, TT), BF16)
    vsc = ctx.scr("at_mbcf", (128, 512))[:].bitcast(BF16)
    k.g("dma_start", out=qbc[:], in_=d["qbc"].ap())
    k.g("dma_start", out=kbc[:], in_=d["kbc"].ap())
    k.g("dma_start", out=bvf[0:1, 0:1024], in_=d["bvrow"].ap())
    k.v("tensor_copy", bvrow[0:1, :], bvf[0:1, 0:1024])
    rtp = ctx.pool8k[:, 0:2048]
    rtd = ctx.pool8k[:, 2048:4096]
    dcol = ctx.scr("l1b_dcol", (128, 8))
    dsel = ctx.scr("l1b_dsel", (128, 8))
    for sub in range(4):
        k.g("dma_start", out=rtp[:, sub * 512:(sub + 1) * 512],
            in_=d["rtp"].ap()[sub * 128:(sub + 1) * 128, :])
        k.g("dma_start", out=rtd[:, sub * 512:(sub + 1) * 512],
            in_=d["rtd"].ap()[sub * 128:(sub + 1) * 128, :])
    k.g("dma_start", out=dcol[:], in_=d["dcol"].ap())
    k.g("dma_start", out=dsel[:], in_=d["dsel"].ap())
    wq = ctx.wp
    for b in range(B):
        emit_layernorm(ctx, x_tiles, xn, b * TT)
        wq.load(ctx, d["qT"].ap(), 0, D)
        for mp in range(DP):
            ps = ctx.PS[2 + mp % 2]
            emit_mm_chain(ctx, ps[:, 0:TT], wq, mp * 128, xn, slice(0, TT))
            k.v("tensor_scalar", hp[mp * 4 + b][:, 0:TT], ps[:, 0:TT],
                qbc[:, mp:mp + 1], 1.0 / 32.0, ALU.add, ALU.mult)
        wq.load(ctx, d["kT"].ap(), 0, D)
        for mp in range(DP):
            ps = ctx.PS[2 + mp % 2]
            emit_mm_chain(ctx, ps[:, 0:TT], wq, mp * 128, xn, slice(0, TT))
            k.v("tensor_scalar", qsc2[:, 0:TT], ps[:, 0:TT], kbc[:, mp:mp + 1], 1.0,
                ALU.add, ALU.mult)
            k.y("dma_start", out=ad["k_in"].ap()[b * D + mp * 128:b * D + (mp + 1) * 128, :],
                in_=qsc2[:, 0:TT])
        wq.load(ctx, d["vT"].ap(), 0, D)
        for ti in range(4):
            qsl = slice(ti * 128, (ti + 1) * 128)
            for half in range(2):
                psv = ctx.PS[2 + half]
                for kp in range(DP):
                    k.t("matmul", psv[0:128, 0:512], xn[kp][:, qsl],
                        wq.sb[kp][:, half * 512:(half + 1) * 512],
                        start=(kp == 0), stop=False)
                k.t("matmul", psv[0:128, 0:512], ctx.onesb1[:],
                    bvrow[0:1, half * 512:(half + 1) * 512], start=False, stop=True)
                k.v("tensor_copy", vsc[:, half * 512:(half + 1) * 512], psv[0:128, 0:512])
            k.y("dma_start", out=ad["v_in"].ap()[b * 512 + ti * 128:b * 512 + (ti + 1) * 128, :],
                in_=vsc)
    emit_ag(ctx, ad)
    wtile = ctx.scr("at_st", (128, 512))
    pts = [ctx.hp[40 + i][:, 0:512] for i in range(4)]
    kout_r = ad["k_out"].ap()
    vout_r = ad["v_out"].ap()
    for b in range(B):
        for kc in range(8):
            for kp in range(DP):
                k.y("dma_start", out=xn[kp][:, 0:TT],
                    in_=kout_r[kc * B * D + b * D + kp * 128:kc * B * D + b * D + (kp + 1) * 128, :])
            for sub in range(4):
                k.y("dma_start", out=hp[32 + sub * 2][:, 0:TT],
                    in_=vout_r[kc * T + b * 512 + sub * 128:kc * T + b * 512 + sub * 128 + 128, 0:512])
                k.y("dma_start", out=hp[32 + sub * 2 + 1][:, 0:TT],
                    in_=vout_r[kc * T + b * 512 + sub * 128:kc * T + b * 512 + sub * 128 + 128, 512:1024])
            for sub in range(4):
                ps = ctx.PS[2 + sub % 2]
                for kp in range(DP):
                    k.t("matmul", ps[0:128, 0:512], xn[kp][:, sub * 128:(sub + 1) * 128],
                        hp[kp * 4 + b][:, 0:TT], start=(kp == 0), stop=(kp == DP - 1))
                k.v("tensor_single_scalar", wtile[:], rtp[:, sub * 512:(sub + 1) * 512],
                    dcol[:, kc:kc + 1], ALU.mult)
                k.v("scalar_tensor_tensor", wtile[:], rtd[:, sub * 512:(sub + 1) * 512],
                    dsel[:, kc:kc + 1], wtile[:], ALU.mult, ALU.add)
                k.v("tensor_mul", pts[sub], ps[0:128, 0:512], wtile[:])
            for mp in range(DP):
                ps = ctx.PS[2 + mp % 2]
                for sub in range(4):
                    k.t("matmul", ps[0:128, 0:512],
                        hp[32 + sub * 2 + mp // 4][:, (mp % 4) * 128:(mp % 4 + 1) * 128],
                        pts[sub], start=(sub == 0), stop=(sub == 3))
                if kc == 0:
                    k.v("tensor_copy", y_sb[mp][:, 0:TT], ps[0:128, 0:512])
                else:
                    k.v("tensor_add", y_sb[mp][:, 0:TT], y_sb[mp][:, 0:TT], ps[0:128, 0:512])
        for mp in range(DP):
            k.v("tensor_add", x_tiles[mp][:, b * TT:(b + 1) * TT],
                x_tiles[mp][:, b * TT:(b + 1) * TT], y_sb[mp][:, 0:TT])


def decl_l1_attn_b(nc):
    d = {}
    for nm in ("qT", "kT", "vT"):
        d[nm] = nc.dram_tensor(f"l1b2_{nm}", [D, D], F32, kind="ExternalInput")
    d["qbc"] = nc.dram_tensor("l1b2_qbc", [128, 8], F32, kind="ExternalInput")
    d["kbc"] = nc.dram_tensor("l1b2_kbc", [128, 8], F32, kind="ExternalInput")
    d["bvrow"] = nc.dram_tensor("l1b2_bvrow", [1, 1024], F32, kind="ExternalInput")
    d["rtp"] = nc.dram_tensor("l1b2_rtp", [512, 512], F32, kind="ExternalInput")
    d["rtd"] = nc.dram_tensor("l1b2_rtd", [512, 512], F32, kind="ExternalInput")
    d["dcol"] = nc.dram_tensor("l1b2_dcol", [128, 8], F32, kind="ExternalInput")
    d["dsel"] = nc.dram_tensor("l1b2_dsel", [128, 8], F32, kind="ExternalInput")
    return d


def stage_l1_attn_b(inp, core):
    ALPHA = 0.99
    W = inp["L1_attn_b_W_qkv"]; bq = inp["L1_attn_b_b_qkv"]
    kl = np.arange(512)[:, None]; qc = np.arange(512)[None, :]
    rtp = (ALPHA ** (qc - kl)).astype(np.float32)
    rtd = np.where(qc >= kl, ALPHA ** np.maximum(qc - kl, 0), 0.0).astype(np.float32)
    def stack(m):
        return np.ascontiguousarray(m.reshape(4, 128, 512).reshape(512, 512))
    dcol = np.zeros((128, 8), np.float32)
    dsel = np.zeros((128, 8), np.float32)
    for kc in range(8):
        if kc < core:
            dcol[:, kc] = ALPHA ** (512.0 * (core - kc))
        elif kc == core:
            dsel[:, kc] = 1.0
    return {
        "l1b2_qT": np.ascontiguousarray(W[0:1024].T),
        "l1b2_kT": np.ascontiguousarray(W[1024:2048].T),
        "l1b2_vT": np.ascontiguousarray(W[2048:3072].T),
        "l1b2_qbc": cols(bq[0:1024]), "l1b2_kbc": cols(bq[1024:2048]),
        "l1b2_bvrow": np.ascontiguousarray(bq[2048:3072].reshape(1, 1024)),
        "l1b2_rtp": stack(rtp), "l1b2_rtd": stack(rtd),
        "l1b2_dcol": dcol, "l1b2_dsel": dsel,
    }


def emit_l2_attn_a(ctx, x_tiles, d, ad):
    k = ctx.k
    xn = [ctx.scr(f"xn{i}", (128, TT), BF16) for i in range(DP)]
    hp = ctx.hp
    sc11 = ctx.scr("l2a_sc11", (1, 2))
    k.g("dma_start", out=sc11[0:1, 0:1], in_=d["decay"].ap())
    k.g("dma_start", out=sc11[0:1, 1:2], in_=d["oscale"].ap())
    k.s("activation", sc11[0:1, 0:1], sc11[0:1, 0:1], AF.Sigmoid)
    k.s("activation", sc11[0:1, 0:1], sc11[0:1, 0:1], AF.Ln)
    colsc = ctx.scr("l2a_colsc", (128, 2))
    k.t("matmul", ctx.PS[4][0:128, 0:2], ctx.onesf1[:], sc11[0:1, 0:2], start=True, stop=True)
    k.v("tensor_copy", colsc[:], ctx.PS[4][0:128, 0:2])
    bt = ctx.pool8k[:, 0:2048].bitcast(F32)
    brow = ctx.scr("l2a_brow", (128, 1))
    bsT = ctx.scr("l2f_h0", (128, TT))[:].bitcast(BF16)[0:64, 0:1024]
    k.g("dma_start", out=bsT, in_=d["basisT"].ap())
    qw = {}
    for nm in ("qc", "kc", "vc", "oc"):
        cf = ctx.scr("at_pt", (128, TT), BF16)[0:64, 0:128]
        k.g("dma_start", out=cf, in_=d[nm + "T"].ap())
        for half in range(2):
            k.t("matmul", ctx.PSW[:, half * 512:(half + 1) * 512], cf,
                bsT[0:64, half * 512:(half + 1) * 512], start=True, stop=True)
        k.v("tensor_copy", bt, ctx.PSW[:, 0:1024])
        if nm != "oc":
            k.v("reduce_max", brow[:], bt, AX.X)
            k.v("tensor_scalar_mul", brow[:], brow[:], -1.0)
            k.s("activation", bt, bt, AF.Exp, bias=brow[:])
            k.v("tensor_reduce", brow[:], bt, AX.X, ALU.add)
            k.v("reciprocal", brow[:], brow[:])
            k.v("tensor_single_scalar", bt, bt, brow[:], ALU.mult)
        else:
            k.v("tensor_single_scalar", bt, bt, colsc[:, 1:2], ALU.mult)
        base = {"qc": 0, "kc": 8, "vc": 16, "oc": 24}[nm]
        wtiles = [ctx.hp[base + i][:, 0:128] for i in range(8)]
        btb = ctx.pool8k[:, 2048:3072]
        k.v("tensor_copy", btb, bt)
        psT = ctx.PS[5][:, 0:64].bitcast(BF16)
        for i in range(8):
            k.t("transpose", psT, btb[:, i * 128:(i + 1) * 128], ctx.ident_bf()[:])
            k.v("tensor_copy", wtiles[i], psT[0:128, 0:128])
        qw[nm] = wtiles
    rtp = ctx.pool8k[:, 0:2048]
    rtd = ctx.pool8k[:, 2048:4096]
    dcol = ctx.scr("l1b_dcol", (128, 8))
    dsel = ctx.scr("l1b_dsel", (128, 8))
    etile = ctx.scr("at_st", (128, 512))
    for sub in range(4):
        k.g("dma_start", out=etile[:], in_=d["ep"].ap()[sub * 128:(sub + 1) * 128, :])
        k.v("tensor_single_scalar", etile[:], etile[:], colsc[:, 0:1], ALU.mult)
        k.s("activation", rtp[:, sub * 512:(sub + 1) * 512], etile[:], AF.Exp)
        k.g("dma_start", out=etile[:], in_=d["ed"].ap()[sub * 128:(sub + 1) * 128, :])
        k.v("tensor_single_scalar", etile[:], etile[:], colsc[:, 0:1], ALU.mult)
        k.s("activation", rtd[:, sub * 512:(sub + 1) * 512], etile[:], AF.Exp)
    exq = ctx.scr("l2a_exq", (128, 8))
    k.g("dma_start", out=exq[:], in_=d["expo"].ap())
    k.v("tensor_single_scalar", exq[:], exq[:], colsc[:, 0:1], ALU.mult)
    k.s("activation", dcol[:], exq[:], AF.Exp)
    k.g("dma_start", out=dsel[:], in_=d["dsel"].ap())
    qfts = [ctx.hp[34 + i] for i in range(4)]
    ksc = ctx.scr("at_qsc2", (128, TT), BF16)
    for b in range(B):
        emit_layernorm(ctx, x_tiles, xn, b * TT)
        ps = ctx.PS[2]
        for kp in range(DP):
            k.t("matmul", ps[0:128, 0:512], qw["qc"][kp], xn[kp][:, 0:TT],
                start=(kp == 0), stop=(kp == DP - 1))
        k.v("tensor_copy", qfts[b][:, 0:TT], ps[0:128, 0:512])
        for kp in range(DP):
            k.t("matmul", ps[0:128, 0:512], qw["kc"][kp], xn[kp][:, 0:TT],
                start=(kp == 0), stop=(kp == DP - 1))
        k.v("tensor_copy", ksc[:, 0:TT], ps[0:128, 0:512])
        k.y("dma_start", out=ad["k_in"].ap()[b * 128:(b + 1) * 128, :], in_=ksc[:, 0:TT])
        for ti in range(4):
            qsl = slice(ti * 128, (ti + 1) * 128)
            psv = ctx.PS[3]
            for kp in range(DP):
                k.t("matmul", psv[0:128, 0:128], xn[kp][:, qsl], qw["vc"][kp],
                    start=(kp == 0), stop=(kp == DP - 1))
            k.v("tensor_copy", ksc[:, 0:128], psv[0:128, 0:128])
            k.y("dma_start", out=ad["v_in"].ap()[b * 512 + ti * 128:b * 512 + (ti + 1) * 128, :],
                in_=ksc[:, 0:128])
    emit_ag(ctx, ad)
    kch = ctx.hp[32][:, 0:TT]
    vch = [ctx.hp[33][:, i * 128:(i + 1) * 128] for i in range(4)]
    wtile = ctx.scr("at_st", (128, 512))
    pts = [ctx.hp[38 + i][:, 0:512] for i in range(4)]
    retr = ctx.scr("at_qsc2", (128, TT), BF16)
    obc = None
    for b in range(B):
        first = True
        for kc in range(8):
            k.y("dma_start", out=kch,
                in_=ad["k_out"].ap()[kc * B * 128 + b * 128:kc * B * 128 + (b + 1) * 128, :])
            for sub in range(4):
                k.y("dma_start", out=vch[sub],
                    in_=ad["v_out"].ap()[kc * T + b * 512 + sub * 128:kc * T + b * 512 + (sub + 1) * 128, :])
            for sub in range(4):
                ps = ctx.PS[2 + sub % 2]
                k.t("matmul", ps[0:128, 0:512], ctx.hp[32][:, sub * 128:(sub + 1) * 128],
                    qfts[b][:, 0:TT], start=True, stop=True)
                k.v("tensor_single_scalar", wtile[:], rtp[:, sub * 512:(sub + 1) * 512],
                    dcol[:, kc:kc + 1], ALU.mult)
                k.v("scalar_tensor_tensor", wtile[:], rtd[:, sub * 512:(sub + 1) * 512],
                    dsel[:, kc:kc + 1], wtile[:], ALU.mult, ALU.add)
                k.v("tensor_mul", pts[sub], ps[0:128, 0:512], wtile[:])
            for sub in range(4):
                k.t("matmul", ctx.PS[4][0:128, 0:512], vch[sub], pts[sub],
                    start=first, stop=(kc == 7 and sub == 3))
                first = False
        k.v("tensor_copy", retr[:, 0:TT], ctx.PS[4][0:128, 0:512])
        for mp in range(DP):
            ps = ctx.PS[2 + mp % 2]
            k.t("matmul", ps[0:128, 0:512], qw["oc"][mp], retr[:, 0:TT],
                start=True, stop=True)
            k.v("tensor_add", x_tiles[mp][:, b * TT:(b + 1) * TT],
                x_tiles[mp][:, b * TT:(b + 1) * TT], ps[0:128, 0:512])


def decl_l2_attn_a(nc):
    d = {}
    d["basisT"] = nc.dram_tensor("l2a_basisT", [64, 1024], F32, kind="ExternalInput")
    for nm in ("qcT", "kcT", "vcT", "ocT"):
        d[nm[:-1] + "T"] = nc.dram_tensor(f"l2a_{nm}", [64, 128], F32, kind="ExternalInput")
    d["decay"] = nc.dram_tensor("l2a_decay", [1, 1], F32, kind="ExternalInput")
    d["oscale"] = nc.dram_tensor("l2a_oscale", [1, 1], F32, kind="ExternalInput")
    d["ep"] = nc.dram_tensor("l2a_ep", [512, 512], F32, kind="ExternalInput")
    d["ed"] = nc.dram_tensor("l2a_ed", [512, 512], F32, kind="ExternalInput")
    d["expo"] = nc.dram_tensor("l2a_expo", [128, 8], F32, kind="ExternalInput")
    d["dsel"] = nc.dram_tensor("l2a_dsel", [128, 8], F32, kind="ExternalInput")
    return d


def stage_l2_attn_a(inp, core):
    kl = np.arange(512)[:, None]; qc = np.arange(512)[None, :]
    ep = (qc - kl - 1.0 + 512.0).astype(np.float32)
    ed = np.where(qc > kl, qc - kl - 1.0, 1e30).astype(np.float32)
    def stack(m):
        return np.ascontiguousarray(m.reshape(4, 128, 512).reshape(512, 512))
    expo = np.full((128, 8), 1e30, np.float32)
    dsel = np.zeros((128, 8), np.float32)
    for kc in range(8):
        if kc < core:
            expo[:, kc] = 512.0 * (core - kc) - 512.0
        elif kc == core:
            dsel[:, kc] = 1.0
    return {
        "l2a_basisT": np.ascontiguousarray(inp["L2_attn_a_basis"].T),
        "l2a_qcT": np.ascontiguousarray(inp["L2_attn_a_q_coeffs"].T),
        "l2a_kcT": np.ascontiguousarray(inp["L2_attn_a_k_coeffs"].T),
        "l2a_vcT": np.ascontiguousarray(inp["L2_attn_a_v_coeffs"].T),
        "l2a_ocT": np.ascontiguousarray(inp["L2_attn_a_o_coeffs"].T),
        "l2a_decay": inp["L2_attn_a_decay"].reshape(1, 1).astype(np.float32),
        "l2a_oscale": inp["L2_attn_a_out_scale"].reshape(1, 1).astype(np.float32),
        "l2a_ep": stack(ep), "l2a_ed": stack(ed),
        "l2a_expo": expo, "l2a_dsel": dsel,
    }


def emit_l0_attn_b(ctx, x_tiles, d):
    k = ctx.k
    hp = ctx.hp
    xn = [ctx.scr(f"xn{i}", (128, TT), BF16) for i in range(DP)]
    qbc = ctx.scr("at_qbc", (128, DP))
    kbc = ctx.scr("at_kbc", (128, DP))
    vbc = ctx.scr("l0b_vbc", (128, DP))
    obc = ctx.scr("at_obc", (128, DP))
    bfc = ctx.scr("l0b_bfc", (16, 1))
    k.g("dma_start", out=qbc[:], in_=d["qbc"].ap())
    k.g("dma_start", out=kbc[:], in_=d["kbc"].ap())
    k.g("dma_start", out=vbc[:], in_=d["vbc"].ap())
    k.g("dma_start", out=obc[:], in_=d["obc"].ap())
    k.g("dma_start", out=bfc[0:16, :], in_=d["bfc"].ap())
    selc = ctx.scr("l0b_sel", (16, 8 * 128), BF16)
    bones_sb = [ctx.scr(f"l0b_bon{i}", (128, 16), BF16) for i in range(DP)]
    for dp in range(DP):
        k.g("dma_start", out=selc[0:16, dp * 128:(dp + 1) * 128], in_=d["sel"].ap()[dp * 16:(dp + 1) * 16, :])
        k.g("dma_start", out=bones_sb[dp][:], in_=d["bonesd"].ap()[dp * 128:(dp + 1) * 128, :])
    hval = ctx.scr("ln_sq", (128, TT))
    wq = ctx.wp
    f_sb = ctx.scr("l0b_f", (16, T), BF16)
    def phi_evac(dst_ap, ps, bcol):
        k.v("tensor_scalar_add", hval[:], ps, bcol)
        k.v("tensor_scalar_min", ctx.scr("ln_sub", (128, TT))[:], hval[:], 0.0)
        k.s("activation", ctx.scr("l0_hval", (128, TT))[:], ctx.scr("ln_sub", (128, TT))[:], AF.Exp)
        k.v("scalar_tensor_tensor", dst_ap, hval[:], 0.0, ctx.scr("l0_hval", (128, TT))[:],
            ALU.max, ALU.add)
    ksc = ctx.scr("at_qsc2", (128, TT), BF16)
    for b in range(B):
        emit_layernorm(ctx, x_tiles, xn, b * TT)
        wq.load(ctx, d["qT"].ap(), 0, D)
        for mp in range(DP):
            ps = ctx.PS[2 + mp % 2]
            emit_mm_chain(ctx, ps[:, 0:TT], wq, mp * 128, xn, slice(0, TT))
            phi_evac(hp[mp * 4 + b][:, 0:TT], ps[:, 0:TT], qbc[:, mp:mp + 1])
        wq.load(ctx, d["kT"].ap(), 0, D)
        for mp in range(DP):
            ps = ctx.PS[2 + mp % 2]
            emit_mm_chain(ctx, ps[:, 0:TT], wq, mp * 128, xn, slice(0, TT))
            phi_evac(ksc[:, 0:TT], ps[:, 0:TT], kbc[:, mp:mp + 1])
            k.y("dma_start", out=d["kdram"].ap()[mp * 128:(mp + 1) * 128, b * TT:(b + 1) * TT],
                in_=ksc[:, 0:TT])
        wq.load(ctx, d["vT"].ap(), 0, D)
        for mp in range(DP):
            ps = ctx.PS[2 + mp % 2]
            emit_mm_chain(ctx, ps[:, 0:TT], wq, mp * 128, xn, slice(0, TT))
            k.v("tensor_scalar_add", ksc[:, 0:TT], ps[:, 0:TT], vbc[:, mp:mp + 1])
            k.y("dma_start", out=d["vdram"].ap()[mp * 128:(mp + 1) * 128, b * TT:(b + 1) * TT],
                in_=ksc[:, 0:TT])
        wq.load(ctx, d["fT"].ap(), 0, 16)
        ps = ctx.PS[4]
        for kp in range(DP):
            k.t("matmul", ps[0:16, 0:TT], wq.sb[kp][:, 0:16], xn[kp][:, 0:TT],
                start=(kp == 0), stop=(kp == DP - 1))
        k.s("activation", f_sb[0:16, b * TT:(b + 1) * TT], ps[0:16, 0:TT], AF.Sigmoid,
            bias=bfc[0:16, :])
    y_sb = [ctx.scr(f"moey{i}", (128, TT)) for i in range(DP)]
    den = ctx.scr("moehb", (128, TT))[0:16, :]
    wts = ctx.scr("moeha", (128, TT))[0:16, :]
    aw = ctx.scr("at_pt", (128, TT), BF16)[0:16, :]
    prod = ctx.scr("ln_sub", (128, TT))
    pk = xn
    vv = [hp[32 + i] for i in range(DP)]
    oall = xn
    for t in range(B):
        for mp in range(DP):
            k.g("memset", y_sb[mp][:, 0:TT], 0.0)
        k.g("memset", den[:, 0:TT], 0.0)
        for s in range(t, -1, -1):
            for mp in range(DP):
                k.y("dma_start", out=pk[mp][:, 0:TT],
                    in_=d["kdram"].ap()[mp * 128:(mp + 1) * 128, s * TT:(s + 1) * TT])
                k.y("dma_start", out=vv[mp][:, 0:TT],
                    in_=d["vdram"].ap()[mp * 128:(mp + 1) * 128, s * TT:(s + 1) * TT])
            prodb = ctx.scr("l0b_prodb", (128, TT), BF16)
            for mp in range(DP):
                k.v("tensor_mul", prodb[:, 0:TT], hp[mp * 4 + t][:, 0:TT], pk[mp][:, 0:TT])
                k.t("matmul", ctx.PS[4][0:16, 0:TT], bones_sb[mp][:], prodb[:, 0:TT],
                    start=(mp == 0), stop=(mp == DP - 1))
            if s == t:
                k.g("memset", wts[:, 0:TT], 1.0)
            else:
                k.v("tensor_mul", wts[:, 0:TT], wts[:, 0:TT],
                    f_sb[0:16, (s + 1) * TT:(s + 2) * TT])
            k.v("tensor_mul", aw[:, 0:TT], ctx.PS[4][0:16, 0:TT], wts[:, 0:TT])
            k.v("tensor_add", den[:, 0:TT], den[:, 0:TT], aw[:, 0:TT])
            for mp in range(DP):
                k.t("matmul", ctx.PS[5][0:128, 0:TT], selc[0:16, mp * 128:(mp + 1) * 128],
                    aw[:, 0:TT], start=True, stop=True)
                k.v("tensor_mul", prod[:, 0:TT], ctx.PS[5][0:128, 0:TT], vv[mp][:, 0:TT])
                k.v("tensor_add", y_sb[mp][:, 0:TT], y_sb[mp][:, 0:TT], prod[:, 0:TT])
        k.v("tensor_scalar_add", den[:, 0:TT], den[:, 0:TT], 1e-6)
        k.v("reciprocal", den[:, 0:TT], den[:, 0:TT])
        k.v("tensor_copy", aw[:, 0:TT], den[:, 0:TT])
        for mp in range(DP):
            k.t("matmul", ctx.PS[5][0:128, 0:TT], selc[0:16, mp * 128:(mp + 1) * 128],
                aw[0:16, 0:TT], start=True, stop=True)
            k.v("tensor_mul", oall[mp][:, 0:TT], ctx.PS[5][0:128, 0:TT], y_sb[mp][:, 0:TT])
        wq.load(ctx, d["oT"].ap(), 0, D)
        for mp in range(DP):
            ps = ctx.PS[2 + mp % 2]
            emit_mm_chain(ctx, ps[:, 0:TT], wq, mp * 128, oall, slice(0, TT))
            k.v("tensor_scalar_add", hval[:], ps[:, 0:TT], obc[:, mp:mp + 1])
            k.v("tensor_add", x_tiles[mp][:, t * TT:(t + 1) * TT],
                x_tiles[mp][:, t * TT:(t + 1) * TT], hval[:])


def decl_l0_attn_b(nc):
    d = {}
    for nm in ("qT", "kT", "vT", "oT"):
        d[nm] = nc.dram_tensor(f"l0b_{nm}", [D, D], F32, kind="ExternalInput")
    d["fT"] = nc.dram_tensor("l0b_fT", [D, 16], F32, kind="ExternalInput")
    for nm in ("qbc", "kbc", "vbc", "obc"):
        d[nm] = nc.dram_tensor(f"l0b_{nm}", [128, 8], F32, kind="ExternalInput")
    d["bfc"] = nc.dram_tensor("l0b_bfc", [16, 1], F32, kind="ExternalInput")
    d["sel"] = nc.dram_tensor("l0b_sel", [8 * 16, 128], F32, kind="ExternalInput")
    d["bonesd"] = nc.dram_tensor("l0b_bones", [8 * 128, 16], F32, kind="ExternalInput")
    d["kdram"] = nc.dram_tensor("l0b_kdram", [D, T], BF16, kind="Internal")
    d["vdram"] = nc.dram_tensor("l0b_vdram", [D, T], BF16, kind="Internal")
    return d


def stage_l0_attn_b(inp, core):
    sel = np.zeros((8 * 16, 128), np.float32)
    bones = np.zeros((8 * 128, 16), np.float32)
    for dp in range(8):
        for m in range(128):
            h = 2 * dp + m // 64
            sel[dp * 16 + h, m] = 1.0
        for kk in range(128):
            bones[dp * 128 + kk, 2 * dp + kk // 64] = 1.0
    return {
        "l0b_qT": np.ascontiguousarray(inp["L0_attn_b_Wq"].T),
        "l0b_kT": np.ascontiguousarray(inp["L0_attn_b_Wk"].T),
        "l0b_vT": np.ascontiguousarray(inp["L0_attn_b_Wv"].T),
        "l0b_oT": np.ascontiguousarray(inp["L0_attn_b_Wo"].T),
        "l0b_fT": np.ascontiguousarray(inp["L0_attn_b_Wf"].T),
        "l0b_qbc": cols(inp["L0_attn_b_bq"]), "l0b_kbc": cols(inp["L0_attn_b_bk"]),
        "l0b_vbc": cols(inp["L0_attn_b_bv"]), "l0b_obc": cols(inp["L0_attn_b_bo"]),
        "l0b_bfc": inp["L0_attn_b_bf"].reshape(16, 1).astype(np.float32),
        "l0b_sel": sel, "l0b_bones": bones,
    }


def emit_l3_attn_b(ctx, x_tiles, d):
    k = ctx.k
    hp = ctx.hp
    xn = [ctx.scr(f"xn{i}", (128, TT), BF16) for i in range(DP)]
    CT = 16
    NS = 16
    wq = ctx.wp
    convwc = ctx.scr("mb_convw", (128, 64))
    convbc = ctx.scr("mb_convb", (128, CT))
    bdtc = ctx.scr("mb_bdt", (128, CT))
    dcp = ctx.scr("mb_dc", (128, CT))
    alog = ctx.scr("ln_sq", (128, TT))[:, 0:256]
    A_sb = ctx.scr("mb_A", (128, CT * NS))
    wdtb = ctx.scr("l0b_f", (16, T), BF16)[0:1, 0:2048]
    hmask = ctx.scr("mb_hmask", (128, 8))
    cmask = ctx.scr("mb_cmask", (128, 8))
    k.g("dma_start", out=convwc[:], in_=d["convwc"].ap())
    k.g("dma_start", out=convbc[:], in_=d["convbc"].ap())
    k.g("dma_start", out=bdtc[:], in_=d["bdtc"].ap())
    k.g("dma_start", out=dcp[:], in_=d["dcolc"].ap())
    k.g("dma_start", out=alog, in_=d["alogc"].ap())
    k.g("dma_start", out=hmask[:], in_=d["hmask"].ap())
    k.g("dma_start", out=cmask[:], in_=d["cmask"].ap())
    row2048 = ctx.scr("ln_rows", (1, 2 * TT))
    k.g("dma_start", out=row2048[0:1, 0:1024], in_=d["wdt"].ap()[0:1, 0:1024])
    k.v("tensor_copy", wdtb[0:1, 0:1024], row2048[0:1, 0:1024])
    k.g("dma_start", out=row2048[0:1, 0:1024], in_=d["wdt"].ap()[0:1, 1024:2048])
    k.v("tensor_copy", wdtb[0:1, 1024:2048], row2048[0:1, 0:1024])
    k.s("activation", A_sb[:], alog, AF.Exp)
    k.v("tensor_scalar_mul", A_sb[:], A_sb[:], -1.0)
    xdbl = ctx.scr("at_st", (128, 512))[0:33, :]
    dtraw = ctx.scr("mb_dtraw", (1, TT), BF16)
    sp_a = ctx.scr("ln_sq", (128, TT))
    sp_b = ctx.scr("ln_sub", (128, TT))
    xcpad = ctx.pool8k[:, 1024:1540]
    hblk = ctx.scr("mb_hblk", (128, 24), BF16)
    gt = ctx.pool8k[:, 0:512]
    dtt = ctx.pool8k[:, 512:1024]
    xct = ctx.pool8k[:, 2048:2560]
    dat = ctx.scr("l0_hval", (128, TT))
    ut = ctx.scr("l0_arg", (128, TT))
    ht = ctx.scr("moeha", (128, TT))
    hend = ctx.scr("mb_hend", (128, 2 * NS))
    hpeall = ctx.scr("ln_sub", (128, TT))[:, 0:256]
    hin = ctx.scr("mb_hin", (128, NS))[:]
    Sst = ctx.scr("mb_S", (128, NS))[:]
    yac = ctx.scr("moehb", (128, TT))
    brow = ctx.scr("mb_brow", (1, TT))
    for b in range(B):
        emit_layernorm(ctx, x_tiles, xn, b * TT)
        for c in range(4):
            wq.load(ctx, d["winT"].ap(), c * 1024, 1024)
            for ml in range(8):
                mp = c * 8 + ml
                ps = ctx.PS[2 + mp % 2]
                emit_mm_chain(ctx, ps[:, 0:TT], wq, ml * 128, xn, slice(0, TT))
                k.v("tensor_copy", xct[:, 0:TT], ps[:, 0:TT])
                tgt = d["xzpre"] if mp < 16 else d["zdram"]
                r0 = (mp % 16) * 128
                k.y("dma_start", out=tgt.ap()[r0:r0 + 128, b * TT:(b + 1) * TT],
                    in_=xct[:, 0:TT])
    for b in range(B):
        k.y("dma_start", out=d["halo_in"].ap()[:, b * 3:(b + 1) * 3],
            in_=d["xzpre"].ap()[:, b * TT + 509:b * TT + 512])
    k.g("collective_compute", "AllGather", ALU.bypass,
        replica_groups=[list(range(NCORES))],
        ins=[d["halo_in"].ap().opt()], outs=[d["halo_out"].ap().opt()])
    hout_r = d["halo_out"].ap().rearrange("(j z p) c -> z p j c", j=NCORES, z=CT)
    for b in range(B):
        for ct in range(CT):
            k.g("memset", xcpad[:, 0:4], 0.0)
            k.y("dma_start", out=hblk[:].rearrange("p (j c) -> p j c", j=8),
                in_=hout_r[ct][:, :, b * 3:(b + 1) * 3])
            for j in range(8):
                k.v("scalar_tensor_tensor", xcpad[:, 0:3], hblk[:, j * 3:(j + 1) * 3],
                    hmask[:, j:j + 1], xcpad[:, 0:3], ALU.mult, ALU.add)
            k.y("dma_start", out=xcpad[:, 3:515],
                in_=d["xzpre"].ap()[ct * 128:(ct + 1) * 128, b * TT:(b + 1) * TT])
            k.v("tensor_single_scalar", sp_a[:, 0:TT], xcpad[:, 0:512], convwc[:, ct * 4:ct * 4 + 1], ALU.mult)
            for tap in range(1, 4):
                k.v("scalar_tensor_tensor", sp_a[:, 0:TT], xcpad[:, tap:tap + 512],
                    convwc[:, ct * 4 + tap:ct * 4 + tap + 1], sp_a[:, 0:TT], ALU.mult, ALU.add)
            k.s("activation", xct[:, 0:TT], sp_a[:, 0:TT], AF.Silu, bias=convbc[:, ct:ct + 1])
            k.y("dma_start", out=d["xcp"].ap()[ct * 128:(ct + 1) * 128, b * TT:(b + 1) * TT],
                in_=xct[:, 0:TT])
        for g2 in range(2):
            wq.load(ctx, d["wxT"].ap(), 0, 33, r0=g2 * 1024)
            for i in range(8):
                k.y("dma_start", out=hp[i][:, 0:TT],
                    in_=d["xcp"].ap()[g2 * 1024 + i * 128:g2 * 1024 + (i + 1) * 128,
                                      b * TT:(b + 1) * TT])
            for i in range(8):
                k.t("matmul", ctx.PS[4][0:33, 0:TT], wq.sb[i][:, 0:33], hp[i][:, 0:TT],
                    start=(g2 == 0 and i == 0), stop=(g2 == 1 and i == 7))
        k.v("tensor_copy", xdbl[0:33, 0:TT], ctx.PS[4][0:33, 0:TT])
        k.y("dma_start", out=d["bcdram"].ap()[:, b * TT:(b + 1) * TT], in_=xdbl[1:33, 0:TT])
        k.v("tensor_copy", dtraw[0:1, 0:TT], xdbl[0:1, 0:TT])
        for ct in range(CT):
            k.t("matmul", ctx.PS[4][0:128, 0:TT], wdtb[0:1, ct * 128:(ct + 1) * 128],
                dtraw[0:1, 0:TT], start=True, stop=True)
            k.v("tensor_scalar_add", sp_a[:, 0:TT], ctx.PS[4][0:128, 0:TT], bdtc[:, ct:ct + 1])
            k.s("activation", sp_b[:, 0:TT], sp_a[:, 0:TT], AF.Abs)
            k.s("activation", sp_b[:, 0:TT], sp_b[:, 0:TT], AF.Exp, scale=-1.0)
            k.s("activation", sp_b[:, 0:TT], sp_b[:, 0:TT], AF.Ln, bias=ctx.k.fconst(1.0))
            k.v("tensor_scalar_max", sp_a[:, 0:TT], sp_a[:, 0:TT], 0.0)
            k.v("tensor_add", sp_a[:, 0:TT], sp_a[:, 0:TT], sp_b[:, 0:TT])
            k.g("dma_start", out=d["dtdram"].ap()[ct * 128:(ct + 1) * 128, b * TT:(b + 1) * TT],
                in_=sp_a[:, 0:TT])
    emit_l3_attn_b_tail(ctx, x_tiles, d)


def emit_l3_attn_b_tail(ctx, x_tiles, d):
    k = ctx.k
    hp = ctx.hp
    CT = 16
    NS = 16
    wq = ctx.wp
    A_sb = ctx.scr("mb_A", (128, CT * NS))
    dcp = ctx.scr("mb_dc", (128, CT))
    cmask = ctx.scr("mb_cmask", (128, 8))
    sp_a = ctx.scr("ln_sq", (128, TT))
    gt = ctx.pool8k[:, 0:512]
    dtt = ctx.pool8k[:, 512:1024]
    xct = ctx.pool8k[:, 2048:2560]
    dat = ctx.scr("l0_hval", (128, TT))
    ut = ctx.scr("l0_arg", (128, TT))
    ht = ctx.scr("moeha", (128, TT))
    hend = ctx.scr("mb_hend", (128, 2 * NS))
    pe = ctx.scr("mb_pe", (128, NS))
    hpeall = ctx.scr("ln_sub", (128, TT))[:, 0:256]
    hin = ctx.scr("mb_hin", (128, NS))[:]
    Sst = ctx.scr("mb_S", (128, NS))[:]
    yac = ctx.scr("moehb", (128, TT))
    brow = ctx.scr("mb_brow", (1, TT))
    for b in range(B):
        for ct in range(CT):
            k.y("dma_start", out=dtt[:, 0:TT],
                in_=d["dtdram"].ap()[ct * 128:(ct + 1) * 128, b * TT:(b + 1) * TT])
            k.y("dma_start", out=xct[:, 0:TT],
                in_=d["xcp"].ap()[ct * 128:(ct + 1) * 128, b * TT:(b + 1) * TT])
            k.v("tensor_mul", gt[:, 0:TT], dtt[:, 0:TT], xct[:, 0:TT])
            k.v("tensor_reduce", hend[:, 0:1], dtt[:, 0:TT], AX.X, ALU.add)
            k.v("tensor_single_scalar", pe[:], A_sb[:, ct * NS:(ct + 1) * NS],
                hend[:, 0:1], ALU.mult)
            k.s("activation", pe[:], pe[:], AF.Exp)
            for n in range(NS):
                k.y("dma_start", out=brow[0:1, 0:TT],
                    in_=d["bcdram"].ap()[n:n + 1, b * TT:(b + 1) * TT])
                k.t("matmul", ctx.PS[4][0:128, 0:TT], ctx.onesf1[:], brow[0:1, 0:TT],
                    start=True, stop=True)
                k.v("tensor_single_scalar", dat[:, 0:TT], dtt[:, 0:TT],
                    A_sb[:, ct * NS + n:ct * NS + n + 1], ALU.mult)
                k.s("activation", dat[:, 0:TT], dat[:, 0:TT], AF.Exp)
                k.v("tensor_mul", ut[:, 0:TT], gt[:, 0:TT], ctx.PS[4][0:128, 0:TT])
                k.v("tensor_tensor_scan", ht[:, 0:TT], dat[:, 0:TT], ut[:, 0:TT], 0.0,
                    ALU.mult, ALU.add)
                k.v("tensor_copy", hend[:, NS + n:NS + n + 1], ht[:, TT - 1:TT])
            k.v("tensor_copy", hend[:, 0:NS], pe[:])
            k.y("dma_start", out=d["hpe_in"].ap()[b * 2048 + ct * 128:b * 2048 + (ct + 1) * 128, :],
                in_=hend[:, 0:2 * NS])
    k.g("collective_compute", "AllGather", ALU.bypass,
        replica_groups=[list(range(NCORES))],
        ins=[d["hpe_in"].ap().opt()], outs=[d["hpe_out"].ap().opt()])
    hpe_r = d["hpe_out"].ap().rearrange("(j z p) c -> z p j c", j=NCORES, z=B * CT)
    ydram = d["ydram"]
    for b in range(B):
        for n in range(NS):
            k.y("dma_start", out=brow[0:1, 0:TT],
                in_=d["bcdram"].ap()[n:n + 1, b * TT:(b + 1) * TT])
            k.t("matmul", ctx.PS[4][0:128, 0:TT], ctx.onesf1[:], brow[0:1, 0:TT],
                start=True, stop=True)
            k.v("tensor_copy", hp[n][:, 0:TT], ctx.PS[4][0:128, 0:TT])
            k.y("dma_start", out=brow[0:1, 0:TT],
                in_=d["bcdram"].ap()[NS + n:NS + n + 1, b * TT:(b + 1) * TT])
            k.t("matmul", ctx.PS[4][0:128, 0:TT], ctx.onesf1[:], brow[0:1, 0:TT],
                start=True, stop=True)
            k.v("tensor_copy", hp[NS + n][:, 0:TT], ctx.PS[4][0:128, 0:TT])
        for ct in range(CT):
            k.y("dma_start", out=hpeall.rearrange("p (j c) -> p j c", j=8),
                in_=hpe_r[b * CT + ct])
            k.g("memset", hin, 0.0)
            k.g("memset", Sst, 0.0)
            for j in range(7):
                k.v("tensor_mul", Sst, Sst, hpeall[:, j * 32:j * 32 + 16])
                k.v("tensor_add", Sst, Sst, hpeall[:, j * 32 + 16:j * 32 + 32])
                k.v("scalar_tensor_tensor", hin, Sst, cmask[:, j:j + 1], hin,
                    ALU.mult, ALU.add)
            k.y("dma_start", out=dtt[:, 0:TT],
                in_=d["dtdram"].ap()[ct * 128:(ct + 1) * 128, b * TT:(b + 1) * TT])
            k.y("dma_start", out=xct[:, 0:TT],
                in_=d["xcp"].ap()[ct * 128:(ct + 1) * 128, b * TT:(b + 1) * TT])
            k.v("tensor_mul", gt[:, 0:TT], dtt[:, 0:TT], xct[:, 0:TT])
            for n in range(NS):
                k.v("tensor_single_scalar", dat[:, 0:TT], dtt[:, 0:TT],
                    A_sb[:, ct * NS + n:ct * NS + n + 1], ALU.mult)
                k.s("activation", dat[:, 0:TT], dat[:, 0:TT], AF.Exp)
                k.v("tensor_mul", ut[:, 0:TT], gt[:, 0:TT], hp[n][:, 0:TT])
                k.v("tensor_tensor_scan", ht[:, 0:TT], dat[:, 0:TT], ut[:, 0:TT],
                    hin[:, n:n + 1], ALU.mult, ALU.add)
                if n == 0:
                    k.v("tensor_mul", yac[:, 0:TT], ht[:, 0:TT], hp[NS][:, 0:TT])
                else:
                    k.v("tensor_mul", ut[:, 0:TT], ht[:, 0:TT], hp[NS + n][:, 0:TT])
                    k.v("tensor_add", yac[:, 0:TT], yac[:, 0:TT], ut[:, 0:TT])
            k.v("scalar_tensor_tensor", yac[:, 0:TT], xct[:, 0:TT], dcp[:, ct:ct + 1],
                yac[:, 0:TT], ALU.mult, ALU.add)
            k.y("dma_start", out=dtt[:, 0:TT],
                in_=d["zdram"].ap()[ct * 128:(ct + 1) * 128, b * TT:(b + 1) * TT])
            k.s("activation", sp_a[:, 0:TT], dtt[:, 0:TT], AF.Silu)
            k.v("tensor_mul", xct[:, 0:TT], yac[:, 0:TT], sp_a[:, 0:TT])
            k.y("dma_start", out=ydram.ap()[ct * 128:(ct + 1) * 128, b * TT:(b + 1) * TT],
                in_=xct[:, 0:TT])
        y2 = [ctx.scr(f"moey{i}", (128, TT)) for i in range(DP)]
        for g2 in range(2):
            wq.load(ctx, d["woutT"].ap(), 0, D, r0=g2 * 1024)
            for i in range(8):
                k.y("dma_start", out=hp[i][:, 0:TT],
                    in_=ydram.ap()[g2 * 1024 + i * 128:g2 * 1024 + (i + 1) * 128,
                                   b * TT:(b + 1) * TT])
            for mp in range(DP):
                ps = ctx.PS[2 + mp % 2]
                emit_mm_chain(ctx, ps[:, 0:TT], wq, mp * 128, hp[0:8], slice(0, TT))
                if g2 == 0:
                    k.v("tensor_copy", y2[mp][:, 0:TT], ps[:, 0:TT])
                else:
                    k.v("tensor_add", y2[mp][:, 0:TT], y2[mp][:, 0:TT], ps[:, 0:TT])
        for mp in range(DP):
            k.v("tensor_add", x_tiles[mp][:, b * TT:(b + 1) * TT],
                x_tiles[mp][:, b * TT:(b + 1) * TT], y2[mp][:, 0:TT])


def decl_l3_attn_b(nc):
    d = {}
    d["winT"] = nc.dram_tensor("mb_winT", [D, 4096], F32, kind="ExternalInput")
    d["convwc"] = nc.dram_tensor("mb_convwc", [128, 64], F32, kind="ExternalInput")
    d["convbc"] = nc.dram_tensor("mb_convbc", [128, 16], F32, kind="ExternalInput")
    d["wxT"] = nc.dram_tensor("mb_wxT", [2048, 33], F32, kind="ExternalInput")
    d["wdt"] = nc.dram_tensor("mb_wdt", [1, 2048], F32, kind="ExternalInput")
    d["bdtc"] = nc.dram_tensor("mb_bdtc", [128, 16], F32, kind="ExternalInput")
    d["alogc"] = nc.dram_tensor("mb_alogc", [128, 256], F32, kind="ExternalInput")
    d["dcolc"] = nc.dram_tensor("mb_dcolc", [128, 16], F32, kind="ExternalInput")
    d["woutT"] = nc.dram_tensor("mb_woutT", [2048, D], F32, kind="ExternalInput")
    d["hmask"] = nc.dram_tensor("mb_hmask", [128, 8], F32, kind="ExternalInput")
    d["cmask"] = nc.dram_tensor("mb_cmask", [128, 8], F32, kind="ExternalInput")
    d["xzpre"] = nc.dram_tensor("mb_xzpre", [2048, T], BF16, kind="Internal")
    d["zdram"] = nc.dram_tensor("mb_zdram", [2048, T], BF16, kind="Internal")
    d["xcp"] = nc.dram_tensor("mb_xcp", [2048, T], BF16, kind="Internal")
    d["dtdram"] = nc.dram_tensor("mb_dtdram", [2048, T], BF16, kind="Internal")
    d["bcdram"] = nc.dram_tensor("mb_bcdram", [32, T], F32, kind="Internal")
    d["ydram"] = nc.dram_tensor("mb_ydram", [2048, T], BF16, kind="Internal")
    d["halo_in"] = nc.dram_tensor("mb_halo_in", [2048, 12], BF16, kind="Internal")
    d["halo_out"] = nc.dram_tensor("mb_halo_out", [NCORES * 2048, 12], BF16, kind="Internal")
    d["hpe_in"] = nc.dram_tensor("mb_hpe_in", [8192, 32], F32, kind="Internal")
    d["hpe_out"] = nc.dram_tensor("mb_hpe_out", [NCORES * 8192, 32], F32, kind="Internal")
    return d


def stage_l3_attn_b(inp, core):
    convw = inp["L3_attn_b_conv_w"].reshape(2048, 4)
    convwc = np.zeros((128, 64), np.float32)
    for ct in range(16):
        for tap in range(4):
            convwc[:, ct * 4 + tap] = convw[ct * 128:(ct + 1) * 128, tap]
    alogc = np.zeros((128, 256), np.float32)
    al = inp["L3_attn_b_A_log"]
    for ct in range(16):
        alogc[:, ct * 16:(ct + 1) * 16] = al[ct * 128:(ct + 1) * 128, :]
    hmask = np.zeros((128, 8), np.float32)
    cmask = np.zeros((128, 8), np.float32)
    if core >= 1:
        hmask[:, core - 1] = 1.0
        cmask[:, core - 1] = 1.0
    return {
        "mb_winT": inp["L3_attn_b_W_in"],
        "mb_convwc": convwc,
        "mb_convbc": cols(inp["L3_attn_b_conv_b"]),
        "mb_wxT": inp["L3_attn_b_W_x"],
        "mb_wdt": inp["L3_attn_b_W_dt"].reshape(1, 2048),
        "mb_bdtc": cols(inp["L3_attn_b_b_dt"]),
        "mb_alogc": alogc,
        "mb_dcolc": cols(inp["L3_attn_b_D_param"]),
        "mb_woutT": inp["L3_attn_b_W_out"],
        "mb_hmask": hmask, "mb_cmask": cmask,
    }


FP4_MIDS = [0.25, 0.75, 1.25, 1.75, 2.5, 3.5, 5.0]
FP4_WTS = [0.5, 0.5, 0.5, 0.5, 1.0, 1.0, 2.0]


def emit_fp4_quant_free(ctx, w_ap, absw, esc_full, out_ap, cw=512):
    k = ctx.k
    nb = cw // 16
    scl = ctx.scr("fp4_scl", (128, 32))
    g = ctx.scr("moeha", (128, TT))
    acc = ctx.scr("moehb", (128, TT))
    sgn = ctx.scr("ln_sub", (128, TT))
    k.s("activation", absw[:, 0:cw], w_ap, AF.Abs)
    k.v("tensor_reduce", scl[:, 0:nb], absw[:, 0:cw].rearrange("p (b s) -> p b s", b=nb),
        AX.X, ALU.max)
    k.v("tensor_scalar_mul", scl[:, 0:nb], scl[:, 0:nb], 1.0 / 6.0)
    k.v("tensor_scalar_max", scl[:, 0:nb], scl[:, 0:nb], 1e-12)
    k.v("tensor_copy", esc_full[:, 0:cw].rearrange("p (b s) -> p b s", b=nb),
        scl[:, 0:nb].unsqueeze(2).broadcast_to((128, nb, 16)))
    k.s("sign", sgn[:, 0:cw], w_ap)
    for i in range(7):
        k.v("scalar_tensor_tensor", g[:, 0:cw], esc_full[:, 0:cw], FP4_MIDS[i],
            absw[:, 0:cw], ALU.mult, ALU.is_lt)
        if i == 0:
            k.v("tensor_scalar_mul", acc[:, 0:cw], g[:, 0:cw], FP4_WTS[0])
        else:
            k.v("scalar_tensor_tensor", acc[:, 0:cw], g[:, 0:cw], FP4_WTS[i],
                acc[:, 0:cw], ALU.mult, ALU.add)
    k.v("tensor_mul", sgn[:, 0:cw], sgn[:, 0:cw], esc_full[:, 0:cw])
    k.v("tensor_mul", out_ap, acc[:, 0:cw], sgn[:, 0:cw])


def emit_l2_ffn_prep(ctx, d):
    k = ctx.k
    wt = ctx.scr("at_st", (128, 512))
    absw = ctx.scr("l0_hval", (128, TT))
    esc = ctx.scr("l0_arg", (128, TT))
    qb = ctx.scr("at_pt", (128, TT), BF16)
    for (src, dst, nrows) in (("wg", "wgq", 2816), ("wu", "wuq", 2816), ("wd", "wdq", 1024)):
        ncols = 1024 if nrows == 2816 else 2816
        for r0 in range(0, nrows, 128):
            for c0 in range(0, ncols, 512):
                cw = min(512, ncols - c0)
                k.y("dma_start", out=wt[:, 0:cw], in_=d[src].ap()[r0:r0 + 128, c0:c0 + cw])
                emit_fp4_quant_free(ctx, wt[:, 0:cw], absw, esc, qb[:, 0:cw], cw)
                k.y("dma_start", out=d[dst].ap()[r0:r0 + 128, c0:c0 + cw], in_=qb[:, 0:cw])


def emit_l2_ffn(ctx, x_tiles, d):
    k = ctx.k
    hp = ctx.hp
    I2 = 2816
    IMP = I2 // 128
    xn = [ctx.scr(f"xn{i}", (128, TT), BF16) for i in range(DP)]
    xnf = [ctx.scr(f"moey{i}", (128, TT)) for i in range(DP)]
    absw = ctx.scr("l0_hval", (128, TT))
    esc = ctx.scr("l0_arg", (128, TT))
    hf = ctx.scr("ln_sq", (128, TT))
    hb = ctx.scr("ln_sub", (128, TT))
    E8 = ctx.scr("l2f_E8", (8, 128), BF16)
    k.g("dma_start", out=E8[0:8, :], in_=d["e8"].ap())
    sclp = ctx.scr("fp4_sclp", (128, 128), BF16)
    sfm = ctx.scr("l2f_sfm", (8, TT), BF16)
    psT = None

    def quant_fm(src_tiles, dst_tiles, ndp):
        for dp in range(ndp):
            for sub in range(4):
                pst = ctx.PS[5][:, 0:128]
                k.t("transpose", pst, ctx.scr("l0_arg", (128, TT))[:, sub * 128:(sub + 1) * 128],
                    ctx.ident_f32()[:])
                k.v("tensor_reduce", ctx.scr("l2f_stm", (128, 8))[:],
                    pst.rearrange("p (b s) -> p b s", b=8), AX.X, ALU.max)
                k.v("tensor_scalar_mul", ctx.scr("l2f_stm", (128, 8))[:],
                    ctx.scr("l2f_stm", (128, 8))[:], 1.0 / 6.0)
                k.v("tensor_scalar_max", ctx.scr("l2f_stm", (128, 8))[:],
                    ctx.scr("l2f_stm", (128, 8))[:], 1e-12)
                k.v("tensor_copy", sclp[:, 0:8], ctx.scr("l2f_stm", (128, 8))[:])
                psb = ctx.PS[5][:, 0:64].bitcast(BF16)
                k.t("transpose", psb, sclp[:], ctx.ident_bf()[:])
                k.v("tensor_copy", sfm[0:8, sub * 128:(sub + 1) * 128], psb[0:8, 0:128])
            k.t("matmul", ctx.PS[4][0:128, 0:TT], E8[0:8, :], sfm[0:8, 0:TT],
                start=True, stop=True)
            sgn = ctx.scr("ln_sub", (128, TT))
            g = ctx.scr("moeha", (128, TT))
            acc = ctx.scr("moehb", (128, TT))
            k.s("sign", sgn[:, 0:TT], src_tiles[dp][:, 0:TT])
            for i in range(7):
                k.v("scalar_tensor_tensor", g[:, 0:TT], ctx.PS[4][0:128, 0:TT], FP4_MIDS[i],
                    ctx.scr("l0_hval", (128, TT))[:, 0:TT], ALU.mult, ALU.is_lt)
                if i == 0:
                    k.v("tensor_scalar_mul", acc[:, 0:TT], g[:, 0:TT], FP4_WTS[0])
                else:
                    k.v("scalar_tensor_tensor", acc[:, 0:TT], g[:, 0:TT], FP4_WTS[i],
                        acc[:, 0:TT], ALU.mult, ALU.add)
            k.v("tensor_mul", sgn[:, 0:TT], sgn[:, 0:TT], ctx.PS[4][0:128, 0:TT])
            k.v("tensor_mul", dst_tiles[dp][:, 0:TT], acc[:, 0:TT], sgn[:, 0:TT])

    def prep_abs(src_tiles, dp):
        k.s("activation", ctx.scr("l0_hval", (128, TT))[:, 0:TT], src_tiles[dp][:, 0:TT], AF.Abs)
        k.v("tensor_copy", ctx.scr("l0_arg", (128, TT))[:, 0:TT],
            ctx.scr("l0_hval", (128, TT))[:, 0:TT])

    wq = ctx.wp
    hq = [hp[IMP + i] for i in range(IMP)]
    hfm = [hp[i] for i in range(IMP)]
    hf32 = [ctx.scr("l2f_h0", (128, TT))]
    for tt in range(NTT):
        tsl = slice(tt * TT, (tt + 1) * TT)
        emit_layernorm(ctx, x_tiles, xn, tt * TT, xn2_tiles=xnf)
        for dp in range(DP):
            prep_abs(xnf, dp)
            quant_one = quant_fm([xnf[dp]], [xn[dp]], 1)
        for mp in range(IMP):
            psg = ctx.PS[2]
            psu = ctx.PS[3]
            for kp in range(DP):
                k.y("dma_start", out=wq.sb[kp][:, 0:128],
                    in_=d["wgq"].ap()[mp * 128:(mp + 1) * 128, kp * 128:(kp + 1) * 128],
                    transpose=True)
                k.y("dma_start", out=wq.sb[kp][:, 128:256],
                    in_=d["wuq"].ap()[mp * 128:(mp + 1) * 128, kp * 128:(kp + 1) * 128],
                    transpose=True)
            for kp in range(DP):
                k.t("matmul", psg[:, 0:TT], wq.sb[kp][:, 0:128], xn[kp][:, 0:TT],
                    start=(kp == 0), stop=(kp == DP - 1))
            for kp in range(DP):
                k.t("matmul", psu[:, 0:TT], wq.sb[kp][:, 128:256], xn[kp][:, 0:TT],
                    start=(kp == 0), stop=(kp == DP - 1))
            k.s("activation", hf[:, 0:TT], psg[:, 0:TT], AF.Silu)
            k.v("tensor_mul", hf32[0][:, 0:TT], hf[:, 0:TT], psu[:, 0:TT])
            prep_abs(hf32, 0)
            quant_fm([hf32[0]], [hq[mp]], 1)
        for mp in range(DP):
            ps = ctx.PS[2 + mp % 2]
            for i, kp in enumerate(range(IMP)):
                if i % 8 == 0:
                    for j in range(min(8, IMP - i)):
                        k.y("dma_start", out=wq.sb[j][:, 256:384],
                            in_=d["wdq"].ap()[mp * 128:(mp + 1) * 128,
                                              (i + j) * 128:(i + j + 1) * 128],
                            transpose=True)
                k.t("matmul", ps[:, 0:TT], wq.sb[i % 8][:, 256:384],
                    hq[kp][:, 0:TT], start=(i == 0), stop=(i == IMP - 1))
            k.v("tensor_add", x_tiles[mp][:, tsl], x_tiles[mp][:, tsl], ps[:, 0:TT])


def decl_l2_ffn(nc):
    d = {}
    d["wg"] = nc.dram_tensor("l2f_wg", [2816, 1024], F32, kind="ExternalInput")
    d["wu"] = nc.dram_tensor("l2f_wu", [2816, 1024], F32, kind="ExternalInput")
    d["wd"] = nc.dram_tensor("l2f_wd", [1024, 2816], F32, kind="ExternalInput")
    d["e8"] = nc.dram_tensor("l2f_e8", [8, 128], F32, kind="ExternalInput")
    d["wgq"] = nc.dram_tensor("l2f_wgq", [2816, 1024], BF16, kind="Internal")
    d["wuq"] = nc.dram_tensor("l2f_wuq", [2816, 1024], BF16, kind="Internal")
    d["wdq"] = nc.dram_tensor("l2f_wdq", [1024, 2816], BF16, kind="Internal")
    return d


def stage_l2_ffn(inp, core):
    e8 = np.zeros((8, 128), np.float32)
    for m in range(128):
        e8[m // 16, m] = 1.0
    return {
        "l2f_wg": inp["L2_ffn_w_gate"],
        "l2f_wu": inp["L2_ffn_w_up"],
        "l2f_wd": inp["L2_ffn_w_down"],
        "l2f_e8": e8,
    }


"""Full megakernel driver: builds the single-launch program + staging."""
import numpy as np
import ml_dtypes


from concourse.bass_utils import run_bass_kernel_spmd


def build_program():
    k = Ker("mk_full", serial=True)
    k.auto_sched = True
    ctx = Ctx(k)
    x_in, x_out, xt = make_x_io(k)
    nc = k.nc
    decls = {}
    decls["l0a"] = decl_attn_softmax(nc, "l0a", ntsets=3, rope=True)
    decls["l0a_ad"] = attn_drams(nc, "l0a")
    decls["l0f"] = decl_l0_ffn(nc)
    decls["l0b"] = decl_l0_attn_b(nc)
    decls["l1a"] = decl_attn_softmax(nc, "l1a", ntsets=1)
    decls["l1a_ad"] = attn_drams(nc, "l1a")
    decls["l1f"] = decl_l1_ffn(nc)
    decls["l1b"] = decl_l1_attn_b(nc)
    decls["l1b_ad"] = attn_drams(nc, "l1b2")
    decls["l2a"] = decl_l2_attn_a(nc)
    decls["l2a_ad"] = attn_drams(nc, "l2a2", kdim=128, vdim=128)
    decls["l2f"] = decl_l2_ffn(nc)
    decls["l2b"] = decl_attn_softmax(nc, "l2b", ntsets=1, ghost=True)
    decls["l2b_ad"] = attn_drams(nc, "l2b")
    decls["l3a"] = decl_attn_softmax(nc, "l3a", ntsets=1)
    decls["l3a_ad"] = attn_drams(nc, "l3a")
    decls["l3f"] = decl_l3_ffn(nc)
    decls["mb"] = decl_l3_attn_b(nc)
    emit_bitnet_quant(ctx, decls["l3f"]["l3f_W1T"], decls["l3f"]["l3f_w1q"], D, 4096, "w1")
    emit_bitnet_quant(ctx, decls["l3f"]["l3f_W2T"], decls["l3f"]["l3f_w2q"], 4096, D, "w2")
    emit_l2_ffn_prep(ctx, decls["l2f"])
    emit_attn_softmax(ctx, xt, decls["l0a"], decls["l0a_ad"], 0.125, ntsets=3, rope=True)
    emit_l0_ffn(ctx, xt, decls["l0f"])
    emit_l0_attn_b(ctx, xt, decls["l0b"])
    emit_l0_ffn(ctx, xt, decls["l0f"])
    emit_attn_softmax(ctx, xt, decls["l1a"], decls["l1a_ad"], 0.125, ntsets=1)
    emit_l1_ffn(ctx, xt, decls["l1f"])
    emit_l1_attn_b(ctx, xt, decls["l1b"], decls["l1b_ad"])
    emit_l1_ffn(ctx, xt, decls["l1f"])
    emit_l2_attn_a(ctx, xt, decls["l2a"], decls["l2a_ad"])
    emit_l2_ffn(ctx, xt, decls["l2f"])
    emit_attn_softmax(ctx, xt, decls["l2b"], decls["l2b_ad"], 0.125, ntsets=1,
                        ghost=True, exactmax=True)
    emit_l2_ffn(ctx, xt, decls["l2f"])
    emit_attn_softmax(ctx, xt, decls["l3a"], decls["l3a_ad"], 0.125, ntsets=1)
    emit_l3_ffn(ctx, xt, decls["l3f"])
    emit_l3_attn_b(ctx, xt, decls["mb"])
    emit_l3_ffn(ctx, xt, decls["l3f"])
    store_x(k, x_out, xt)
    return k


def stage_all(inp, core):
    m = {}
    m.update(stage_l0_attn_a(inp, core))
    m.update(stage_l0_ffn(inp))
    m.update(stage_l0_attn_b(inp, core))
    m.update(stage_l1_attn_a(inp, core))
    m.update(stage_l1_ffn(inp))
    m.update(stage_l1_attn_b(inp, core))
    m.update(stage_l2_attn_a(inp, core))
    m.update(stage_l2_ffn(inp, core))
    m.update(stage_l2_attn_b(inp, core))
    m.update(stage_l3_attn_a(inp, core))
    m.update(stage_l3_ffn(inp))
    m.update(stage_l3_attn_b(inp, core))
    return m


def kernel(**inputs):
    x = inputs["L0_attn_a_hidden_states"]
    k = build_program()
    nc = k.build()
    in_maps = []
    for c in range(NCORES):
        m = {"x_in": host_fm(x, c)}
        m.update(stage_all(inputs, c))
        in_maps.append(m)
    res = run_bass_kernel_spmd(nc, in_maps, core_ids=list(range(NCORES)))
    return host_unfm([res.results[c]["x_out"] for c in range(NCORES)])


# version 03 (final)
# speedup vs baseline: 1.1735x; 1.1735x over previous
import numpy as np
import concourse.bass as bass
import concourse.mybir as mybir

F32 = mybir.dt.float32
BF16 = mybir.dt.bfloat16
AF = mybir.ActivationFunctionType
ALU = mybir.AluOpType
AX = mybir.AxisListType

ENGINES = ("tensor", "vector", "scalar", "gpsimd", "sync")


class Sem:
    __slots__ = ("h", "count")

    def __init__(self, h):
        self.h = h
        self.count = 0


class Op:
    __slots__ = ("engine", "method", "args", "kwargs", "waits", "incs", "isdma")

    def __init__(self, engine, method, args, kwargs, waits, incs, isdma):
        self.engine = engine
        self.method = method
        self.args = args
        self.kwargs = kwargs
        self.waits = waits
        self.incs = incs
        self.isdma = isdma


class Ker:
    def __init__(self, name="k", serial=True):
        self.nc = bass.Bass("TRN2", target_bir_lowering=False, name=name)
        self.seq = []
        self.sems = []
        self.serial = serial
        self._c = None
        self._built = False

    def sem(self, name=None):
        if name is None:
            name = f"sem{len(self.sems)}"
        s = Sem(self.nc.alloc_semaphore(name=name))
        self.sems.append(s)
        return s

    @property
    def c(self):
        if self._c is None:
            self._c = self.sem("chain")
        return self._c

    def sb(self, name, shape, dt=F32, **kw):
        return self.nc.alloc_sbuf_tensor(name, list(shape), dt, **kw)

    def ps(self, name, shape, dt=F32):
        return self.nc.alloc_psum_tensor(name, list(shape), dt)

    def dram(self, name, shape, dt=F32, kind="Internal", **kw):
        return self.nc.dram_tensor(name, list(shape), dt, kind=kind, **kw)

    def const(self, data, name=None):
        return self.nc.inline_tensor(np.ascontiguousarray(data), name=name)

    def fconst(self, val, dt=F32):
        if not hasattr(self, "_fconsts"):
            self._fconsts = {}
        key = (float(val), dt)
        if key not in self._fconsts:
            t = self.sb(f"fc_{len(self._fconsts)}", (128, 1), dt)
            self.g("memset", t[:], float(val))
            self._fconsts[key] = t
        return self._fconsts[key][:]

    def op(self, engine, method, *args, wait=None, inc=None, **kwargs):
        isdma = method.startswith("dma") or method == "collective_compute"
        for (s, dlt) in (inc or []):
            s.count += dlt
        self.seq.append(Op(engine, method, args, kwargs, list(wait or []), list(inc or []), isdma))

    def t(self, method, *a, **k): self.op("tensor", method, *a, **k)
    def v(self, method, *a, **k): self.op("vector", method, *a, **k)
    def s(self, method, *a, **k): self.op("scalar", method, *a, **k)
    def g(self, method, *a, **k): self.op("gpsimd", method, *a, **k)
    def y(self, method, *a, **k): self.op("sync", method, *a, **k)

    def wait(self, engine, sem, val=None):
        if val is None and sem is self._c:
            v = None
        elif val is None:
            v = sem.count
        else:
            v = val
        self.seq.append(Op(engine, "_wait", (), {}, [(sem, v)], [], False))

    def barrier(self):
        pass



    def _ap_ranges(self, a):
        import concourse.bass as bass
        if not hasattr(a, "tensor") or not hasattr(a, "ap"):
            return None
        try:
            t = a.tensor
            name = t.name
            esz = {"dt.float32": 4, "dt.bfloat16": 2, "dt.int32": 4, "dt.uint8": 1,
                   "dt.float16": 2}.get(str(a.dtype), 4)
            off = a.offset
            if not isinstance(off, int):
                return (name, 0, 1 << 40, False)
            ext = 1
            for (step, count) in list(a.ap):
                ext += abs(int(step)) * (int(count) - 1)
            start = int(off) * esz
            end = start + ext * esz
            spc = str(getattr(a, "space", ""))
            is_psum = "PSUM" in spc.upper() or name.startswith("PS")
            return (name, start, end, is_psum)
        except Exception:
            return None

    def _op_accesses(self, o):
        m = o.method
        args = list(o.args)
        kw = o.kwargs
        outs, ins = [], []
        if m == "dma_start":
            if "out" in kw:
                outs.append(kw["out"])
            if "in_" in kw:
                ins.append(kw["in_"])
            for a in args:
                ins.append(a)
        elif m == "collective_compute":
            return "BARRIER"
        elif m in ("memset", "memzero"):
            outs.append(args[0])
        elif m == "max_with_indices":
            outs += args[0:2]; ins += args[2:]
        else:
            if args:
                outs.append(args[0])
                ins += args[1:]
            if "accum_out" in kw and kw["accum_out"] is not None:
                outs.append(kw["accum_out"])
            for kname, v in kw.items():
                if kname in ("bias", "scale", "alpha", "scalar1", "scalar2", "initial"):
                    ins.append(v)
        if m in ("matmul", "transpose"):
            ins.append(args[0])
        rr, ww = [], []
        for a in ins:
            r = self._ap_ranges(a)
            if r is not None:
                rr.append(r)
        for a in outs:
            r = self._ap_ranges(a)
            if r is not None:
                ww.append(r)
        return (rr, ww)

    def _build_auto(self):
        nc = self.nc
        seq = self.seq
        n = len(seq)
        GR = 2048
        R = 4
        sC = {e: self.sem(f"ac_{e}") for e in ENGINES}
        for o in seq:
            o.waits = [(s2, v2) for (s2, v2) in o.waits if v2 is not None]
        openg = []
        opidx_on_engine = []
        nloc = {e: 0 for e in ENGINES}
        for o in seq:
            openg.append(o.engine)
            opidx_on_engine.append(nloc[o.engine])
            nloc[o.engine] += 1
        lasync = {e: [] for e in ENGINES}
        for o in seq:
            lasync[o.engine].append(o.isdma)
        isbar_g = [o.method == "collective_compute" for o in seq]
        bar_local = {(openg[i], opidx_on_engine[i]) for i in range(n) if isbar_g[i]}
        last_writer = {}
        last_readers = {}
        bar_after = None
        coll_cnt = {e: 0 for e in ENGINES}
        last_on = {e: -1 for e in ENGINES}
        last_dma_on = {e: -1 for e in ENGINES}
        deps_of = [None] * n
        bar_info = [None] * n
        TD = {e: set() for e in ENGINES}
        for i, o in enumerate(seq):
            e = o.engine
            li = opidx_on_engine[i]
            if isbar_g[i]:
                wc = {e2: last_on[e2] for e2 in ENGINES if last_on[e2] >= 0}
                wd = {e2: last_dma_on[e2] for e2 in ENGINES if last_dma_on[e2] >= 0}
                wb = {e2: c for e2, c in coll_cnt.items() if c > 0}
                for e2, dl in wd.items():
                    TD[e2].add(dl)
                bar_info[i] = (wc, wd, wb)
                coll_cnt[e] += 1
                bar_after = (e, coll_cnt[e])
                last_on[e] = li
                continue
            rr, ww = self._op_accesses(o)
            deps = []

            def keep(de, dl, e=e):
                if de != e:
                    return True
                return lasync[de][dl] or de == "vector"

            def slots(r):
                name, s0, s1, isps = r
                if isps:
                    yield (name, -1)
                    return
                for sl in range(s0 // GR, max(s0 // GR, (s1 - 1) // GR) + 1):
                    yield (name, sl)

            for r in rr:
                for key in slots(r):
                    lw = last_writer.get(key)
                    if lw is not None and keep(*lw):
                        deps.append(lw)
            for r in ww:
                for key in slots(r):
                    lw = last_writer.get(key)
                    if lw is not None and keep(*lw):
                        deps.append(lw)
                    lrs = last_readers.get(key)
                    if lrs:
                        for e2, l2 in lrs.items():
                            if keep(e2, l2):
                                deps.append((e2, l2))
            need = {}
            for (de, dl) in deps:
                kind = "D" if lasync[de][dl] else "C"
                kk = (de, kind)
                need[kk] = max(need.get(kk, -1), dl)
            for (de, kind), dl in need.items():
                if kind == "D":
                    TD[de].add(dl)
            deps_of[i] = (need, bar_after)
            for r in rr:
                for key in slots(r):
                    last_readers.setdefault(key, {})
                    prev2 = last_readers[key].get(e, -1)
                    last_readers[key][e] = max(prev2, li)
            for r in ww:
                for key in slots(r):
                    last_writer[key] = (e, li)
                    last_readers[key] = {}
            last_on[e] = li
            if o.isdma:
                last_dma_on[e] = li
        for e in ENGINES:
            if last_dma_on[e] >= 0:
                TD[e].add(last_dma_on[e])
        flagC = {e: set() for e in ENGINES}
        for item in deps_of:
            if item:
                need, _ = item
                for (de, kind), dl in need.items():
                    if kind == "C":
                        flagC[de].add(dl)
        meth_local0 = {e: [] for e in ENGINES}
        for o in seq:
            meth_local0[o.engine].append(o.method)
        for i in range(n):
            if bar_info[i]:
                wc = bar_info[i][0]
                for e2, dl in wc.items():
                    if (not lasync[e2][dl] and (e2, dl) not in bar_local
                            and meth_local0[e2][dl] != "_wait"):
                        flagC[e2].add(dl)
        meth_local = {e: [] for e in ENGINES}
        for o in seq:
            meth_local[o.engine].append(o.method)
        for e in ENGINES:
            for li in range(nloc[e] - 1, -1, -1):
                if (not lasync[e][li] and (e, li) not in bar_local
                        and meth_local[e][li] != "_wait"):
                    flagC[e].add(li)
                    break
        cumC = {e: [0] * nloc[e] for e in ENGINES}
        for e in ENGINES:
            rc = 0
            for li in range(nloc[e]):
                if li in flagC[e]:
                    rc += 1
                cumC[e][li] = rc
        dma_sem = {}
        drain = {}
        closeW = {}
        dsems = {}
        dfinal = {}
        sB = {}
        for e in ENGINES:
            if coll_cnt[e] > 0:
                sB[e] = self.sem(f"ab_{e}")
            dmas = [li for li in range(nloc[e])
                    if lasync[e][li] and (e, li) not in bar_local]
            if not dmas:
                continue
            sems_e = [self.sem(f"ad_{e}{r}") for r in range(R)]
            dsems[e] = sems_e
            semtot = [0] * R
            ep = 0
            first = True
            for li in dmas:
                r = ep % R
                if first and semtot[r] > 0:
                    drain[(e, li)] = (sems_e[r], semtot[r])
                first = False
                semtot[r] += 16
                dma_sem[(e, li)] = sems_e[r]
                if li in TD[e]:
                    closeW[(e, li)] = [(sems_e[r2], semtot[r2])
                                       for r2 in range(R) if semtot[r2] > 0]
                    ep += 1
                    first = True
            dfinal[e] = [(sems_e[r2], semtot[r2]) for r2 in range(R) if semtot[r2] > 0]
        waited = {e: {} for e in ENGINES}

        def wait_val(o, e, sem, v, key):
            if v > waited[e].get(key, 0):
                o.waits.append((sem, v))
                waited[e][key] = v

        def wait_dma_prefix(o, e, de, dl):
            for (sem, v) in closeW[(de, dl)]:
                wait_val(o, e, sem, v, ("D", de, id(sem)))

        for i, o in enumerate(seq):
            e = o.engine
            li = opidx_on_engine[i]
            if isbar_g[i]:
                wc, wd, wb = bar_info[i]
                for e2, c in wb.items():
                    wait_val(o, e, sB[e2], c, ("B", e2))
                for e2, dl in wc.items():
                    v = cumC[e2][dl]
                    if v > 0 and not (e2 == e):
                        wait_val(o, e, sC[e2], v, ("C", e2))
                    elif v > 0:
                        wait_val(o, e, sC[e2], v, ("C", e2))
                for e2, dl in wd.items():
                    wait_dma_prefix(o, e, e2, dl)
                o.incs.append((sB[e], 1))
                continue
            need, ba = deps_of[i]
            if ba is not None:
                be, bc = ba
                wait_val(o, e, sB[be], bc, ("B", be))
            for (de, kind), dl in sorted(need.items()):
                if kind == "D":
                    wait_dma_prefix(o, e, de, dl)
                else:
                    v = cumC[de][dl]
                    if v > 0:
                        wait_val(o, e, sC[de], v, ("C", de))
            dr = drain.get((e, li))
            if dr is not None:
                o.waits.append((dr[0], dr[1]))
            if (e, li) in dma_sem:
                o.incs.append((dma_sem[(e, li)], 16))
            elif li in flagC[e]:
                o.incs.append((sC[e], 1))
        tail = Op("sync", "_wait", (), {}, [], [], False)
        for e in ENGINES:
            if nloc[e] and cumC[e][nloc[e] - 1] > 0:
                tail.waits.append((sC[e], cumC[e][nloc[e] - 1]))
            for (sem, v) in dfinal.get(e, []):
                tail.waits.append((sem, v))
            if coll_cnt[e] > 0:
                tail.waits.append((sB[e], coll_cnt[e]))
        seq.append(tail)

    def build(self):
        assert not self._built
        self._built = True
        nc = self.nc
        seq = self.seq
        n = len(seq)
        import os
        full = os.environ.get("TRNKIT_FULLSERIAL") == "1"
        brk = set(filter(None, (os.environ.get("TRNKIT_BREAK") or "vector").split(",")))
        if os.environ.get("TRNKIT_AUTO") == "1" or getattr(self, "auto_sched", False):
            self._build_auto()
            seq = self.seq
            n = len(seq)
        elif self.serial:
            c = self.c
            cum = 0
            prev = None
            for i, o in enumerate(seq):
                nxt = seq[i + 1] if i + 1 < n else None
                if full or o.engine in brk:
                    nxt = None
                o.waits = [(s, cum if v is None else v) for (s, v) in o.waits]
                need_wait = prev is not None and (full or o.engine in brk
                                                  or prev.engine != o.engine or prev.isdma)
                if need_wait:
                    o.waits.append((c, cum))
                if o.method != "_wait":
                    if o.isdma:
                        if o.method == "collective_compute":
                            o.incs.append((c, 1)); cum += 1
                        else:
                            o.incs.append((c, 16)); cum += 16
                    elif nxt is None or nxt.engine != o.engine:
                        o.incs.append((c, 1)); cum += 1
                prev = o
            c.count = cum
        per = {e: [] for e in ENGINES}
        for o in seq:
            per[o.engine].append(o)
        with nc.Block() as block:
            for ename in ENGINES:
                recs = per[ename]
                if not recs:
                    continue

                def body(eng, recs=recs):
                    for o in recs:
                        for (s, v) in o.waits:
                            eng.wait_ge(s.h, v)
                        if o.method == "_wait":
                            continue
                        inst = getattr(eng, o.method)(*o.args, **o.kwargs)
                        for (s, d) in o.incs:
                            if inst is not None:
                                inst.then_inc(s.h, d)
                            else:
                                eng.nop().then_inc(s.h, d)

                getattr(block, ename)(body)
        nc.all_engine_barrier()
        nums = sorted({s.h.num for s in self.sems})
        for nn in nums:
            nc.gpsimd.sem_clear(range(nn, nn + 1))
        nc.all_engine_barrier()
        return nc


"""Emitters for the mk_hybrid_d4 megakernel. Feature-major convention:
activations [d on partitions (tiles of 128), tokens on free]. Per-core:
T=2048 tokens = 4 batches x 512 seq (batch-major column groups).

All emitters record into a trnkit.Ker (serial mode for v0 correctness).
Host staging helpers produce in_map entries (layout-only numpy).

PSUM pool (8 banks): ctx.PS[0..5] six (128,512) banks; ctx.PSW one (128,1024) = 2 banks.
Convention (serial mode, no concurrency): norm uses PS[0],PS[1],PSW; proj uses PS[2],PS[3];
other emitters may use PS[4],PS[5].
"""
import numpy as np
import ml_dtypes

import concourse.mybir as mybir

U8 = mybir.dt.uint8
I32 = mybir.dt.int32

B = 4
SC = 512
T = B * SC
D = 1024
DP = D // 128
TT = 512
NTT = T // TT
NCORES = 8
LSEQ = 4096
TWO_PI = float(2 * np.pi)


def bf(a):
    return np.ascontiguousarray(a).astype(ml_dtypes.bfloat16)


def cols(vec):
    n = vec.shape[0]
    return np.ascontiguousarray(vec.reshape(n // 128, 128).T)


class Ctx:
    def __init__(self, k: Ker):
        self.k = k
        self.onesf = k.sb("ones_f", (128, 1), F32)
        self.onesf1 = k.sb("ones_f1", (1, 128), F32)
        self.onesb1 = k.sb("ones_b1", (1, 128), BF16)
        k.g("memset", self.onesf[:], 1.0)
        k.g("memset", self.onesf1[:], 1.0)
        k.g("memset", self.onesb1[:], 1.0)
        self.PS = [k.ps(f"PS{i}", (128, 512)) for i in range(6)]
        self.PSW = k.ps("PSW", (128, 1024))
        self.identb = None
        self._scr = {}
        self._wp = None
        self._hp = None
        self._pool8k = None

    @property
    def pool8k(self):
        if self._pool8k is None:
            self._pool8k = self.k.sb("pool8k", (128, 4096), BF16)
        return self._pool8k

    @property
    def wp(self):
        if self._wp is None:
            self._wp = WStripes(self, "wp", 8, 1024)
        return self._wp

    @property
    def hp(self):
        if self._hp is None:
            self._hp = [self.k.sb(f"hp{i}", (128, TT), BF16) for i in range(44)]
        return self._hp

    def scr(self, name, shape, dt=F32):
        key = name
        if key not in self._scr:
            self._scr[key] = self.k.sb("scr_" + name, shape, dt)
            self._scrmeta = getattr(self, "_scrmeta", {})
            self._scrmeta[key] = (tuple(shape), dt)
        else:
            m = self._scrmeta[key]
            assert m == (tuple(shape), dt), f"scr {name} shape mismatch {m} vs {shape},{dt}"
        return self._scr[key]

    def ident_bf(self):
        if self.identb is None:
            self.identb = self.k.sb("ident_bf", (128, 128), BF16)
            ic = self.k.const(np.eye(128, dtype=np.float32), "identc")
            self.k.g("dma_start", out=self.identb[:], in_=ic.ap())
        return self.identb

    def ident_f32(self):
        if not hasattr(self, "identf"):
            self.identf = self.k.sb("ident_f32", (128, 128))
            icf = self.k.const(np.eye(128, dtype=np.float32), "identcf")
            self.k.g("dma_start", out=self.identf[:], in_=icf.ap())
        return self.identf

    def bcast_row(self, row_ap, out_psum_ap, n):
        self.k.t("matmul", out_psum_ap, self.onesf1[:], row_ap, start=True, stop=True)


def emit_layernorm(ctx, x_tiles, xn_tiles, tt0, ttw=TT, xn2_tiles=None, xn_local=True):
    k = ctx.k
    psA, psB, psW = ctx.PS[0], ctx.PS[1], ctx.PSW
    sq = ctx.scr("ln_sq", (128, TT))
    rows = ctx.scr("ln_rows", (1, 2 * TT))
    sl = slice(tt0, tt0 + ttw)
    for dp in range(DP):
        k.s("activation", sq[:, 0:ttw], x_tiles[dp][:, sl], AF.Square)
        k.t("matmul", psA[0:1, 0:ttw], ctx.onesf[:], x_tiles[dp][:, sl],
            start=(dp == 0), stop=(dp == DP - 1))
        k.t("matmul", psB[0:1, 0:ttw], ctx.onesf[:], sq[:, 0:ttw],
            start=(dp == 0), stop=(dp == DP - 1))
    k.v("tensor_scalar_mul", rows[0:1, 0:ttw], psA[0:1, 0:ttw], 1.0 / D)
    k.v("tensor_scalar_mul", rows[0:1, TT:TT + ttw], psB[0:1, 0:ttw], 1.0 / D)
    musq = ctx.scr("ln_musq", (1, TT))
    k.v("tensor_mul", musq[0:1, 0:ttw], rows[0:1, 0:ttw], rows[0:1, 0:ttw])
    k.v("tensor_sub", rows[0:1, TT:TT + ttw], musq[0:1, 0:ttw], rows[0:1, TT:TT + ttw])
    k.s("activation", rows[0:1, TT:TT + ttw], rows[0:1, TT:TT + ttw], AF.Sqrt,
        bias=ctx.k.fconst(1e-5)[0:1, :], scale=-1.0)
    k.v("reciprocal", rows[0:1, TT:TT + ttw], rows[0:1, TT:TT + ttw])
    k.t("matmul", psW[:, 0:ttw], ctx.onesf1[:], rows[0:1, 0:ttw], start=True, stop=True)
    k.t("matmul", psW[:, TT:TT + ttw], ctx.onesf1[:], rows[0:1, TT:TT + ttw], start=True, stop=True)
    sub = ctx.scr("ln_sub", (128, TT))
    osl = slice(0, ttw) if xn_local else sl
    for dp in range(DP):
        k.v("tensor_sub", sub[:, 0:ttw], x_tiles[dp][:, sl], psW[:, 0:ttw])
        k.v("tensor_mul", xn_tiles[dp][:, osl], sub[:, 0:ttw], psW[:, TT:TT + ttw])
        if xn2_tiles is not None:
            k.v("tensor_mul", xn2_tiles[dp][:, osl], sub[:, 0:ttw], psW[:, TT:TT + ttw])


class WStripes:

    def __init__(self, ctx, tag, kp_n, ncols):
        self.KP = kp_n
        self.ncols = ncols
        self.sb = [ctx.k.sb(f"w_{tag}_{i}", (128, ncols), BF16) for i in range(kp_n)]

    def load(self, ctx, w_dram_ap, c0, cw, r0=0, dst0=0):
        for kp in range(self.KP):
            ctx.k.g("dma_start", out=self.sb[kp][:, dst0:dst0 + cw],
                    in_=w_dram_ap[r0 + kp * 128:r0 + (kp + 1) * 128, c0:c0 + cw])


def emit_mm_chain(ctx, ps_ap, stripes, m0, in_tiles, tsl, kps=None, start=True, stop=True):
    k = ctx.k
    kps = kps if kps is not None else range(stripes.KP)
    kps = list(kps)
    for i, kp in enumerate(kps):
        k.t("matmul", ps_ap, stripes.sb[kp][:, m0:m0 + 128], in_tiles[kp][:, tsl],
            start=(start and i == 0), stop=(stop and i == len(kps) - 1))


def emit_sin(ctx, out_ap, in_ap, shape):
    k = ctx.k
    p, n = shape
    mi = ctx.scr("l2f_h0", (128, TT))[:].bitcast(I32)
    mf = ctx.scr("moeha", (128, TT))
    k.v("tensor_scalar_mul", mi[0:p, 0:n], in_ap, 1.0 / TWO_PI)
    k.v("tensor_copy", mf[0:p, 0:n], mi[0:p, 0:n])
    k.v("scalar_tensor_tensor", mf[0:p, 0:n], mf[0:p, 0:n], -TWO_PI, in_ap,
        ALU.mult, ALU.add)
    msk = ctx.scr("l2f_h0", (128, TT))
    k.v("tensor_scalar", msk[0:p, 0:n], mf[0:p, 0:n], float(np.pi), -TWO_PI,
        ALU.is_gt, ALU.mult)
    k.v("tensor_tensor", mf[0:p, 0:n], mf[0:p, 0:n], msk[0:p, 0:n], ALU.add)
    k.v("tensor_scalar", msk[0:p, 0:n], mf[0:p, 0:n], float(-np.pi), TWO_PI,
        ALU.is_lt, ALU.mult)
    k.v("tensor_tensor", mf[0:p, 0:n], mf[0:p, 0:n], msk[0:p, 0:n], ALU.add)
    k.s("activation", out_ap, mf[0:p, 0:n], AF.Sin)


def emit_l0_ffn(ctx, x_tiles, d):
    k = ctx.k
    DFF = 4096
    MP1 = DFF // 128
    xn = [ctx.scr(f"xn{i}", (128, TT), BF16) for i in range(DP)]
    hp = ctx.hp
    w1s = ctx.wp
    w2s = ctx.wp
    W1CH = 4
    b1c = ctx.scr("l0_b1c", (128, MP1))
    mpc = ctx.scr("l0_mpc", (128, MP1))
    thc = ctx.scr("l0_thc", (128, MP1))
    b2c = ctx.scr("l0_b2c", (128, DP))
    tmc = ctx.scr("l0_tmc", (128, MP1))
    k.g("dma_start", out=b1c[:], in_=d["l0f_b1c"].ap())
    k.g("dma_start", out=thc[:], in_=d["l0f_thetac"].ap())
    k.g("dma_start", out=mpc[:], in_=d["l0f_wic"].ap())
    k.g("dma_start", out=tmc[:], in_=d["l0f_wrc"].ap())
    k.g("dma_start", out=b2c[:], in_=d["l0f_b2c"].ap())
    sn = ctx.scr("l0_sn", (128, MP1))
    cs = ctx.scr("l0_cs", (128, MP1))
    emit_sin(ctx, sn[:], thc[:], (128, MP1))
    k.v("tensor_scalar_add", cs[:], thc[:], float(np.pi / 2))
    emit_sin(ctx, cs[:], cs[:], (128, MP1))
    k.v("tensor_mul", cs[:], cs[:], tmc[:])
    k.v("tensor_mul", sn[:], sn[:], mpc[:])
    k.v("tensor_add", cs[:], cs[:], sn[:])
    k.v("tensor_mul", mpc[:], mpc[:], cs[:])
    k.v("tensor_scalar_mul", mpc[:], mpc[:], 0.1)

    hval = ctx.scr("l0_hval", (128, TT))
    arg = ctx.scr("l0_arg", (128, TT))
    ps8 = ctx.PS + [ctx.PSW[:, 0:512], ctx.PSW[:, 512:1024]]
    for tt in range(NTT):
        tsl = slice(tt * TT, (tt + 1) * TT)
        emit_layernorm(ctx, x_tiles, xn, tt * TT)
        xnsl = slice(tt * TT, (tt + 1) * TT) if False else slice(0, TT)
        for c in range(4):
            w1s.load(ctx, d["l0f_W1"].ap(), c * 1024, 1024)
            for ml in range(8):
                mp = c * 8 + ml
                ps = ctx.PS[2 + mp % 2]
                emit_mm_chain(ctx, ps[:, 0:TT], w1s, ml * 128, xn, xnsl)
                k.v("tensor_scalar_add", hval[:], ps[:, 0:TT], b1c[:, mp:mp + 1])
                k.v("tensor_scalar", arg[:], hval[:], 0.1, thc[:, mp:mp + 1], ALU.mult, ALU.add)
                emit_sin(ctx, arg[:], arg[:], (128, TT))
                k.v("scalar_tensor_tensor", hval[:], arg[:], mpc[:, mp:mp + 1], hval[:],
                    ALU.mult, ALU.add)
                k.v("tensor_relu", hp[mp][:, 0:TT], hval[:])
        for kc in range(4):
            w2s.load(ctx, d["l0f_W2"].ap(), 0, D, r0=kc * 1024)
            for mp in range(DP):
                emit_mm_chain(ctx, ps8[mp][0:128, 0:TT] if mp >= 6 else ps8[mp][:, 0:TT],
                              w2s, mp * 128, hp[kc * 8:(kc + 1) * 8], slice(0, TT),
                              start=(kc == 0), stop=(kc == 3))
        for mp in range(DP):
            psap = ps8[mp][0:128, 0:TT] if mp >= 6 else ps8[mp][:, 0:TT]
            k.v("tensor_scalar_add", hval[:], psap, b2c[:, mp:mp + 1])
            k.v("tensor_add", x_tiles[mp][:, tsl], x_tiles[mp][:, tsl], hval[:])


def stage_l0_ffn(inp):
    return {
        "l0f_W1": inp["L0_ffn_W1"],
        "l0f_b1c": cols(inp["L0_ffn_b1"]),
        "l0f_thetac": cols(inp["L0_ffn_quantum_phases"]),
        "l0f_wrc": cols(inp["L0_ffn_quantum_weights_real"]),
        "l0f_wic": cols(inp["L0_ffn_quantum_weights_imag"]),
        "l0f_W2": inp["L0_ffn_W2"],
        "l0f_b2c": cols(inp["L0_ffn_b2"]),
    }


def decl_l0_ffn(nc):
    d = {}
    d["l0f_W1"] = nc.dram_tensor("l0f_W1", [D, 4096], F32, kind="ExternalInput")
    d["l0f_b1c"] = nc.dram_tensor("l0f_b1c", [128, 32], F32, kind="ExternalInput")
    d["l0f_thetac"] = nc.dram_tensor("l0f_thetac", [128, 32], F32, kind="ExternalInput")
    d["l0f_wrc"] = nc.dram_tensor("l0f_wrc", [128, 32], F32, kind="ExternalInput")
    d["l0f_wic"] = nc.dram_tensor("l0f_wic", [128, 32], F32, kind="ExternalInput")
    d["l0f_W2"] = nc.dram_tensor("l0f_W2", [4096, D], F32, kind="ExternalInput")
    d["l0f_b2c"] = nc.dram_tensor("l0f_b2c", [128, 8], F32, kind="ExternalInput")
    return d


def host_fm(x_full, core):
    xs = x_full[:, core * SC:(core + 1) * SC, :]
    return np.ascontiguousarray(xs.transpose(2, 0, 1).reshape(D, T))


def host_unfm(x_fm_list):
    out = np.zeros((B, LSEQ, D), dtype=np.float32)
    for c, xf in enumerate(x_fm_list):
        out[:, c * SC:(c + 1) * SC, :] = np.asarray(xf, dtype=np.float32).reshape(D, B, SC).transpose(1, 2, 0)
    return out


def make_x_io(k):
    nc = k.nc
    x_in = nc.dram_tensor("x_in", [D, T], F32, kind="ExternalInput")
    x_out = nc.dram_tensor("x_out", [D, T], F32, kind="ExternalOutput")
    xt = [k.sb(f"x{i}", (128, T), F32) for i in range(DP)]
    for i in range(DP):
        k.y("dma_start", out=xt[i][:], in_=x_in.ap()[i * 128:(i + 1) * 128, :])
    return x_in, x_out, xt


def store_x(k, x_out, xt):
    for i in range(DP):
        k.y("dma_start", out=x_out.ap()[i * 128:(i + 1) * 128, :], in_=xt[i][:])
    k.wait("sync", k.c)


def emit_bitnet_quant(ctx, w_dram, wq_dram, n_in, n_out, tag, row0=0, row1=None):
    k = ctx.k
    row1 = row1 if row1 is not None else n_in
    wt = ctx.scr("ln_sq", (128, 512))
    ab = ctx.scr("ln_sub", (128, 512))
    acc = ctx.scr("bq_acc", (128, 1))
    col = ctx.scr("bq_col", (128, 1))
    thr = ctx.scr("bq_thr", (128, 2))
    row11 = ctx.scr("bq_row", (1, 2))
    qa = ctx.scr("l0_hval", (128, 512))
    qb = ctx.scr("at_pt", (128, 512), BF16)
    ps = ctx.PS[4]
    k.g("memset", acc[:], 0.0)
    for r0 in range(0, n_in, 128):
        for c0 in range(0, n_out, 512):
            k.y("dma_start", out=wt[:], in_=w_dram.ap()[r0:r0 + 128, c0:c0 + 512])
            k.s("activation", ab[:], wt[:], AF.Abs)
            k.v("tensor_reduce", col[:], ab[:], AX.X, ALU.add)
            k.v("tensor_add", acc[:], acc[:], col[:])
    k.t("matmul", ps[0:1, 0:1], ctx.onesf[:], acc[:], start=True, stop=True)
    k.v("tensor_copy", row11[0:1, 0:1], ps[0:1, 0:1])
    k.t("matmul", ps[:, 0:1], ctx.onesf1[:], row11[0:1, 0:1], start=True, stop=True)
    NW = float(n_in * n_out)
    k.v("tensor_scalar", thr[:, 0:1], ps[:, 0:1], 0.5 / NW, 0.5e-5, ALU.mult, ALU.add)
    k.v("tensor_scalar", thr[:, 1:2], ps[:, 0:1], -0.5 / NW, -0.5e-5, ALU.mult, ALU.add)
    for r0 in range(row0, row1, 128):
        for c0 in range(0, n_out, 512):
            k.y("dma_start", out=wt[:], in_=w_dram.ap()[r0:r0 + 128, c0:c0 + 512])
            k.v("tensor_single_scalar", qa[:], wt[:], thr[:, 0:1], ALU.is_gt)
            k.v("tensor_single_scalar", ab[:], wt[:], thr[:, 1:2], ALU.is_lt)
            k.v("tensor_sub", qb[:], qa[:], ab[:])
            k.y("dma_start", out=wq_dram.ap()[r0:r0 + 128, c0:c0 + 512], in_=qb[:])


def emit_l3_ffn(ctx, x_tiles, d):
    k = ctx.k
    DFF = 4096
    MP1 = DFF // 128
    xn = [ctx.scr(f"xn{i}", (128, TT), BF16) for i in range(DP)]
    hp = ctx.hp
    w1s = ctx.wp
    w2s = ctx.wp
    b1c = ctx.scr("l3_b1c", (128, MP1))
    b2c = ctx.scr("l3_b2c", (128, DP))
    k.g("dma_start", out=b1c[:], in_=d["l3f_b1c"].ap())
    k.g("dma_start", out=b2c[:], in_=d["l3f_b2c"].ap())
    hval = ctx.scr("l0_hval", (128, TT))
    ps8 = ctx.PS + [ctx.PSW[:, 0:512], ctx.PSW[:, 512:1024]]
    for tt in range(NTT):
        tsl = slice(tt * TT, (tt + 1) * TT)
        emit_layernorm(ctx, x_tiles, xn, tt * TT)
        for c in range(4):
            w1s.load(ctx, d["l3f_w1q"].ap(), c * 1024, 1024)
            for ml in range(8):
                mp = c * 8 + ml
                ps = ctx.PS[2 + mp % 2]
                emit_mm_chain(ctx, ps[:, 0:TT], w1s, ml * 128, xn, slice(0, TT))
                k.v("tensor_scalar_add", hval[:], ps[:, 0:TT], b1c[:, mp:mp + 1])
                k.v("tensor_relu", hp[mp][:, 0:TT], hval[:])
        for kc in range(4):
            w2s.load(ctx, d["l3f_w2q"].ap(), 0, D, r0=kc * 1024)
            for mp in range(DP):
                emit_mm_chain(ctx, ps8[mp][:, 0:TT], w2s, mp * 128, hp[kc * 8:(kc + 1) * 8],
                              slice(0, TT), start=(kc == 0), stop=(kc == 3))
        for mp in range(DP):
            k.v("tensor_scalar_add", hval[:], ps8[mp][:, 0:TT], b2c[:, mp:mp + 1])
            k.v("tensor_add", x_tiles[mp][:, tsl], x_tiles[mp][:, tsl], hval[:])


def decl_l3_ffn(nc, quant=True):
    d = {}
    d["l3f_W1T"] = nc.dram_tensor("l3f_W1T", [D, 4096], F32, kind="ExternalInput")
    d["l3f_W2T"] = nc.dram_tensor("l3f_W2T", [4096, D], F32, kind="ExternalInput")
    d["l3f_b1c"] = nc.dram_tensor("l3f_b1c", [128, 32], F32, kind="ExternalInput")
    d["l3f_b2c"] = nc.dram_tensor("l3f_b2c", [128, 8], F32, kind="ExternalInput")
    d["l3f_w1q"] = nc.dram_tensor("l3f_w1q", [D, 4096], BF16, kind="Internal")
    d["l3f_w2q"] = nc.dram_tensor("l3f_w2q", [4096, D], BF16, kind="Internal")
    return d


def stage_l3_ffn(inp):
    return {
        "l3f_W1T": np.ascontiguousarray(inp["L3_ffn_W1"].T),
        "l3f_W2T": np.ascontiguousarray(inp["L3_ffn_W2"].T),
        "l3f_b1c": cols(inp["L3_ffn_b1"]),
        "l3f_b2c": cols(inp["L3_ffn_b2"]),
    }


def emit_l1_ffn(ctx, x_tiles, d):
    k = ctx.k
    H = 5632
    HMP = H // 128
    HC = 512
    NHC = H // HC
    HCM = HC // 128
    xn = [ctx.scr(f"xn{i}", (128, TT), BF16) for i in range(DP)]
    hp = ctx.hp
    y_sb = [ctx.scr(f"moey{i}", (128, TT)) for i in range(DP)]
    w13 = ctx.wp
    w2s = ctx.wp
    gws = ctx.scr("moegw", (128, DP * 8), BF16)
    for kp in range(DP):
        k.g("dma_start", out=gws[:, kp * 8:(kp + 1) * 8], in_=d["l1f_gateT"].ap()[kp * 128:(kp + 1) * 128, :])
    ha = ctx.scr("moeha", (128, TT))
    hb = ctx.scr("moehb", (128, TT))
    lg = ctx.scr("moelg", (128, 8))
    m1 = ctx.scr("moem1", (128, 2))
    cwq = ctx.scr("moecw", (128, 4), BF16)
    cwpad = ctx.scr("c2r_pad", (128, 128), BF16)
    cwrow = ctx.pool8k[0:1, 0:2048]
    cwT = ctx.scr("c2r_T", (128, 128), BF16)
    cwbc = ctx.scr("at_pt", (128, TT), BF16)
    k.g("memset", cwpad[:], 0.0)
    psT = ctx.PS[5][:, 0:64].bitcast(BF16)
    for tt in range(NTT):
        tsl = slice(tt * TT, (tt + 1) * TT)
        emit_layernorm(ctx, x_tiles, xn, tt * TT)
        for tq in range(4):
            qsl = slice(tq * 128, (tq + 1) * 128)
            for kp in range(DP):
                k.t("matmul", ctx.PS[4][0:128, 0:8], xn[kp][:, qsl], gws[:, kp * 8:(kp + 1) * 8],
                    start=(kp == 0), stop=(kp == DP - 1))
            k.v("tensor_copy", lg[:], ctx.PS[4][0:128, 0:8])
            k.v("reduce_max", m1[:, 0:1], lg[:, 0:4], AX.X)
            k.v("tensor_single_scalar", ha[:, 0:4], lg[:, 0:4], m1[:, 0:1], ALU.is_ge)
            k.v("scalar_tensor_tensor", hb[:, 0:4], ha[:, 0:4], -1e30, lg[:, 0:4], ALU.mult, ALU.add)
            k.v("reduce_max", m1[:, 1:2], hb[:, 0:4], AX.X)
            k.v("tensor_single_scalar", ha[:, 0:4], lg[:, 0:4], m1[:, 1:2], ALU.is_ge)
            k.v("tensor_scalar_mul", m1[:, 0:1], m1[:, 0:1], -1.0)
            k.s("activation", hb[:, 0:4], lg[:, 0:4], AF.Exp, bias=m1[:, 0:1])
            k.v("tensor_mul", hb[:, 0:4], hb[:, 0:4], ha[:, 0:4])
            k.v("tensor_reduce", m1[:, 1:2], hb[:, 0:4], AX.X, ALU.add)
            k.v("reciprocal", m1[:, 1:2], m1[:, 1:2])
            k.v("tensor_single_scalar", cwq[:, 0:4], hb[:, 0:4], m1[:, 1:2], ALU.mult)
            k.v("tensor_copy", cwpad[:, 0:4], cwq[:, 0:4])
            k.t("transpose", psT, cwpad[:], ctx.ident_bf()[:])
            k.v("tensor_copy", cwT[:], psT[0:128, 0:128])
            for e in range(4):
                k.g("dma_start", out=cwrow[0:1, e * TT + tq * 128:e * TT + (tq + 1) * 128],
                    in_=cwT[e:e + 1, 0:128])
        for e in range(4):
            k.t("matmul", ctx.PS[4][0:128, 0:TT], ctx.onesb1[:], cwrow[0:1, e * TT:(e + 1) * TT],
                start=True, stop=True)
            k.v("tensor_copy", cwbc[:], ctx.PS[4][0:128, 0:TT])
            for hc in range(NHC):
                w13.load(ctx, d["l1f_w1"].ap()[e], hc * HC, HC, dst0=0)
                w13.load(ctx, d["l1f_w3"].ap()[e], hc * HC, HC, dst0=HC)
                for ml in range(HCM):
                    mp = hc * HCM + ml
                    emit_mm_chain(ctx, ctx.PS[2][:, 0:TT], w13, ml * 128, xn, slice(0, TT))
                    k.s("activation", ha[:], ctx.PS[2][:, 0:TT], AF.Silu)
                    emit_mm_chain(ctx, ctx.PS[3][:, 0:TT], w13, HC + ml * 128, xn, slice(0, TT))
                    k.v("tensor_mul", hb[:], ctx.PS[3][:, 0:TT], cwbc[:])
                    k.v("tensor_mul", hp[mp][:, 0:TT], ha[:], hb[:])
            for kc in range(6):
                kps = list(range(kc * 8, min(44, (kc + 1) * 8)))
                for i, kp in enumerate(kps):
                    k.g("dma_start", out=w2s.sb[i][:, 0:D],
                        in_=d["l1f_w2"].ap()[e, kp * 128:(kp + 1) * 128, :])
                for mp in range(DP):
                    ps = ctx.PS[2 + mp % 2]
                    for i, kp in enumerate(kps):
                        k.t("matmul", ps[:, 0:TT], w2s.sb[i][:, mp * 128:(mp + 1) * 128],
                            hp[kp][:, 0:TT], start=(i == 0), stop=(i == len(kps) - 1))
                    if e == 0 and kc == 0:
                        k.v("tensor_copy", y_sb[mp][:, 0:TT], ps[:, 0:TT])
                    else:
                        k.v("tensor_add", y_sb[mp][:, 0:TT], y_sb[mp][:, 0:TT], ps[:, 0:TT])
        for mp in range(DP):
            k.v("tensor_add", x_tiles[mp][:, tsl], x_tiles[mp][:, tsl], y_sb[mp][:, 0:TT])


def decl_l1_ffn(nc):
    d = {}
    d["l1f_gateT"] = nc.dram_tensor("l1f_gateT", [D, 8], F32, kind="ExternalInput")
    d["l1f_w1"] = nc.dram_tensor("l1f_w1", [4, D, 5632], F32, kind="ExternalInput")
    d["l1f_w2"] = nc.dram_tensor("l1f_w2", [4, 5632, D], F32, kind="ExternalInput")
    d["l1f_w3"] = nc.dram_tensor("l1f_w3", [4, D, 5632], F32, kind="ExternalInput")
    return d


def stage_l1_ffn(inp):
    return {
        "l1f_gateT": np.ascontiguousarray(inp["L1_ffn_gate_w"].T),
        "l1f_w1": np.ascontiguousarray(inp["L1_ffn_w1"][:4]),
        "l1f_w2": np.ascontiguousarray(inp["L1_ffn_w2"][:4]),
        "l1f_w3": np.ascontiguousarray(inp["L1_ffn_w3"][:4]),
    }



def attn_drams(nc, tag, kdim=1024, vdim=1024):
    d = {}
    d["k_in"] = nc.dram_tensor(f"{tag}_kin", [B * kdim, SC], BF16, kind="Internal")
    d["k_out"] = nc.dram_tensor(f"{tag}_kout", [NCORES * B * kdim, SC], BF16, kind="Internal")
    d["v_in"] = nc.dram_tensor(f"{tag}_vin", [T, vdim], BF16, kind="Internal")
    d["v_out"] = nc.dram_tensor(f"{tag}_vout", [NCORES * T, vdim], BF16, kind="Internal")
    return d


def emit_ag(ctx, ad):
    k = ctx.k
    k.g("collective_compute", "AllGather", ALU.bypass,
        replica_groups=[list(range(NCORES))],
        ins=[ad["k_in"].ap().opt()], outs=[ad["k_out"].ap().opt()])
    k.g("collective_compute", "AllGather", ALU.bypass,
        replica_groups=[list(range(NCORES))],
        ins=[ad["v_in"].ap().opt()], outs=[ad["v_out"].ap().opt()])


def emit_cols_to_row(ctx, src_ap, n, dst_row, dst_off, f32=False):
    k = ctx.k
    if f32:
        pad = ctx.scr("ln_sq", (128, TT))[:, 0:128]
        cT = ctx.scr("ln_sub", (128, TT))[:, 0:128]
        psT = ctx.PS[5][:, 0:128]
        ident = ctx.ident_f32()
    else:
        pad = ctx.scr("c2r_pad", (128, 128), BF16)
        cT = ctx.scr("c2r_T", (128, 128), BF16)
        psT = ctx.PS[5][:, 0:64].bitcast(BF16)
        ident = ctx.ident_bf()
    k.v("tensor_copy", pad[:, 0:n], src_ap)
    k.t("transpose", psT, pad[:, 0:128] if f32 else pad[:], ident[:])
    k.v("tensor_copy", cT[:, 0:128] if f32 else cT[:], psT[0:128, 0:128])
    for j in range(n):
        k.g("dma_start", out=dst_row[0:1, dst_off + j * 128:dst_off + (j + 1) * 128],
            in_=cT[j:j + 1, 0:128])


def emit_qkv_phase(ctx, x_tiles, d, ad, qscale, rope=None):
    k = ctx.k
    xn = [ctx.scr(f"xn{i}", (128, TT), BF16) for i in range(DP)]
    qsc2 = ctx.scr("at_qsc2", (128, TT), BF16)
    vsc = ctx.scr("at_mbcf", (128, 512))[:].bitcast(BF16)
    bvrow = ctx.scr("at_bvrow", (1, 1024), BF16)
    bvf = ctx.scr("ln_rows", (1, 2 * TT))
    hp = ctx.hp
    wq = ctx.wp
    k.g("dma_start", out=bvf[0:1, :], in_=d["bvrow"].ap())
    k.v("tensor_copy", bvrow[0:1, :], bvf[0:1, :])
    qbc = ctx.scr("at_qbc", (128, DP))
    kbc = ctx.scr("at_kbc", (128, DP))
    k.g("dma_start", out=qbc[:], in_=d["qbc"].ap())
    k.g("dma_start", out=kbc[:], in_=d["kbc"].ap())
    for b in range(B):
        if rope is not None:
            k.g("dma_start", out=rope["rc"][:, 0:TT], in_=rope["cosd"].ap()[:, b * TT:(b + 1) * TT])
            k.g("dma_start", out=rope["rc"][:, TT:2 * TT], in_=rope["sind"].ap()[:, b * TT:(b + 1) * TT])
        emit_layernorm(ctx, x_tiles, xn, b * TT)
        wq.load(ctx, d["qT"].ap(), 0, D)
        for mp in range(DP):
            ps = ctx.PS[2 + mp % 2]
            emit_mm_chain(ctx, ps[:, 0:TT], wq, mp * 128, xn, slice(0, TT))
            k.v("tensor_scalar", hp[mp * 4 + b][:, 0:TT], ps[:, 0:TT],
                qbc[:, mp:mp + 1], qscale, ALU.add, ALU.mult)
            if rope is not None:
                emit_rope(ctx, hp[mp * 4 + b], b, rope, qsc2)
        wq.load(ctx, d["kT"].ap(), 0, D)
        for mp in range(DP):
            ps = ctx.PS[2 + mp % 2]
            emit_mm_chain(ctx, ps[:, 0:TT], wq, mp * 128, xn, slice(0, TT))
            k.v("tensor_scalar", qsc2[:, 0:TT], ps[:, 0:TT],
                kbc[:, mp:mp + 1], 1.0, ALU.add, ALU.mult)
            if rope is not None:
                emit_rope(ctx, qsc2, b, rope, ctx.scr("at_qsc3", (128, TT), BF16))
            k.y("dma_start", out=ad["k_in"].ap()[b * D + mp * 128:b * D + (mp + 1) * 128, :],
                in_=qsc2[:, 0:TT])
        wq.load(ctx, d["vT"].ap(), 0, D)
        for ti in range(4):
            qsl = slice(ti * 128, (ti + 1) * 128)
            for half in range(2):
                psv = ctx.PS[2 + half]
                for kp in range(DP):
                    k.t("matmul", psv[0:128, 0:512], xn[kp][:, qsl],
                        wq.sb[kp][:, half * 512:(half + 1) * 512],
                        start=(kp == 0), stop=False)
                k.t("matmul", psv[0:128, 0:512], ctx.onesb1[:],
                    bvrow[0:1, half * 512:(half + 1) * 512], start=False, stop=True)
                k.v("tensor_copy", vsc[:, half * 512:(half + 1) * 512], psv[0:128, 0:512])
            k.y("dma_start", out=ad["v_in"].ap()[b * 512 + ti * 128:b * 512 + (ti + 1) * 128, :],
                in_=vsc)
    return xn


def emit_rope(ctx, tile, b, rope, scratch):
    k = ctx.k
    rot = ctx.PS[4]
    k.t("matmul", rot[:, 0:TT], rope["RT"][:], tile[:, 0:TT], start=True, stop=True)
    k.v("tensor_mul", scratch[:, 0:TT], tile[:, 0:TT], rope["cos"])
    k.v("tensor_mul", tile[:, 0:TT], rot[:, 0:TT], rope["sin"])
    k.v("tensor_add", tile[:, 0:TT], tile[:, 0:TT], scratch[:, 0:TT])


def emit_attn_softmax(ctx, x_tiles, d, ad, qscale, ntsets, rope=False, ghost=False, exactmax=False):
    k = ctx.k
    hp = ctx.hp
    hp0 = ctx.hp
    tsets = []
    for i in range(ntsets):
        tiles = [hp0[32 + i * 4 + sub] for sub in range(4)]
        sel = ctx.scr(f"at_sel{i}", (128, 8))
        for sub in range(4):
            k.g("dma_start", out=tiles[sub][:, 0:512],
                in_=d[f"tset{i}_tiles"].ap()[sub * 128:(sub + 1) * 128, :])
        k.g("dma_start", out=sel[:], in_=d[f"tset{i}_sel"].ap())
        tsets.append((tiles, sel))
    tsetsQ = []
    if exactmax:
        for i in range(ntsets):
            tilesq = [hp0[36 + i * 4 + qi] for qi in range(4)]
            for qi in range(4):
                k.g("dma_start", out=tilesq[qi][:, 0:512],
                    in_=d[f"tset{i}_tilesQ"].ap()[qi * 128:(qi + 1) * 128, :])
            tsetsQ.append(tilesq)
    offc = ctx.scr("at_offc", (128, 8))
    k.g("dma_start", out=offc[:], in_=d["offc"].ap())
    ghostc = None
    if ghost:
        ghostc = ctx.scr("at_ghostc", (1, 16))
        k.g("dma_start", out=ghostc[0:1, :], in_=d["ghostc"].ap())
    ropecfg = None
    if rope:
        rc = ctx.scr("at_ropec", (128, 2 * TT), BF16)
        rt = ctx.scr("at_ropeRT", (128, 128), BF16)
        k.g("dma_start", out=rt[:], in_=d["ropeRT"].ap())
        ropecfg = {"RT": rt, "cos": rc[:, 0:TT], "sin": rc[:, TT:2 * TT],
                   "cosd": d["ropecos"], "sind": d["ropesin"], "rc": rc}

    xn = emit_qkv_phase(ctx, x_tiles, d, ad, qscale, ropecfg)
    emit_ag(ctx, ad)

    kbh = ctx.pool8k[0:64, 0:4096]
    vbh01 = (ctx.wp.sb[0], ctx.wp.sb[1])
    mcols = ctx.scr("at_mcols", (128, 8))
    mall = ctx.scr("at_mall", (128, 4))
    mrow = ctx.scr("ln_rows", (1, 2 * TT))[0:1, 0:256].bitcast(BF16)
    mbc = ctx.scr("at_mbc", (128, 512), BF16)
    st = ctx.scr("at_st", (128, 512))
    pt = ctx.scr("at_pt", (128, 512), BF16)
    mrowf = ctx.scr("mb_brow", (1, TT)) if exactmax else None
    mbcf = ctx.scr("at_mbcf", (128, 512))
    lrow = ctx.scr("ln_musq", (1, TT))
    rlb = ctx.scr("at_mbcf", (128, 512))[0:64, :]
    onesbc = ctx.scr("at_onesbc", (128, 1), BF16)
    obc = ctx.scr("at_obc", (128, DP))
    k.g("memset", onesbc[:], 1.0)
    k.g("dma_start", out=obc[:], in_=d["obc"].ap())
    kout_r = ad["k_out"].ap().rearrange("(kc x) s -> x kc s", kc=NCORES)
    vout_r = ad["v_out"].ap().rearrange("(kc z hi p) o -> z p kc hi o", kc=NCORES, z=B, hi=4)
    oall = xn

    qwork = ctx.scr("at_qwork", (64, TT), BF16)
    for b in range(B):
        for h in range(16):
            qtile = hp[(h // 2) * 4 + b]
            r0 = (h % 2) * 64
            k.v("tensor_copy", qwork[:, 0:TT], qtile[r0:r0 + 64, 0:TT])
            qall = qwork[0:64, 0:TT]
            k.y("dma_start", out=kbh.rearrange("p (kc s) -> p kc s", kc=8),
                in_=kout_r[b * D + h * 64:b * D + h * 64 + 64])
            for kc in range(8):
                vb = vbh01[kc // 4]
                c0 = (kc % 4) * 256
                k.y("dma_start",
                    out=vb[:, c0:c0 + 256].rearrange("p (hi dd) -> p hi dd", hi=4),
                    in_=vout_r[b][:, kc, :, h * 64:(h + 1) * 64])
            for qi in range(4):
                qlhs = qwork[0:64, qi * 128:(qi + 1) * 128]
                for kc in range(8):
                    ps = ctx.PS[2 + kc % 2]
                    k.t("matmul", ps[0:128, 0:512], qlhs, kbh[0:64, kc * 512:(kc + 1) * 512],
                        start=True, stop=True)
                    if exactmax:
                        k.v("tensor_single_scalar", st[:], ps[0:128, 0:512],
                            offc[:, kc:kc + 1], ALU.add)
                        for i, (tiles_, sel_) in enumerate(tsets):
                            k.v("scalar_tensor_tensor", st[:], tsetsQ[i][qi][:, 0:512],
                                sel_[:, kc:kc + 1], st[:], ALU.mult, ALU.add)
                        k.v("reduce_max", mcols[:, kc:kc + 1], st[:], AX.X)
                    else:
                        k.v("reduce_max", mcols[:, kc:kc + 1], ps[0:128, 0:512], AX.X)
                k.v("reduce_max", mall[:, qi:qi + 1], mcols[:], AX.X)
            if exactmax:
                emit_cols_to_row(ctx, mall[:], 4, mrowf, 0, f32=True)
                k.t("matmul", ctx.PS[4][0:128, 0:512], ctx.onesf1[:], mrowf[0:1, :],
                    start=True, stop=True)
                k.v("tensor_copy", mbcf[:], ctx.PS[4][0:128, 0:512])
            else:
                emit_cols_to_row(ctx, mall[:], 4, mrow, 0)
                k.t("matmul", ctx.PS[4][0:128, 0:512], ctx.onesb1[:], mrow,
                    start=True, stop=True)
                k.v("tensor_copy", mbc[:], ctx.PS[4][0:128, 0:512])
            first = True
            if exactmax:
                sts = [st, st]
                pts = [pt, pt]
            else:
                rc_pt = ctx.scr("at_ropec", (128, 2 * TT), BF16)[:, 0:512]
                sts = [st, mbcf]
                pts = [pt, rc_pt]
            for kc in range(8):
                for sub in range(4):
                    klhs = kbh[0:64, kc * 512 + sub * 128:kc * 512 + (sub + 1) * 128]
                    k.t("matmul", ctx.PS[sub][0:128, 0:512], klhs, qall,
                        start=True, stop=True)
                for sub in range(4):
                    ti = kc * 4 + sub
                    sb_ = sts[ti % 2]
                    pb_ = pts[ti % 2]
                    k.v("tensor_sub", sb_[:], ctx.PS[sub][0:128, 0:512],
                        mbcf[:] if exactmax else mbc[:])
                    for (tiles, sel) in tsets:
                        k.v("scalar_tensor_tensor", sb_[:], tiles[sub][:, 0:512],
                            sel[:, kc:kc + 1], sb_[:], ALU.mult, ALU.add)
                    k.s("activation", pb_[:], sb_[:], AF.Exp, bias=offc[:, kc:kc + 1])
                    k.t("matmul", ctx.PS[4][0:1, 0:512], onesbc[:], pb_[:],
                        start=first, stop=(kc == 7 and sub == 3))
                    vb = vbh01[ti // 16]
                    k.t("matmul", ctx.PS[5][0:64, 0:512], vb[:, (ti % 16) * 64:(ti % 16 + 1) * 64],
                        pb_[:], start=first, stop=(kc == 7 and sub == 3))
                    first = False
            if ghost:
                k.v("tensor_single_scalar", lrow[0:1, :], ctx.PS[4][0:1, 0:512],
                    ghostc[0:1, h:h + 1], ALU.add)
                k.v("reciprocal", lrow[0:1, :], lrow[0:1, :])
            else:
                k.v("reciprocal", lrow[0:1, :], ctx.PS[4][0:1, 0:512])
            k.t("matmul", ctx.PS[2][0:64, 0:512], ctx.onesf1[0:1, 0:64], lrow[0:1, :],
                start=True, stop=True)
            k.v("tensor_copy", rlb[:], ctx.PS[2][0:64, 0:512])
            k.v("tensor_mul", oall[h // 2][r0:r0 + 64, 0:TT], ctx.PS[5][0:64, 0:512], rlb[:])
        ctx.wp.load(ctx, d["oT"].ap(), 0, D)
        hv = ctx.scr("at_st", (128, 512))
        for mp in range(DP):
            ps = ctx.PS[2 + mp % 2]
            emit_mm_chain(ctx, ps[:, 0:TT], ctx.wp, mp * 128, oall, slice(0, TT))
            k.v("scalar_tensor_tensor", hv[:], ps[:, 0:TT], 1.0,
                x_tiles[mp][:, b * TT:(b + 1) * TT], ALU.mult, ALU.add)
            k.v("tensor_scalar_add", x_tiles[mp][:, b * TT:(b + 1) * TT], hv[:],
                obc[:, mp:mp + 1])


def decl_attn_softmax(nc, tag, ntsets, rope=False, ghost=False):
    d = {}
    for nm in ("qT", "kT", "vT", "oT"):
        d[nm] = nc.dram_tensor(f"{tag}_{nm}", [D, D], F32, kind="ExternalInput")
    for nm in ("qbc", "kbc", "obc"):
        d[nm] = nc.dram_tensor(f"{tag}_{nm}", [128, 8], F32, kind="ExternalInput")
    d["bvrow"] = nc.dram_tensor(f"{tag}_bvrow", [1, 1024], F32, kind="ExternalInput")
    d["offc"] = nc.dram_tensor(f"{tag}_offc", [128, 8], F32, kind="ExternalInput")
    for i in range(ntsets):
        d[f"tset{i}_tiles"] = nc.dram_tensor(f"{tag}_ts{i}t", [512, 512], F32, kind="ExternalInput")
        d[f"tset{i}_sel"] = nc.dram_tensor(f"{tag}_ts{i}s", [128, 8], F32, kind="ExternalInput")
        d[f"tset{i}_tilesQ"] = nc.dram_tensor(f"{tag}_ts{i}tq", [512, 512], F32, kind="ExternalInput")
    if ghost:
        d["ghostc"] = nc.dram_tensor(f"{tag}_ghostc", [1, 16], F32, kind="ExternalInput")
    if rope:
        d["ropeRT"] = nc.dram_tensor(f"{tag}_ropeRT", [128, 128], F32, kind="ExternalInput")
        d["ropecos"] = nc.dram_tensor(f"{tag}_ropecos", [128, T], F32, kind="ExternalInput")
        d["ropesin"] = nc.dram_tensor(f"{tag}_ropesin", [128, T], F32, kind="ExternalInput")
    return d


def l0a_tq(tiles):
    full = tiles
    return np.ascontiguousarray(full.T)


def mask_causal(core):
    r = core
    offc = np.zeros((128, 8), np.float32)
    for kc in range(8):
        if kc > r:
            offc[:, kc] = -1e30
    tiles = np.zeros((512, 512), np.float32)
    for sub in range(4):
        for krow in range(128):
            kl = sub * 128 + krow
            tiles[sub * 128 + krow, :] = np.where(np.arange(512) >= kl, 0.0, -1e30)
    sel = np.zeros((128, 8), np.float32)
    sel[:, r] = 1.0
    return offc, tiles, sel


def mask_l1a(core):
    offc = np.zeros((128, 8), np.float32)
    tiles = np.zeros((512, 512), np.float32)
    for sub in range(4):
        for krow in range(128):
            kl = sub * 128 + krow
            tiles[sub * 128 + krow, :] = np.where(np.arange(512) >= kl, 0.0, -1e30)
    sel = np.ones((128, 8), np.float32)
    return offc, tiles, sel


def mask_l0a(core):
    r = core
    qg = 512 * r + np.arange(512)[None, :]
    offc = np.full((128, 8), -1e30, np.float32)
    tsets = []
    for which in range(3):
        kcs = {0: r, 1: r - 1, 2: 0}[which]
        sel = np.zeros((128, 8), np.float32)
        tiles = np.full((512, 512), 0.0, np.float32)
        if kcs >= 0 and not (which == 2 and (r == 0 or r == 1)):
            kg = 512 * kcs + np.arange(512)[:, None]
            diff = qg - kg
            win = (diff >= 0) & (diff < 256)
            col0 = (kg == 0)
            keep = win | col0
            tiles = np.where(keep, 0.0, -1e30).astype(np.float32)
            sel[:, kcs] = 1.0
            offc[:, kcs] = 0.0
        tsets.append((tiles, sel))
    return offc, tsets


def stage_l2_attn_b(inp, core):
    W = inp["L2_attn_b_Wqkv_w"]; bqkv = inp["L2_attn_b_Wqkv_b"]
    offc, tiles, sel = mask_causal(core)
    return {
        "l2b_qT": np.ascontiguousarray(W[0:1024].T), "l2b_kT": np.ascontiguousarray(W[1024:2048].T),
        "l2b_vT": np.ascontiguousarray(W[2048:3072].T),
        "l2b_oT": np.ascontiguousarray(inp["L2_attn_b_Wo_w"].T),
        "l2b_qbc": cols(bqkv[0:1024]), "l2b_kbc": cols(bqkv[1024:2048]),
        "l2b_obc": cols(inp["L2_attn_b_Wo_b"]),
        "l2b_bvrow": np.ascontiguousarray(bqkv[2048:3072].reshape(1, 1024)).astype(np.float32),
        "l2b_offc": offc, "l2b_ts0t": tiles, "l2b_ts0s": sel,
        "l2b_ts0tq": np.ascontiguousarray(tiles.T.reshape(4, 128, 512).reshape(512, 512)) if False else l0a_tq(tiles),
        "l2b_ghostc": np.ascontiguousarray(inp["L2_attn_b_ghost"].reshape(1, 16)),
    }


def stage_l3_attn_a(inp, core):
    qkv = inp["L3_attn_a_qkv_w"].reshape(1024, 16 * 192)
    qkvb = inp["L3_attn_a_qkv_b"].reshape(16 * 192)
    idx = np.arange(16 * 192).reshape(16, 192)
    qT = qkv[:, idx[:, 0:64].reshape(-1)]
    kT = qkv[:, idx[:, 64:128].reshape(-1)]
    vT = qkv[:, idx[:, 128:192].reshape(-1)]
    qb = qkvb[idx[:, 0:64].reshape(-1)]
    kb = qkvb[idx[:, 64:128].reshape(-1)]
    vb = qkvb[idx[:, 128:192].reshape(-1)]
    offc, tiles, sel = mask_causal(core)
    return {
        "l3a_qT": np.ascontiguousarray(qT), "l3a_kT": np.ascontiguousarray(kT),
        "l3a_vT": np.ascontiguousarray(vT),
        "l3a_oT": np.ascontiguousarray(inp["L3_attn_a_out_w"].reshape(1024, 1024)),
        "l3a_qbc": cols(qb), "l3a_kbc": cols(kb), "l3a_obc": cols(inp["L3_attn_a_out_b"]),
        "l3a_bvrow": np.ascontiguousarray(vb.reshape(1, 1024)),
        "l3a_offc": offc, "l3a_ts0t": tiles, "l3a_ts0s": sel,
        "l3a_ts0tq": l0a_tq(tiles),
    }


def stage_l1_attn_a(inp, core):
    offc, tiles, sel = mask_l1a(core)
    return {
        "l1a_qT": np.ascontiguousarray(inp["L1_attn_a_Wq"].T),
        "l1a_kT": np.ascontiguousarray(inp["L1_attn_a_Wk"].T),
        "l1a_vT": np.ascontiguousarray(inp["L1_attn_a_Wv"].T),
        "l1a_oT": np.ascontiguousarray(inp["L1_attn_a_Wo"].T),
        "l1a_qbc": cols(inp["L1_attn_a_bq"]), "l1a_kbc": cols(inp["L1_attn_a_bk"]),
        "l1a_obc": cols(inp["L1_attn_a_bo"]),
        "l1a_bvrow": np.ascontiguousarray(inp["L1_attn_a_bv"].reshape(1, 1024)),
        "l1a_offc": offc, "l1a_ts0t": tiles, "l1a_ts0s": sel,
        "l1a_ts0tq": l0a_tq(tiles),
    }


def host_rope_tables(core):
    DHEAD = 64
    inv_freq = 1.0 / 10000.0 ** (np.arange(0, DHEAD, 2, dtype=np.float64) / DHEAD)
    pos = np.arange(LSEQ, dtype=np.float64)
    freqs = pos[:, None] * inv_freq[None, :]
    emb = np.concatenate([freqs, freqs], axis=-1)
    cost, sint = np.cos(emb), np.sin(emb)
    sl = slice(core * SC, (core + 1) * SC)
    c = cost[sl].T
    sn = sint[sl].T
    def expand(t):
        t2 = np.concatenate([t, t], axis=0)
        return np.ascontiguousarray(np.tile(t2, (1, B))).astype(np.float32)
    R = np.zeros((64, 64), np.float32)
    R[np.arange(32), np.arange(32) + 32] = -1.0
    R[np.arange(32, 64), np.arange(32)] = 1.0
    RT = np.zeros((128, 128), np.float32)
    RT[0:64, 0:64] = R.T
    RT[64:128, 64:128] = R.T
    return expand(c), expand(sn), RT


def stage_l0_attn_a(inp, core):
    offc, tsets = mask_l0a(core)
    cosx, sinx, RT = host_rope_tables(core)
    m = {
        "l0a_qT": np.ascontiguousarray(inp["L0_attn_a_q_w"].T),
        "l0a_kT": np.ascontiguousarray(inp["L0_attn_a_k_w"].T),
        "l0a_vT": np.ascontiguousarray(inp["L0_attn_a_v_w"].T),
        "l0a_oT": np.ascontiguousarray(inp["L0_attn_a_o_w"].T),
        "l0a_qbc": cols(inp["L0_attn_a_q_b"]), "l0a_kbc": cols(inp["L0_attn_a_k_b"]),
        "l0a_obc": cols(inp["L0_attn_a_o_b"]),
        "l0a_bvrow": np.ascontiguousarray(inp["L0_attn_a_v_b"].reshape(1, 1024)),
        "l0a_offc": offc,
        "l0a_ropecos": cosx, "l0a_ropesin": sinx, "l0a_ropeRT": RT,
    }
    for i, (tiles, sel) in enumerate(tsets):
        m[f"l0a_ts{i}t"] = tiles
        m[f"l0a_ts{i}s"] = sel
        m[f"l0a_ts{i}tq"] = l0a_tq(tiles)
    return m


def emit_l1_attn_b(ctx, x_tiles, d, ad):
    k = ctx.k
    hp = ctx.hp
    xn = [ctx.scr(f"xn{i}", (128, TT), BF16) for i in range(DP)]
    y_sb = [ctx.scr(f"moey{i}", (128, TT)) for i in range(DP)]
    qbc = ctx.scr("at_qbc", (128, DP))
    kbc = ctx.scr("at_kbc", (128, DP))
    bvrow = ctx.scr("at_bvrow", (1, 1024), BF16)
    bvf = ctx.scr("ln_rows", (1, 2 * TT))
    qsc2 = ctx.scr("at_qsc2", (128, TT), BF16)
    vsc = ctx.scr("at_mbcf", (128, 512))[:].bitcast(BF16)
    k.g("dma_start", out=qbc[:], in_=d["qbc"].ap())
    k.g("dma_start", out=kbc[:], in_=d["kbc"].ap())
    k.g("dma_start", out=bvf[0:1, 0:1024], in_=d["bvrow"].ap())
    k.v("tensor_copy", bvrow[0:1, :], bvf[0:1, 0:1024])
    rtp = ctx.pool8k[:, 0:2048]
    rtd = ctx.pool8k[:, 2048:4096]
    dcol = ctx.scr("l1b_dcol", (128, 8))
    dsel = ctx.scr("l1b_dsel", (128, 8))
    for sub in range(4):
        k.g("dma_start", out=rtp[:, sub * 512:(sub + 1) * 512],
            in_=d["rtp"].ap()[sub * 128:(sub + 1) * 128, :])
        k.g("dma_start", out=rtd[:, sub * 512:(sub + 1) * 512],
            in_=d["rtd"].ap()[sub * 128:(sub + 1) * 128, :])
    k.g("dma_start", out=dcol[:], in_=d["dcol"].ap())
    k.g("dma_start", out=dsel[:], in_=d["dsel"].ap())
    wq = ctx.wp
    for b in range(B):
        emit_layernorm(ctx, x_tiles, xn, b * TT)
        wq.load(ctx, d["qT"].ap(), 0, D)
        for mp in range(DP):
            ps = ctx.PS[2 + mp % 2]
            emit_mm_chain(ctx, ps[:, 0:TT], wq, mp * 128, xn, slice(0, TT))
            k.v("tensor_scalar", hp[mp * 4 + b][:, 0:TT], ps[:, 0:TT],
                qbc[:, mp:mp + 1], 1.0 / 32.0, ALU.add, ALU.mult)
        wq.load(ctx, d["kT"].ap(), 0, D)
        for mp in range(DP):
            ps = ctx.PS[2 + mp % 2]
            emit_mm_chain(ctx, ps[:, 0:TT], wq, mp * 128, xn, slice(0, TT))
            k.v("tensor_scalar", qsc2[:, 0:TT], ps[:, 0:TT], kbc[:, mp:mp + 1], 1.0,
                ALU.add, ALU.mult)
            k.y("dma_start", out=ad["k_in"].ap()[b * D + mp * 128:b * D + (mp + 1) * 128, :],
                in_=qsc2[:, 0:TT])
        wq.load(ctx, d["vT"].ap(), 0, D)
        for ti in range(4):
            qsl = slice(ti * 128, (ti + 1) * 128)
            for half in range(2):
                psv = ctx.PS[2 + half]
                for kp in range(DP):
                    k.t("matmul", psv[0:128, 0:512], xn[kp][:, qsl],
                        wq.sb[kp][:, half * 512:(half + 1) * 512],
                        start=(kp == 0), stop=False)
                k.t("matmul", psv[0:128, 0:512], ctx.onesb1[:],
                    bvrow[0:1, half * 512:(half + 1) * 512], start=False, stop=True)
                k.v("tensor_copy", vsc[:, half * 512:(half + 1) * 512], psv[0:128, 0:512])
            k.y("dma_start", out=ad["v_in"].ap()[b * 512 + ti * 128:b * 512 + (ti + 1) * 128, :],
                in_=vsc)
    emit_ag(ctx, ad)
    wtile = ctx.scr("at_st", (128, 512))
    pts = [ctx.hp[40 + i][:, 0:512] for i in range(4)]
    kout_r = ad["k_out"].ap()
    vout_r = ad["v_out"].ap()
    for b in range(B):
        for kc in range(8):
            for kp in range(DP):
                k.y("dma_start", out=xn[kp][:, 0:TT],
                    in_=kout_r[kc * B * D + b * D + kp * 128:kc * B * D + b * D + (kp + 1) * 128, :])
            for sub in range(4):
                k.y("dma_start", out=hp[32 + sub * 2][:, 0:TT],
                    in_=vout_r[kc * T + b * 512 + sub * 128:kc * T + b * 512 + sub * 128 + 128, 0:512])
                k.y("dma_start", out=hp[32 + sub * 2 + 1][:, 0:TT],
                    in_=vout_r[kc * T + b * 512 + sub * 128:kc * T + b * 512 + sub * 128 + 128, 512:1024])
            for sub in range(4):
                ps = ctx.PS[2 + sub % 2]
                for kp in range(DP):
                    k.t("matmul", ps[0:128, 0:512], xn[kp][:, sub * 128:(sub + 1) * 128],
                        hp[kp * 4 + b][:, 0:TT], start=(kp == 0), stop=(kp == DP - 1))
                k.v("tensor_single_scalar", wtile[:], rtp[:, sub * 512:(sub + 1) * 512],
                    dcol[:, kc:kc + 1], ALU.mult)
                k.v("scalar_tensor_tensor", wtile[:], rtd[:, sub * 512:(sub + 1) * 512],
                    dsel[:, kc:kc + 1], wtile[:], ALU.mult, ALU.add)
                k.v("tensor_mul", pts[sub], ps[0:128, 0:512], wtile[:])
            for mp in range(DP):
                ps = ctx.PS[2 + mp % 2]
                for sub in range(4):
                    k.t("matmul", ps[0:128, 0:512],
                        hp[32 + sub * 2 + mp // 4][:, (mp % 4) * 128:(mp % 4 + 1) * 128],
                        pts[sub], start=(sub == 0), stop=(sub == 3))
                if kc == 0:
                    k.v("tensor_copy", y_sb[mp][:, 0:TT], ps[0:128, 0:512])
                else:
                    k.v("tensor_add", y_sb[mp][:, 0:TT], y_sb[mp][:, 0:TT], ps[0:128, 0:512])
        for mp in range(DP):
            k.v("tensor_add", x_tiles[mp][:, b * TT:(b + 1) * TT],
                x_tiles[mp][:, b * TT:(b + 1) * TT], y_sb[mp][:, 0:TT])


def decl_l1_attn_b(nc):
    d = {}
    for nm in ("qT", "kT", "vT"):
        d[nm] = nc.dram_tensor(f"l1b2_{nm}", [D, D], F32, kind="ExternalInput")
    d["qbc"] = nc.dram_tensor("l1b2_qbc", [128, 8], F32, kind="ExternalInput")
    d["kbc"] = nc.dram_tensor("l1b2_kbc", [128, 8], F32, kind="ExternalInput")
    d["bvrow"] = nc.dram_tensor("l1b2_bvrow", [1, 1024], F32, kind="ExternalInput")
    d["rtp"] = nc.dram_tensor("l1b2_rtp", [512, 512], F32, kind="ExternalInput")
    d["rtd"] = nc.dram_tensor("l1b2_rtd", [512, 512], F32, kind="ExternalInput")
    d["dcol"] = nc.dram_tensor("l1b2_dcol", [128, 8], F32, kind="ExternalInput")
    d["dsel"] = nc.dram_tensor("l1b2_dsel", [128, 8], F32, kind="ExternalInput")
    return d


def stage_l1_attn_b(inp, core):
    ALPHA = 0.99
    W = inp["L1_attn_b_W_qkv"]; bq = inp["L1_attn_b_b_qkv"]
    kl = np.arange(512)[:, None]; qc = np.arange(512)[None, :]
    rtp = (ALPHA ** (qc - kl)).astype(np.float32)
    rtd = np.where(qc >= kl, ALPHA ** np.maximum(qc - kl, 0), 0.0).astype(np.float32)
    def stack(m):
        return np.ascontiguousarray(m.reshape(4, 128, 512).reshape(512, 512))
    dcol = np.zeros((128, 8), np.float32)
    dsel = np.zeros((128, 8), np.float32)
    for kc in range(8):
        if kc < core:
            dcol[:, kc] = ALPHA ** (512.0 * (core - kc))
        elif kc == core:
            dsel[:, kc] = 1.0
    return {
        "l1b2_qT": np.ascontiguousarray(W[0:1024].T),
        "l1b2_kT": np.ascontiguousarray(W[1024:2048].T),
        "l1b2_vT": np.ascontiguousarray(W[2048:3072].T),
        "l1b2_qbc": cols(bq[0:1024]), "l1b2_kbc": cols(bq[1024:2048]),
        "l1b2_bvrow": np.ascontiguousarray(bq[2048:3072].reshape(1, 1024)),
        "l1b2_rtp": stack(rtp), "l1b2_rtd": stack(rtd),
        "l1b2_dcol": dcol, "l1b2_dsel": dsel,
    }


def emit_l2_attn_a(ctx, x_tiles, d, ad):
    k = ctx.k
    xn = [ctx.scr(f"xn{i}", (128, TT), BF16) for i in range(DP)]
    hp = ctx.hp
    sc11 = ctx.scr("l2a_sc11", (1, 2))
    k.g("dma_start", out=sc11[0:1, 0:1], in_=d["decay"].ap())
    k.g("dma_start", out=sc11[0:1, 1:2], in_=d["oscale"].ap())
    k.s("activation", sc11[0:1, 0:1], sc11[0:1, 0:1], AF.Sigmoid)
    k.s("activation", sc11[0:1, 0:1], sc11[0:1, 0:1], AF.Ln)
    colsc = ctx.scr("l2a_colsc", (128, 2))
    k.t("matmul", ctx.PS[4][0:128, 0:2], ctx.onesf1[:], sc11[0:1, 0:2], start=True, stop=True)
    k.v("tensor_copy", colsc[:], ctx.PS[4][0:128, 0:2])
    bt = ctx.pool8k[:, 0:2048].bitcast(F32)
    brow = ctx.scr("l2a_brow", (128, 1))
    bsT = ctx.scr("l2f_h0", (128, TT))[:].bitcast(BF16)[0:64, 0:1024]
    k.g("dma_start", out=bsT, in_=d["basisT"].ap())
    qw = {}
    for nm in ("qc", "kc", "vc", "oc"):
        cf = ctx.scr("at_pt", (128, TT), BF16)[0:64, 0:128]
        k.g("dma_start", out=cf, in_=d[nm + "T"].ap())
        for half in range(2):
            k.t("matmul", ctx.PSW[:, half * 512:(half + 1) * 512], cf,
                bsT[0:64, half * 512:(half + 1) * 512], start=True, stop=True)
        k.v("tensor_copy", bt, ctx.PSW[:, 0:1024])
        if nm != "oc":
            k.v("reduce_max", brow[:], bt, AX.X)
            k.v("tensor_scalar_mul", brow[:], brow[:], -1.0)
            k.s("activation", bt, bt, AF.Exp, bias=brow[:])
            k.v("tensor_reduce", brow[:], bt, AX.X, ALU.add)
            k.v("reciprocal", brow[:], brow[:])
            k.v("tensor_single_scalar", bt, bt, brow[:], ALU.mult)
        else:
            k.v("tensor_single_scalar", bt, bt, colsc[:, 1:2], ALU.mult)
        base = {"qc": 0, "kc": 8, "vc": 16, "oc": 24}[nm]
        wtiles = [ctx.hp[base + i][:, 0:128] for i in range(8)]
        btb = ctx.pool8k[:, 2048:3072]
        k.v("tensor_copy", btb, bt)
        psT = ctx.PS[5][:, 0:64].bitcast(BF16)
        for i in range(8):
            k.t("transpose", psT, btb[:, i * 128:(i + 1) * 128], ctx.ident_bf()[:])
            k.v("tensor_copy", wtiles[i], psT[0:128, 0:128])
        qw[nm] = wtiles
    rtp = ctx.pool8k[:, 0:2048]
    rtd = ctx.pool8k[:, 2048:4096]
    dcol = ctx.scr("l1b_dcol", (128, 8))
    dsel = ctx.scr("l1b_dsel", (128, 8))
    etile = ctx.scr("at_st", (128, 512))
    for sub in range(4):
        k.g("dma_start", out=etile[:], in_=d["ep"].ap()[sub * 128:(sub + 1) * 128, :])
        k.v("tensor_single_scalar", etile[:], etile[:], colsc[:, 0:1], ALU.mult)
        k.s("activation", rtp[:, sub * 512:(sub + 1) * 512], etile[:], AF.Exp)
        k.g("dma_start", out=etile[:], in_=d["ed"].ap()[sub * 128:(sub + 1) * 128, :])
        k.v("tensor_single_scalar", etile[:], etile[:], colsc[:, 0:1], ALU.mult)
        k.s("activation", rtd[:, sub * 512:(sub + 1) * 512], etile[:], AF.Exp)
    exq = ctx.scr("l2a_exq", (128, 8))
    k.g("dma_start", out=exq[:], in_=d["expo"].ap())
    k.v("tensor_single_scalar", exq[:], exq[:], colsc[:, 0:1], ALU.mult)
    k.s("activation", dcol[:], exq[:], AF.Exp)
    k.g("dma_start", out=dsel[:], in_=d["dsel"].ap())
    qfts = [ctx.hp[34 + i] for i in range(4)]
    ksc = ctx.scr("at_qsc2", (128, TT), BF16)
    for b in range(B):
        emit_layernorm(ctx, x_tiles, xn, b * TT)
        ps = ctx.PS[2]
        for kp in range(DP):
            k.t("matmul", ps[0:128, 0:512], qw["qc"][kp], xn[kp][:, 0:TT],
                start=(kp == 0), stop=(kp == DP - 1))
        k.v("tensor_copy", qfts[b][:, 0:TT], ps[0:128, 0:512])
        for kp in range(DP):
            k.t("matmul", ps[0:128, 0:512], qw["kc"][kp], xn[kp][:, 0:TT],
                start=(kp == 0), stop=(kp == DP - 1))
        k.v("tensor_copy", ksc[:, 0:TT], ps[0:128, 0:512])
        k.y("dma_start", out=ad["k_in"].ap()[b * 128:(b + 1) * 128, :], in_=ksc[:, 0:TT])
        for ti in range(4):
            qsl = slice(ti * 128, (ti + 1) * 128)
            psv = ctx.PS[3]
            for kp in range(DP):
                k.t("matmul", psv[0:128, 0:128], xn[kp][:, qsl], qw["vc"][kp],
                    start=(kp == 0), stop=(kp == DP - 1))
            k.v("tensor_copy", ksc[:, 0:128], psv[0:128, 0:128])
            k.y("dma_start", out=ad["v_in"].ap()[b * 512 + ti * 128:b * 512 + (ti + 1) * 128, :],
                in_=ksc[:, 0:128])
    emit_ag(ctx, ad)
    kch = ctx.hp[32][:, 0:TT]
    vch = [ctx.hp[33][:, i * 128:(i + 1) * 128] for i in range(4)]
    wtile = ctx.scr("at_st", (128, 512))
    pts = [ctx.hp[38 + i][:, 0:512] for i in range(4)]
    retr = ctx.scr("at_qsc2", (128, TT), BF16)
    obc = None
    for b in range(B):
        first = True
        for kc in range(8):
            k.y("dma_start", out=kch,
                in_=ad["k_out"].ap()[kc * B * 128 + b * 128:kc * B * 128 + (b + 1) * 128, :])
            for sub in range(4):
                k.y("dma_start", out=vch[sub],
                    in_=ad["v_out"].ap()[kc * T + b * 512 + sub * 128:kc * T + b * 512 + (sub + 1) * 128, :])
            for sub in range(4):
                ps = ctx.PS[2 + sub % 2]
                k.t("matmul", ps[0:128, 0:512], ctx.hp[32][:, sub * 128:(sub + 1) * 128],
                    qfts[b][:, 0:TT], start=True, stop=True)
                k.v("tensor_single_scalar", wtile[:], rtp[:, sub * 512:(sub + 1) * 512],
                    dcol[:, kc:kc + 1], ALU.mult)
                k.v("scalar_tensor_tensor", wtile[:], rtd[:, sub * 512:(sub + 1) * 512],
                    dsel[:, kc:kc + 1], wtile[:], ALU.mult, ALU.add)
                k.v("tensor_mul", pts[sub], ps[0:128, 0:512], wtile[:])
            for sub in range(4):
                k.t("matmul", ctx.PS[4][0:128, 0:512], vch[sub], pts[sub],
                    start=first, stop=(kc == 7 and sub == 3))
                first = False
        k.v("tensor_copy", retr[:, 0:TT], ctx.PS[4][0:128, 0:512])
        for mp in range(DP):
            ps = ctx.PS[2 + mp % 2]
            k.t("matmul", ps[0:128, 0:512], qw["oc"][mp], retr[:, 0:TT],
                start=True, stop=True)
            k.v("tensor_add", x_tiles[mp][:, b * TT:(b + 1) * TT],
                x_tiles[mp][:, b * TT:(b + 1) * TT], ps[0:128, 0:512])


def decl_l2_attn_a(nc):
    d = {}
    d["basisT"] = nc.dram_tensor("l2a_basisT", [64, 1024], F32, kind="ExternalInput")
    for nm in ("qcT", "kcT", "vcT", "ocT"):
        d[nm[:-1] + "T"] = nc.dram_tensor(f"l2a_{nm}", [64, 128], F32, kind="ExternalInput")
    d["decay"] = nc.dram_tensor("l2a_decay", [1, 1], F32, kind="ExternalInput")
    d["oscale"] = nc.dram_tensor("l2a_oscale", [1, 1], F32, kind="ExternalInput")
    d["ep"] = nc.dram_tensor("l2a_ep", [512, 512], F32, kind="ExternalInput")
    d["ed"] = nc.dram_tensor("l2a_ed", [512, 512], F32, kind="ExternalInput")
    d["expo"] = nc.dram_tensor("l2a_expo", [128, 8], F32, kind="ExternalInput")
    d["dsel"] = nc.dram_tensor("l2a_dsel", [128, 8], F32, kind="ExternalInput")
    return d


def stage_l2_attn_a(inp, core):
    kl = np.arange(512)[:, None]; qc = np.arange(512)[None, :]
    ep = (qc - kl - 1.0 + 512.0).astype(np.float32)
    ed = np.where(qc > kl, qc - kl - 1.0, 1e30).astype(np.float32)
    def stack(m):
        return np.ascontiguousarray(m.reshape(4, 128, 512).reshape(512, 512))
    expo = np.full((128, 8), 1e30, np.float32)
    dsel = np.zeros((128, 8), np.float32)
    for kc in range(8):
        if kc < core:
            expo[:, kc] = 512.0 * (core - kc) - 512.0
        elif kc == core:
            dsel[:, kc] = 1.0
    return {
        "l2a_basisT": np.ascontiguousarray(inp["L2_attn_a_basis"].T),
        "l2a_qcT": np.ascontiguousarray(inp["L2_attn_a_q_coeffs"].T),
        "l2a_kcT": np.ascontiguousarray(inp["L2_attn_a_k_coeffs"].T),
        "l2a_vcT": np.ascontiguousarray(inp["L2_attn_a_v_coeffs"].T),
        "l2a_ocT": np.ascontiguousarray(inp["L2_attn_a_o_coeffs"].T),
        "l2a_decay": inp["L2_attn_a_decay"].reshape(1, 1).astype(np.float32),
        "l2a_oscale": inp["L2_attn_a_out_scale"].reshape(1, 1).astype(np.float32),
        "l2a_ep": stack(ep), "l2a_ed": stack(ed),
        "l2a_expo": expo, "l2a_dsel": dsel,
    }


def emit_l0_attn_b(ctx, x_tiles, d):
    k = ctx.k
    hp = ctx.hp
    xn = [ctx.scr(f"xn{i}", (128, TT), BF16) for i in range(DP)]
    qbc = ctx.scr("at_qbc", (128, DP))
    kbc = ctx.scr("at_kbc", (128, DP))
    vbc = ctx.scr("l0b_vbc", (128, DP))
    obc = ctx.scr("at_obc", (128, DP))
    bfc = ctx.scr("l0b_bfc", (16, 1))
    k.g("dma_start", out=qbc[:], in_=d["qbc"].ap())
    k.g("dma_start", out=kbc[:], in_=d["kbc"].ap())
    k.g("dma_start", out=vbc[:], in_=d["vbc"].ap())
    k.g("dma_start", out=obc[:], in_=d["obc"].ap())
    k.g("dma_start", out=bfc[0:16, :], in_=d["bfc"].ap())
    selc = ctx.scr("l0b_sel", (16, 8 * 128), BF16)
    bones_sb = [ctx.scr(f"l0b_bon{i}", (128, 16), BF16) for i in range(DP)]
    for dp in range(DP):
        k.g("dma_start", out=selc[0:16, dp * 128:(dp + 1) * 128], in_=d["sel"].ap()[dp * 16:(dp + 1) * 16, :])
        k.g("dma_start", out=bones_sb[dp][:], in_=d["bonesd"].ap()[dp * 128:(dp + 1) * 128, :])
    hval = ctx.scr("ln_sq", (128, TT))
    wq = ctx.wp
    f_sb = ctx.scr("l0b_f", (16, T), BF16)
    def phi_evac(dst_ap, ps, bcol):
        k.v("tensor_scalar_add", hval[:], ps, bcol)
        k.v("tensor_scalar_min", ctx.scr("ln_sub", (128, TT))[:], hval[:], 0.0)
        k.s("activation", ctx.scr("l0_hval", (128, TT))[:], ctx.scr("ln_sub", (128, TT))[:], AF.Exp)
        k.v("scalar_tensor_tensor", dst_ap, hval[:], 0.0, ctx.scr("l0_hval", (128, TT))[:],
            ALU.max, ALU.add)
    ksc = ctx.scr("at_qsc2", (128, TT), BF16)
    for b in range(B):
        emit_layernorm(ctx, x_tiles, xn, b * TT)
        wq.load(ctx, d["qT"].ap(), 0, D)
        for mp in range(DP):
            ps = ctx.PS[2 + mp % 2]
            emit_mm_chain(ctx, ps[:, 0:TT], wq, mp * 128, xn, slice(0, TT))
            phi_evac(hp[mp * 4 + b][:, 0:TT], ps[:, 0:TT], qbc[:, mp:mp + 1])
        wq.load(ctx, d["kT"].ap(), 0, D)
        for mp in range(DP):
            ps = ctx.PS[2 + mp % 2]
            emit_mm_chain(ctx, ps[:, 0:TT], wq, mp * 128, xn, slice(0, TT))
            phi_evac(ksc[:, 0:TT], ps[:, 0:TT], kbc[:, mp:mp + 1])
            k.y("dma_start", out=d["kdram"].ap()[mp * 128:(mp + 1) * 128, b * TT:(b + 1) * TT],
                in_=ksc[:, 0:TT])
        wq.load(ctx, d["vT"].ap(), 0, D)
        for mp in range(DP):
            ps = ctx.PS[2 + mp % 2]
            emit_mm_chain(ctx, ps[:, 0:TT], wq, mp * 128, xn, slice(0, TT))
            k.v("tensor_scalar_add", ksc[:, 0:TT], ps[:, 0:TT], vbc[:, mp:mp + 1])
            k.y("dma_start", out=d["vdram"].ap()[mp * 128:(mp + 1) * 128, b * TT:(b + 1) * TT],
                in_=ksc[:, 0:TT])
        wq.load(ctx, d["fT"].ap(), 0, 16)
        ps = ctx.PS[4]
        for kp in range(DP):
            k.t("matmul", ps[0:16, 0:TT], wq.sb[kp][:, 0:16], xn[kp][:, 0:TT],
                start=(kp == 0), stop=(kp == DP - 1))
        k.s("activation", f_sb[0:16, b * TT:(b + 1) * TT], ps[0:16, 0:TT], AF.Sigmoid,
            bias=bfc[0:16, :])
    y_sb = [ctx.scr(f"moey{i}", (128, TT)) for i in range(DP)]
    den = ctx.scr("moehb", (128, TT))[0:16, :]
    wts = ctx.scr("moeha", (128, TT))[0:16, :]
    aw = ctx.scr("at_pt", (128, TT), BF16)[0:16, :]
    prod = ctx.scr("ln_sub", (128, TT))
    pk = xn
    vv = [hp[32 + i] for i in range(DP)]
    oall = xn
    for t in range(B):
        for mp in range(DP):
            k.g("memset", y_sb[mp][:, 0:TT], 0.0)
        k.g("memset", den[:, 0:TT], 0.0)
        for s in range(t, -1, -1):
            for mp in range(DP):
                k.y("dma_start", out=pk[mp][:, 0:TT],
                    in_=d["kdram"].ap()[mp * 128:(mp + 1) * 128, s * TT:(s + 1) * TT])
                k.y("dma_start", out=vv[mp][:, 0:TT],
                    in_=d["vdram"].ap()[mp * 128:(mp + 1) * 128, s * TT:(s + 1) * TT])
            prodb = ctx.scr("l0b_prodb", (128, TT), BF16)
            for mp in range(DP):
                k.v("tensor_mul", prodb[:, 0:TT], hp[mp * 4 + t][:, 0:TT], pk[mp][:, 0:TT])
                k.t("matmul", ctx.PS[4][0:16, 0:TT], bones_sb[mp][:], prodb[:, 0:TT],
                    start=(mp == 0), stop=(mp == DP - 1))
            if s == t:
                k.g("memset", wts[:, 0:TT], 1.0)
            else:
                k.v("tensor_mul", wts[:, 0:TT], wts[:, 0:TT],
                    f_sb[0:16, (s + 1) * TT:(s + 2) * TT])
            k.v("tensor_mul", aw[:, 0:TT], ctx.PS[4][0:16, 0:TT], wts[:, 0:TT])
            k.v("tensor_add", den[:, 0:TT], den[:, 0:TT], aw[:, 0:TT])
            for mp in range(DP):
                k.t("matmul", ctx.PS[5][0:128, 0:TT], selc[0:16, mp * 128:(mp + 1) * 128],
                    aw[:, 0:TT], start=True, stop=True)
                k.v("tensor_mul", prod[:, 0:TT], ctx.PS[5][0:128, 0:TT], vv[mp][:, 0:TT])
                k.v("tensor_add", y_sb[mp][:, 0:TT], y_sb[mp][:, 0:TT], prod[:, 0:TT])
        k.v("tensor_scalar_add", den[:, 0:TT], den[:, 0:TT], 1e-6)
        k.v("reciprocal", den[:, 0:TT], den[:, 0:TT])
        k.v("tensor_copy", aw[:, 0:TT], den[:, 0:TT])
        for mp in range(DP):
            k.t("matmul", ctx.PS[5][0:128, 0:TT], selc[0:16, mp * 128:(mp + 1) * 128],
                aw[0:16, 0:TT], start=True, stop=True)
            k.v("tensor_mul", oall[mp][:, 0:TT], ctx.PS[5][0:128, 0:TT], y_sb[mp][:, 0:TT])
        wq.load(ctx, d["oT"].ap(), 0, D)
        for mp in range(DP):
            ps = ctx.PS[2 + mp % 2]
            emit_mm_chain(ctx, ps[:, 0:TT], wq, mp * 128, oall, slice(0, TT))
            k.v("tensor_scalar_add", hval[:], ps[:, 0:TT], obc[:, mp:mp + 1])
            k.v("tensor_add", x_tiles[mp][:, t * TT:(t + 1) * TT],
                x_tiles[mp][:, t * TT:(t + 1) * TT], hval[:])


def decl_l0_attn_b(nc):
    d = {}
    for nm in ("qT", "kT", "vT", "oT"):
        d[nm] = nc.dram_tensor(f"l0b_{nm}", [D, D], F32, kind="ExternalInput")
    d["fT"] = nc.dram_tensor("l0b_fT", [D, 16], F32, kind="ExternalInput")
    for nm in ("qbc", "kbc", "vbc", "obc"):
        d[nm] = nc.dram_tensor(f"l0b_{nm}", [128, 8], F32, kind="ExternalInput")
    d["bfc"] = nc.dram_tensor("l0b_bfc", [16, 1], F32, kind="ExternalInput")
    d["sel"] = nc.dram_tensor("l0b_sel", [8 * 16, 128], F32, kind="ExternalInput")
    d["bonesd"] = nc.dram_tensor("l0b_bones", [8 * 128, 16], F32, kind="ExternalInput")
    d["kdram"] = nc.dram_tensor("l0b_kdram", [D, T], BF16, kind="Internal")
    d["vdram"] = nc.dram_tensor("l0b_vdram", [D, T], BF16, kind="Internal")
    return d


def stage_l0_attn_b(inp, core):
    sel = np.zeros((8 * 16, 128), np.float32)
    bones = np.zeros((8 * 128, 16), np.float32)
    for dp in range(8):
        for m in range(128):
            h = 2 * dp + m // 64
            sel[dp * 16 + h, m] = 1.0
        for kk in range(128):
            bones[dp * 128 + kk, 2 * dp + kk // 64] = 1.0
    return {
        "l0b_qT": np.ascontiguousarray(inp["L0_attn_b_Wq"].T),
        "l0b_kT": np.ascontiguousarray(inp["L0_attn_b_Wk"].T),
        "l0b_vT": np.ascontiguousarray(inp["L0_attn_b_Wv"].T),
        "l0b_oT": np.ascontiguousarray(inp["L0_attn_b_Wo"].T),
        "l0b_fT": np.ascontiguousarray(inp["L0_attn_b_Wf"].T),
        "l0b_qbc": cols(inp["L0_attn_b_bq"]), "l0b_kbc": cols(inp["L0_attn_b_bk"]),
        "l0b_vbc": cols(inp["L0_attn_b_bv"]), "l0b_obc": cols(inp["L0_attn_b_bo"]),
        "l0b_bfc": inp["L0_attn_b_bf"].reshape(16, 1).astype(np.float32),
        "l0b_sel": sel, "l0b_bones": bones,
    }


def emit_l3_attn_b(ctx, x_tiles, d):
    k = ctx.k
    hp = ctx.hp
    xn = [ctx.scr(f"xn{i}", (128, TT), BF16) for i in range(DP)]
    CT = 16
    NS = 16
    wq = ctx.wp
    convwc = ctx.scr("mb_convw", (128, 64))
    convbc = ctx.scr("mb_convb", (128, CT))
    bdtc = ctx.scr("mb_bdt", (128, CT))
    dcp = ctx.scr("mb_dc", (128, CT))
    alog = ctx.scr("ln_sq", (128, TT))[:, 0:256]
    A_sb = ctx.scr("mb_A", (128, CT * NS))
    wdtb = ctx.scr("l0b_f", (16, T), BF16)[0:1, 0:2048]
    hmask = ctx.scr("mb_hmask", (128, 8))
    cmask = ctx.scr("mb_cmask", (128, 8))
    k.g("dma_start", out=convwc[:], in_=d["convwc"].ap())
    k.g("dma_start", out=convbc[:], in_=d["convbc"].ap())
    k.g("dma_start", out=bdtc[:], in_=d["bdtc"].ap())
    k.g("dma_start", out=dcp[:], in_=d["dcolc"].ap())
    k.g("dma_start", out=alog, in_=d["alogc"].ap())
    k.g("dma_start", out=hmask[:], in_=d["hmask"].ap())
    k.g("dma_start", out=cmask[:], in_=d["cmask"].ap())
    row2048 = ctx.scr("ln_rows", (1, 2 * TT))
    k.g("dma_start", out=row2048[0:1, 0:1024], in_=d["wdt"].ap()[0:1, 0:1024])
    k.v("tensor_copy", wdtb[0:1, 0:1024], row2048[0:1, 0:1024])
    k.g("dma_start", out=row2048[0:1, 0:1024], in_=d["wdt"].ap()[0:1, 1024:2048])
    k.v("tensor_copy", wdtb[0:1, 1024:2048], row2048[0:1, 0:1024])
    k.s("activation", A_sb[:], alog, AF.Exp)
    k.v("tensor_scalar_mul", A_sb[:], A_sb[:], -1.0)
    xdbl = ctx.scr("at_st", (128, 512))[0:33, :]
    dtraw = ctx.scr("mb_dtraw", (1, TT), BF16)
    sp_a = ctx.scr("ln_sq", (128, TT))
    sp_b = ctx.scr("ln_sub", (128, TT))
    xcpad = ctx.pool8k[:, 1024:1540]
    hblk = ctx.scr("mb_hblk", (128, 24), BF16)
    gt = ctx.pool8k[:, 0:512]
    dtt = ctx.pool8k[:, 512:1024]
    xct = ctx.pool8k[:, 2048:2560]
    dat = ctx.scr("l0_hval", (128, TT))
    ut = ctx.scr("l0_arg", (128, TT))
    ht = ctx.scr("moeha", (128, TT))
    hend = ctx.scr("mb_hend", (128, 2 * NS))
    hpeall = ctx.scr("ln_sub", (128, TT))[:, 0:256]
    hin = ctx.scr("mb_hin", (128, NS))[:]
    Sst = ctx.scr("mb_S", (128, NS))[:]
    yac = ctx.scr("moehb", (128, TT))
    brow = ctx.scr("mb_brow", (1, TT))
    for b in range(B):
        emit_layernorm(ctx, x_tiles, xn, b * TT)
        for c in range(4):
            wq.load(ctx, d["winT"].ap(), c * 1024, 1024)
            for ml in range(8):
                mp = c * 8 + ml
                ps = ctx.PS[2 + mp % 2]
                emit_mm_chain(ctx, ps[:, 0:TT], wq, ml * 128, xn, slice(0, TT))
                k.v("tensor_copy", xct[:, 0:TT], ps[:, 0:TT])
                tgt = d["xzpre"] if mp < 16 else d["zdram"]
                r0 = (mp % 16) * 128
                k.y("dma_start", out=tgt.ap()[r0:r0 + 128, b * TT:(b + 1) * TT],
                    in_=xct[:, 0:TT])
    for b in range(B):
        k.y("dma_start", out=d["halo_in"].ap()[:, b * 3:(b + 1) * 3],
            in_=d["xzpre"].ap()[:, b * TT + 509:b * TT + 512])
    k.g("collective_compute", "AllGather", ALU.bypass,
        replica_groups=[list(range(NCORES))],
        ins=[d["halo_in"].ap().opt()], outs=[d["halo_out"].ap().opt()])
    hout_r = d["halo_out"].ap().rearrange("(j z p) c -> z p j c", j=NCORES, z=CT)
    for b in range(B):
        for ct in range(CT):
            k.g("memset", xcpad[:, 0:4], 0.0)
            k.y("dma_start", out=hblk[:].rearrange("p (j c) -> p j c", j=8),
                in_=hout_r[ct][:, :, b * 3:(b + 1) * 3])
            for j in range(8):
                k.v("scalar_tensor_tensor", xcpad[:, 0:3], hblk[:, j * 3:(j + 1) * 3],
                    hmask[:, j:j + 1], xcpad[:, 0:3], ALU.mult, ALU.add)
            k.y("dma_start", out=xcpad[:, 3:515],
                in_=d["xzpre"].ap()[ct * 128:(ct + 1) * 128, b * TT:(b + 1) * TT])
            k.v("tensor_single_scalar", sp_a[:, 0:TT], xcpad[:, 0:512], convwc[:, ct * 4:ct * 4 + 1], ALU.mult)
            for tap in range(1, 4):
                k.v("scalar_tensor_tensor", sp_a[:, 0:TT], xcpad[:, tap:tap + 512],
                    convwc[:, ct * 4 + tap:ct * 4 + tap + 1], sp_a[:, 0:TT], ALU.mult, ALU.add)
            k.s("activation", xct[:, 0:TT], sp_a[:, 0:TT], AF.Silu, bias=convbc[:, ct:ct + 1])
            k.y("dma_start", out=d["xcp"].ap()[ct * 128:(ct + 1) * 128, b * TT:(b + 1) * TT],
                in_=xct[:, 0:TT])
        for g2 in range(2):
            wq.load(ctx, d["wxT"].ap(), 0, 33, r0=g2 * 1024)
            for i in range(8):
                k.y("dma_start", out=hp[i][:, 0:TT],
                    in_=d["xcp"].ap()[g2 * 1024 + i * 128:g2 * 1024 + (i + 1) * 128,
                                      b * TT:(b + 1) * TT])
            for i in range(8):
                k.t("matmul", ctx.PS[4][0:33, 0:TT], wq.sb[i][:, 0:33], hp[i][:, 0:TT],
                    start=(g2 == 0 and i == 0), stop=(g2 == 1 and i == 7))
        k.v("tensor_copy", xdbl[0:33, 0:TT], ctx.PS[4][0:33, 0:TT])
        k.y("dma_start", out=d["bcdram"].ap()[:, b * TT:(b + 1) * TT], in_=xdbl[1:33, 0:TT])
        k.v("tensor_copy", dtraw[0:1, 0:TT], xdbl[0:1, 0:TT])
        for ct in range(CT):
            k.t("matmul", ctx.PS[4][0:128, 0:TT], wdtb[0:1, ct * 128:(ct + 1) * 128],
                dtraw[0:1, 0:TT], start=True, stop=True)
            k.v("tensor_scalar_add", sp_a[:, 0:TT], ctx.PS[4][0:128, 0:TT], bdtc[:, ct:ct + 1])
            k.s("activation", sp_b[:, 0:TT], sp_a[:, 0:TT], AF.Abs)
            k.s("activation", sp_b[:, 0:TT], sp_b[:, 0:TT], AF.Exp, scale=-1.0)
            k.s("activation", sp_b[:, 0:TT], sp_b[:, 0:TT], AF.Ln, bias=ctx.k.fconst(1.0))
            k.v("tensor_scalar_max", sp_a[:, 0:TT], sp_a[:, 0:TT], 0.0)
            k.v("tensor_add", sp_a[:, 0:TT], sp_a[:, 0:TT], sp_b[:, 0:TT])
            k.g("dma_start", out=d["dtdram"].ap()[ct * 128:(ct + 1) * 128, b * TT:(b + 1) * TT],
                in_=sp_a[:, 0:TT])
    emit_l3_attn_b_tail(ctx, x_tiles, d)


def emit_l3_attn_b_tail(ctx, x_tiles, d):
    k = ctx.k
    hp = ctx.hp
    CT = 16
    NS = 16
    wq = ctx.wp
    A_sb = ctx.scr("mb_A", (128, CT * NS))
    dcp = ctx.scr("mb_dc", (128, CT))
    cmask = ctx.scr("mb_cmask", (128, 8))
    sp_a = ctx.scr("ln_sq", (128, TT))
    gt = ctx.pool8k[:, 0:512]
    dtt = ctx.pool8k[:, 512:1024]
    xct = ctx.pool8k[:, 2048:2560]
    dat = ctx.scr("l0_hval", (128, TT))
    ut = ctx.scr("l0_arg", (128, TT))
    ht = ctx.scr("moeha", (128, TT))
    hend = ctx.scr("mb_hend", (128, 2 * NS))
    pe = ctx.scr("mb_pe", (128, NS))
    hpeall = ctx.scr("ln_sub", (128, TT))[:, 0:256]
    hin = ctx.scr("mb_hin", (128, NS))[:]
    Sst = ctx.scr("mb_S", (128, NS))[:]
    yac = ctx.scr("moehb", (128, TT))
    brow = ctx.scr("mb_brow", (1, TT))
    for b in range(B):
        for ct in range(CT):
            k.y("dma_start", out=dtt[:, 0:TT],
                in_=d["dtdram"].ap()[ct * 128:(ct + 1) * 128, b * TT:(b + 1) * TT])
            k.y("dma_start", out=xct[:, 0:TT],
                in_=d["xcp"].ap()[ct * 128:(ct + 1) * 128, b * TT:(b + 1) * TT])
            k.v("tensor_mul", gt[:, 0:TT], dtt[:, 0:TT], xct[:, 0:TT])
            k.v("tensor_reduce", hend[:, 0:1], dtt[:, 0:TT], AX.X, ALU.add)
            k.v("tensor_single_scalar", pe[:], A_sb[:, ct * NS:(ct + 1) * NS],
                hend[:, 0:1], ALU.mult)
            k.s("activation", pe[:], pe[:], AF.Exp)
            for n in range(NS):
                k.y("dma_start", out=brow[0:1, 0:TT],
                    in_=d["bcdram"].ap()[n:n + 1, b * TT:(b + 1) * TT])
                k.t("matmul", ctx.PS[4][0:128, 0:TT], ctx.onesf1[:], brow[0:1, 0:TT],
                    start=True, stop=True)
                k.v("tensor_single_scalar", dat[:, 0:TT], dtt[:, 0:TT],
                    A_sb[:, ct * NS + n:ct * NS + n + 1], ALU.mult)
                k.s("activation", dat[:, 0:TT], dat[:, 0:TT], AF.Exp)
                k.v("tensor_mul", ut[:, 0:TT], gt[:, 0:TT], ctx.PS[4][0:128, 0:TT])
                k.v("tensor_tensor_scan", ht[:, 0:TT], dat[:, 0:TT], ut[:, 0:TT], 0.0,
                    ALU.mult, ALU.add)
                k.v("tensor_copy", hend[:, NS + n:NS + n + 1], ht[:, TT - 1:TT])
            k.v("tensor_copy", hend[:, 0:NS], pe[:])
            k.y("dma_start", out=d["hpe_in"].ap()[b * 2048 + ct * 128:b * 2048 + (ct + 1) * 128, :],
                in_=hend[:, 0:2 * NS])
    k.g("collective_compute", "AllGather", ALU.bypass,
        replica_groups=[list(range(NCORES))],
        ins=[d["hpe_in"].ap().opt()], outs=[d["hpe_out"].ap().opt()])
    hpe_r = d["hpe_out"].ap().rearrange("(j z p) c -> z p j c", j=NCORES, z=B * CT)
    ydram = d["ydram"]
    for b in range(B):
        for n in range(NS):
            k.y("dma_start", out=brow[0:1, 0:TT],
                in_=d["bcdram"].ap()[n:n + 1, b * TT:(b + 1) * TT])
            k.t("matmul", ctx.PS[4][0:128, 0:TT], ctx.onesf1[:], brow[0:1, 0:TT],
                start=True, stop=True)
            k.v("tensor_copy", hp[n][:, 0:TT], ctx.PS[4][0:128, 0:TT])
            k.y("dma_start", out=brow[0:1, 0:TT],
                in_=d["bcdram"].ap()[NS + n:NS + n + 1, b * TT:(b + 1) * TT])
            k.t("matmul", ctx.PS[4][0:128, 0:TT], ctx.onesf1[:], brow[0:1, 0:TT],
                start=True, stop=True)
            k.v("tensor_copy", hp[NS + n][:, 0:TT], ctx.PS[4][0:128, 0:TT])
        for ct in range(CT):
            k.y("dma_start", out=hpeall.rearrange("p (j c) -> p j c", j=8),
                in_=hpe_r[b * CT + ct])
            k.g("memset", hin, 0.0)
            k.g("memset", Sst, 0.0)
            for j in range(7):
                k.v("tensor_mul", Sst, Sst, hpeall[:, j * 32:j * 32 + 16])
                k.v("tensor_add", Sst, Sst, hpeall[:, j * 32 + 16:j * 32 + 32])
                k.v("scalar_tensor_tensor", hin, Sst, cmask[:, j:j + 1], hin,
                    ALU.mult, ALU.add)
            k.y("dma_start", out=dtt[:, 0:TT],
                in_=d["dtdram"].ap()[ct * 128:(ct + 1) * 128, b * TT:(b + 1) * TT])
            k.y("dma_start", out=xct[:, 0:TT],
                in_=d["xcp"].ap()[ct * 128:(ct + 1) * 128, b * TT:(b + 1) * TT])
            k.v("tensor_mul", gt[:, 0:TT], dtt[:, 0:TT], xct[:, 0:TT])
            for n in range(NS):
                k.v("tensor_single_scalar", dat[:, 0:TT], dtt[:, 0:TT],
                    A_sb[:, ct * NS + n:ct * NS + n + 1], ALU.mult)
                k.s("activation", dat[:, 0:TT], dat[:, 0:TT], AF.Exp)
                k.v("tensor_mul", ut[:, 0:TT], gt[:, 0:TT], hp[n][:, 0:TT])
                k.v("tensor_tensor_scan", ht[:, 0:TT], dat[:, 0:TT], ut[:, 0:TT],
                    hin[:, n:n + 1], ALU.mult, ALU.add)
                if n == 0:
                    k.v("tensor_mul", yac[:, 0:TT], ht[:, 0:TT], hp[NS][:, 0:TT])
                else:
                    k.v("tensor_mul", ut[:, 0:TT], ht[:, 0:TT], hp[NS + n][:, 0:TT])
                    k.v("tensor_add", yac[:, 0:TT], yac[:, 0:TT], ut[:, 0:TT])
            k.v("scalar_tensor_tensor", yac[:, 0:TT], xct[:, 0:TT], dcp[:, ct:ct + 1],
                yac[:, 0:TT], ALU.mult, ALU.add)
            k.y("dma_start", out=dtt[:, 0:TT],
                in_=d["zdram"].ap()[ct * 128:(ct + 1) * 128, b * TT:(b + 1) * TT])
            k.s("activation", sp_a[:, 0:TT], dtt[:, 0:TT], AF.Silu)
            k.v("tensor_mul", xct[:, 0:TT], yac[:, 0:TT], sp_a[:, 0:TT])
            k.y("dma_start", out=ydram.ap()[ct * 128:(ct + 1) * 128, b * TT:(b + 1) * TT],
                in_=xct[:, 0:TT])
        y2 = [ctx.scr(f"moey{i}", (128, TT)) for i in range(DP)]
        for g2 in range(2):
            wq.load(ctx, d["woutT"].ap(), 0, D, r0=g2 * 1024)
            for i in range(8):
                k.y("dma_start", out=hp[i][:, 0:TT],
                    in_=ydram.ap()[g2 * 1024 + i * 128:g2 * 1024 + (i + 1) * 128,
                                   b * TT:(b + 1) * TT])
            for mp in range(DP):
                ps = ctx.PS[2 + mp % 2]
                emit_mm_chain(ctx, ps[:, 0:TT], wq, mp * 128, hp[0:8], slice(0, TT))
                if g2 == 0:
                    k.v("tensor_copy", y2[mp][:, 0:TT], ps[:, 0:TT])
                else:
                    k.v("tensor_add", y2[mp][:, 0:TT], y2[mp][:, 0:TT], ps[:, 0:TT])
        for mp in range(DP):
            k.v("tensor_add", x_tiles[mp][:, b * TT:(b + 1) * TT],
                x_tiles[mp][:, b * TT:(b + 1) * TT], y2[mp][:, 0:TT])


def decl_l3_attn_b(nc):
    d = {}
    d["winT"] = nc.dram_tensor("mb_winT", [D, 4096], F32, kind="ExternalInput")
    d["convwc"] = nc.dram_tensor("mb_convwc", [128, 64], F32, kind="ExternalInput")
    d["convbc"] = nc.dram_tensor("mb_convbc", [128, 16], F32, kind="ExternalInput")
    d["wxT"] = nc.dram_tensor("mb_wxT", [2048, 33], F32, kind="ExternalInput")
    d["wdt"] = nc.dram_tensor("mb_wdt", [1, 2048], F32, kind="ExternalInput")
    d["bdtc"] = nc.dram_tensor("mb_bdtc", [128, 16], F32, kind="ExternalInput")
    d["alogc"] = nc.dram_tensor("mb_alogc", [128, 256], F32, kind="ExternalInput")
    d["dcolc"] = nc.dram_tensor("mb_dcolc", [128, 16], F32, kind="ExternalInput")
    d["woutT"] = nc.dram_tensor("mb_woutT", [2048, D], F32, kind="ExternalInput")
    d["hmask"] = nc.dram_tensor("mb_hmask", [128, 8], F32, kind="ExternalInput")
    d["cmask"] = nc.dram_tensor("mb_cmask", [128, 8], F32, kind="ExternalInput")
    d["xzpre"] = nc.dram_tensor("mb_xzpre", [2048, T], BF16, kind="Internal")
    d["zdram"] = nc.dram_tensor("mb_zdram", [2048, T], BF16, kind="Internal")
    d["xcp"] = nc.dram_tensor("mb_xcp", [2048, T], BF16, kind="Internal")
    d["dtdram"] = nc.dram_tensor("mb_dtdram", [2048, T], BF16, kind="Internal")
    d["bcdram"] = nc.dram_tensor("mb_bcdram", [32, T], F32, kind="Internal")
    d["ydram"] = nc.dram_tensor("mb_ydram", [2048, T], BF16, kind="Internal")
    d["halo_in"] = nc.dram_tensor("mb_halo_in", [2048, 12], BF16, kind="Internal")
    d["halo_out"] = nc.dram_tensor("mb_halo_out", [NCORES * 2048, 12], BF16, kind="Internal")
    d["hpe_in"] = nc.dram_tensor("mb_hpe_in", [8192, 32], F32, kind="Internal")
    d["hpe_out"] = nc.dram_tensor("mb_hpe_out", [NCORES * 8192, 32], F32, kind="Internal")
    return d


def stage_l3_attn_b(inp, core):
    convw = inp["L3_attn_b_conv_w"].reshape(2048, 4)
    convwc = np.zeros((128, 64), np.float32)
    for ct in range(16):
        for tap in range(4):
            convwc[:, ct * 4 + tap] = convw[ct * 128:(ct + 1) * 128, tap]
    alogc = np.zeros((128, 256), np.float32)
    al = inp["L3_attn_b_A_log"]
    for ct in range(16):
        alogc[:, ct * 16:(ct + 1) * 16] = al[ct * 128:(ct + 1) * 128, :]
    hmask = np.zeros((128, 8), np.float32)
    cmask = np.zeros((128, 8), np.float32)
    if core >= 1:
        hmask[:, core - 1] = 1.0
        cmask[:, core - 1] = 1.0
    return {
        "mb_winT": inp["L3_attn_b_W_in"],
        "mb_convwc": convwc,
        "mb_convbc": cols(inp["L3_attn_b_conv_b"]),
        "mb_wxT": inp["L3_attn_b_W_x"],
        "mb_wdt": inp["L3_attn_b_W_dt"].reshape(1, 2048),
        "mb_bdtc": cols(inp["L3_attn_b_b_dt"]),
        "mb_alogc": alogc,
        "mb_dcolc": cols(inp["L3_attn_b_D_param"]),
        "mb_woutT": inp["L3_attn_b_W_out"],
        "mb_hmask": hmask, "mb_cmask": cmask,
    }


FP4_MIDS = [0.25, 0.75, 1.25, 1.75, 2.5, 3.5, 5.0]
FP4_WTS = [0.5, 0.5, 0.5, 0.5, 1.0, 1.0, 2.0]


def emit_fp4_quant_free(ctx, w_ap, absw, esc_full, out_ap, cw=512):
    k = ctx.k
    nb = cw // 16
    scl = ctx.scr("fp4_scl", (128, 32))
    g = ctx.scr("moeha", (128, TT))
    acc = ctx.scr("moehb", (128, TT))
    sgn = ctx.scr("ln_sub", (128, TT))
    k.s("activation", absw[:, 0:cw], w_ap, AF.Abs)
    k.v("tensor_reduce", scl[:, 0:nb], absw[:, 0:cw].rearrange("p (b s) -> p b s", b=nb),
        AX.X, ALU.max)
    k.v("tensor_scalar_mul", scl[:, 0:nb], scl[:, 0:nb], 1.0 / 6.0)
    k.v("tensor_scalar_max", scl[:, 0:nb], scl[:, 0:nb], 1e-12)
    k.v("tensor_copy", esc_full[:, 0:cw].rearrange("p (b s) -> p b s", b=nb),
        scl[:, 0:nb].unsqueeze(2).broadcast_to((128, nb, 16)))
    k.s("sign", sgn[:, 0:cw], w_ap)
    for i in range(7):
        k.v("scalar_tensor_tensor", g[:, 0:cw], esc_full[:, 0:cw], FP4_MIDS[i],
            absw[:, 0:cw], ALU.mult, ALU.is_lt)
        if i == 0:
            k.v("tensor_scalar_mul", acc[:, 0:cw], g[:, 0:cw], FP4_WTS[0])
        else:
            k.v("scalar_tensor_tensor", acc[:, 0:cw], g[:, 0:cw], FP4_WTS[i],
                acc[:, 0:cw], ALU.mult, ALU.add)
    k.v("tensor_mul", sgn[:, 0:cw], sgn[:, 0:cw], esc_full[:, 0:cw])
    k.v("tensor_mul", out_ap, acc[:, 0:cw], sgn[:, 0:cw])


def emit_l2_ffn_prep(ctx, d):
    k = ctx.k
    wt = ctx.scr("at_st", (128, 512))
    absw = ctx.scr("l0_hval", (128, TT))
    esc = ctx.scr("l0_arg", (128, TT))
    qb = ctx.scr("at_pt", (128, TT), BF16)
    for (src, dst, nrows) in (("wg", "wgq", 2816), ("wu", "wuq", 2816), ("wd", "wdq", 1024)):
        ncols = 1024 if nrows == 2816 else 2816
        for r0 in range(0, nrows, 128):
            for c0 in range(0, ncols, 512):
                cw = min(512, ncols - c0)
                k.y("dma_start", out=wt[:, 0:cw], in_=d[src].ap()[r0:r0 + 128, c0:c0 + cw])
                emit_fp4_quant_free(ctx, wt[:, 0:cw], absw, esc, qb[:, 0:cw], cw)
                k.y("dma_start", out=d[dst].ap()[r0:r0 + 128, c0:c0 + cw], in_=qb[:, 0:cw])


def emit_l2_ffn(ctx, x_tiles, d):
    k = ctx.k
    hp = ctx.hp
    I2 = 2816
    IMP = I2 // 128
    xn = [ctx.scr(f"xn{i}", (128, TT), BF16) for i in range(DP)]
    xnf = [ctx.scr(f"moey{i}", (128, TT)) for i in range(DP)]
    absw = ctx.scr("l0_hval", (128, TT))
    esc = ctx.scr("l0_arg", (128, TT))
    hf = ctx.scr("ln_sq", (128, TT))
    hb = ctx.scr("ln_sub", (128, TT))
    E8 = ctx.scr("l2f_E8", (8, 128), BF16)
    k.g("dma_start", out=E8[0:8, :], in_=d["e8"].ap())
    sclp = ctx.scr("fp4_sclp", (128, 128), BF16)
    sfm = ctx.scr("l2f_sfm", (8, TT), BF16)
    psT = None

    def quant_fm(src_tiles, dst_tiles, ndp):
        for dp in range(ndp):
            for sub in range(4):
                pst = ctx.PS[5][:, 0:128]
                k.t("transpose", pst, ctx.scr("l0_arg", (128, TT))[:, sub * 128:(sub + 1) * 128],
                    ctx.ident_f32()[:])
                k.v("tensor_reduce", ctx.scr("l2f_stm", (128, 8))[:],
                    pst.rearrange("p (b s) -> p b s", b=8), AX.X, ALU.max)
                k.v("tensor_scalar_mul", ctx.scr("l2f_stm", (128, 8))[:],
                    ctx.scr("l2f_stm", (128, 8))[:], 1.0 / 6.0)
                k.v("tensor_scalar_max", ctx.scr("l2f_stm", (128, 8))[:],
                    ctx.scr("l2f_stm", (128, 8))[:], 1e-12)
                k.v("tensor_copy", sclp[:, 0:8], ctx.scr("l2f_stm", (128, 8))[:])
                psb = ctx.PS[5][:, 0:64].bitcast(BF16)
                k.t("transpose", psb, sclp[:], ctx.ident_bf()[:])
                k.v("tensor_copy", sfm[0:8, sub * 128:(sub + 1) * 128], psb[0:8, 0:128])
            k.t("matmul", ctx.PS[4][0:128, 0:TT], E8[0:8, :], sfm[0:8, 0:TT],
                start=True, stop=True)
            sgn = ctx.scr("ln_sub", (128, TT))
            g = ctx.scr("moeha", (128, TT))
            acc = ctx.scr("moehb", (128, TT))
            k.s("sign", sgn[:, 0:TT], src_tiles[dp][:, 0:TT])
            for i in range(7):
                k.v("scalar_tensor_tensor", g[:, 0:TT], ctx.PS[4][0:128, 0:TT], FP4_MIDS[i],
                    ctx.scr("l0_hval", (128, TT))[:, 0:TT], ALU.mult, ALU.is_lt)
                if i == 0:
                    k.v("tensor_scalar_mul", acc[:, 0:TT], g[:, 0:TT], FP4_WTS[0])
                else:
                    k.v("scalar_tensor_tensor", acc[:, 0:TT], g[:, 0:TT], FP4_WTS[i],
                        acc[:, 0:TT], ALU.mult, ALU.add)
            k.v("tensor_mul", sgn[:, 0:TT], sgn[:, 0:TT], ctx.PS[4][0:128, 0:TT])
            k.v("tensor_mul", dst_tiles[dp][:, 0:TT], acc[:, 0:TT], sgn[:, 0:TT])

    def prep_abs(src_tiles, dp):
        k.s("activation", ctx.scr("l0_hval", (128, TT))[:, 0:TT], src_tiles[dp][:, 0:TT], AF.Abs)
        k.v("tensor_copy", ctx.scr("l0_arg", (128, TT))[:, 0:TT],
            ctx.scr("l0_hval", (128, TT))[:, 0:TT])

    wq = ctx.wp
    hq = [hp[IMP + i] for i in range(IMP)]
    hfm = [hp[i] for i in range(IMP)]
    hf32 = [ctx.scr("l2f_h0", (128, TT))]
    for tt in range(NTT):
        tsl = slice(tt * TT, (tt + 1) * TT)
        emit_layernorm(ctx, x_tiles, xn, tt * TT, xn2_tiles=xnf)
        for dp in range(DP):
            prep_abs(xnf, dp)
            quant_one = quant_fm([xnf[dp]], [xn[dp]], 1)
        for mp in range(IMP):
            psg = ctx.PS[2]
            psu = ctx.PS[3]
            for kp in range(DP):
                k.y("dma_start", out=wq.sb[kp][:, 0:128],
                    in_=d["wgq"].ap()[mp * 128:(mp + 1) * 128, kp * 128:(kp + 1) * 128],
                    transpose=True)
                k.y("dma_start", out=wq.sb[kp][:, 128:256],
                    in_=d["wuq"].ap()[mp * 128:(mp + 1) * 128, kp * 128:(kp + 1) * 128],
                    transpose=True)
            for kp in range(DP):
                k.t("matmul", psg[:, 0:TT], wq.sb[kp][:, 0:128], xn[kp][:, 0:TT],
                    start=(kp == 0), stop=(kp == DP - 1))
            for kp in range(DP):
                k.t("matmul", psu[:, 0:TT], wq.sb[kp][:, 128:256], xn[kp][:, 0:TT],
                    start=(kp == 0), stop=(kp == DP - 1))
            k.s("activation", hf[:, 0:TT], psg[:, 0:TT], AF.Silu)
            k.v("tensor_mul", hf32[0][:, 0:TT], hf[:, 0:TT], psu[:, 0:TT])
            prep_abs(hf32, 0)
            quant_fm([hf32[0]], [hq[mp]], 1)
        for mp in range(DP):
            ps = ctx.PS[2 + mp % 2]
            for i, kp in enumerate(range(IMP)):
                if i % 8 == 0:
                    for j in range(min(8, IMP - i)):
                        k.y("dma_start", out=wq.sb[j][:, 256:384],
                            in_=d["wdq"].ap()[mp * 128:(mp + 1) * 128,
                                              (i + j) * 128:(i + j + 1) * 128],
                            transpose=True)
                k.t("matmul", ps[:, 0:TT], wq.sb[i % 8][:, 256:384],
                    hq[kp][:, 0:TT], start=(i == 0), stop=(i == IMP - 1))
            k.v("tensor_add", x_tiles[mp][:, tsl], x_tiles[mp][:, tsl], ps[:, 0:TT])


def decl_l2_ffn(nc):
    d = {}
    d["wg"] = nc.dram_tensor("l2f_wg", [2816, 1024], F32, kind="ExternalInput")
    d["wu"] = nc.dram_tensor("l2f_wu", [2816, 1024], F32, kind="ExternalInput")
    d["wd"] = nc.dram_tensor("l2f_wd", [1024, 2816], F32, kind="ExternalInput")
    d["e8"] = nc.dram_tensor("l2f_e8", [8, 128], F32, kind="ExternalInput")
    d["wgq"] = nc.dram_tensor("l2f_wgq", [2816, 1024], BF16, kind="Internal")
    d["wuq"] = nc.dram_tensor("l2f_wuq", [2816, 1024], BF16, kind="Internal")
    d["wdq"] = nc.dram_tensor("l2f_wdq", [1024, 2816], BF16, kind="Internal")
    return d


def stage_l2_ffn(inp, core):
    e8 = np.zeros((8, 128), np.float32)
    for m in range(128):
        e8[m // 16, m] = 1.0
    return {
        "l2f_wg": inp["L2_ffn_w_gate"],
        "l2f_wu": inp["L2_ffn_w_up"],
        "l2f_wd": inp["L2_ffn_w_down"],
        "l2f_e8": e8,
    }


"""Full megakernel driver: builds the single-launch program + staging."""
import numpy as np
import ml_dtypes


from concourse.bass_utils import run_bass_kernel_spmd


def build_program():
    k = Ker("mk_full", serial=True)
    k.auto_sched = True
    ctx = Ctx(k)
    x_in, x_out, xt = make_x_io(k)
    nc = k.nc
    decls = {}
    decls["l0a"] = decl_attn_softmax(nc, "l0a", ntsets=3, rope=True)
    decls["l0a_ad"] = attn_drams(nc, "l0a")
    decls["l0f"] = decl_l0_ffn(nc)
    decls["l0b"] = decl_l0_attn_b(nc)
    decls["l1a"] = decl_attn_softmax(nc, "l1a", ntsets=1)
    decls["l1a_ad"] = attn_drams(nc, "l1a")
    decls["l1f"] = decl_l1_ffn(nc)
    decls["l1b"] = decl_l1_attn_b(nc)
    decls["l1b_ad"] = attn_drams(nc, "l1b2")
    decls["l2a"] = decl_l2_attn_a(nc)
    decls["l2a_ad"] = attn_drams(nc, "l2a2", kdim=128, vdim=128)
    decls["l2f"] = decl_l2_ffn(nc)
    decls["l2b"] = decl_attn_softmax(nc, "l2b", ntsets=1, ghost=True)
    decls["l2b_ad"] = attn_drams(nc, "l2b")
    decls["l3a"] = decl_attn_softmax(nc, "l3a", ntsets=1)
    decls["l3a_ad"] = attn_drams(nc, "l3a")
    decls["l3f"] = decl_l3_ffn(nc)
    decls["mb"] = decl_l3_attn_b(nc)
    emit_bitnet_quant(ctx, decls["l3f"]["l3f_W1T"], decls["l3f"]["l3f_w1q"], D, 4096, "w1")
    emit_bitnet_quant(ctx, decls["l3f"]["l3f_W2T"], decls["l3f"]["l3f_w2q"], 4096, D, "w2")
    emit_l2_ffn_prep(ctx, decls["l2f"])
    emit_attn_softmax(ctx, xt, decls["l0a"], decls["l0a_ad"], 0.125, ntsets=3, rope=True)
    emit_l0_ffn(ctx, xt, decls["l0f"])
    emit_l0_attn_b(ctx, xt, decls["l0b"])
    emit_l0_ffn(ctx, xt, decls["l0f"])
    emit_attn_softmax(ctx, xt, decls["l1a"], decls["l1a_ad"], 0.125, ntsets=1)
    emit_l1_ffn(ctx, xt, decls["l1f"])
    emit_l1_attn_b(ctx, xt, decls["l1b"], decls["l1b_ad"])
    emit_l1_ffn(ctx, xt, decls["l1f"])
    emit_l2_attn_a(ctx, xt, decls["l2a"], decls["l2a_ad"])
    emit_l2_ffn(ctx, xt, decls["l2f"])
    emit_attn_softmax(ctx, xt, decls["l2b"], decls["l2b_ad"], 0.125, ntsets=1,
                        ghost=True, exactmax=True)
    emit_l2_ffn(ctx, xt, decls["l2f"])
    emit_attn_softmax(ctx, xt, decls["l3a"], decls["l3a_ad"], 0.125, ntsets=1)
    emit_l3_ffn(ctx, xt, decls["l3f"])
    emit_l3_attn_b(ctx, xt, decls["mb"])
    emit_l3_ffn(ctx, xt, decls["l3f"])
    store_x(k, x_out, xt)
    return k


def stage_all(inp, core):
    m = {}
    m.update(stage_l0_attn_a(inp, core))
    m.update(stage_l0_ffn(inp))
    m.update(stage_l0_attn_b(inp, core))
    m.update(stage_l1_attn_a(inp, core))
    m.update(stage_l1_ffn(inp))
    m.update(stage_l1_attn_b(inp, core))
    m.update(stage_l2_attn_a(inp, core))
    m.update(stage_l2_ffn(inp, core))
    m.update(stage_l2_attn_b(inp, core))
    m.update(stage_l3_attn_a(inp, core))
    m.update(stage_l3_ffn(inp))
    m.update(stage_l3_attn_b(inp, core))
    return m


def kernel(**inputs):
    x = inputs["L0_attn_a_hidden_states"]
    k = build_program()
    nc = k.build()
    in_maps = []
    for c in range(NCORES):
        m = {"x_in": host_fm(x, c)}
        m.update(stage_all(inputs, c))
        in_maps.append(m)
    res = run_bass_kernel_spmd(nc, in_maps, core_ids=list(range(NCORES)))
    return host_unfm([res.results[c]["x_out"] for c in range(NCORES)])
